# version 18
# baseline (speedup 1.0000x reference)
import math
from contextlib import ExitStack

import numpy as np
import ml_dtypes
import concourse.bass as bass
import concourse.mybir as mybir
from concourse.bass_utils import run_bass_kernel_spmd

F32 = mybir.dt.float32
BF16 = mybir.dt.bfloat16
AF = mybir.ActivationFunctionType
ALU = mybir.AluOpType
AX = mybir.AxisListType

S = 2048
D = 2048
H = 8
DH = 256
HID = 5632
NCD = D // 128
NST = S // 128
NH3 = HID // 128
L_INIT = 0.8 - 0.6 * math.exp(-0.3)
OML = 1.0 - L_INIT
EPS_LN = 1e-5
EPS_RMS = 1e-6
EPS_Q = 1e-8

TAPS = ()
DEV_HEADS = 8
DEV_QUANT = True
DEV_STAGE = 'full'
DEV_PREFETCH = True
PHASES = 99

ENGINES = ("sync", "tensor", "vector", "scalar", "gpsimd")


class Ch:
    def __init__(self, nc, ctx, name):
        self.sem = ctx.enter_context(nc.semaphore(name))
        self.n = 0
        self.name = name


class Guard:
    __slots__ = ("w", "r")

    def __init__(self):
        self.w = None
        self.r = []


class Emit:
    OWN = {"tensor": "cT"}

    def __init__(self):
        self.streams = {e: [] for e in ENGINES}
        self.waited = {e: {} for e in ENGINES}
        self.counts = {e: 0 for e in ENGINES}

    def _wait(self, eng, ch, tok):
        if tok is None or tok <= 0:
            return
        if self.OWN.get(eng) == ch.name:
            return
        w = self.waited[eng]
        if w.get(ch.name, 0) >= tok:
            return
        w[ch.name] = tok
        self.streams[eng].append(lambda e, s=ch.sem, t=tok: e.wait_ge(s, t))

    def do(self, eng, fn, reads=(), writes=(), waits=(), inc=None, inc_n=1):
        need = []
        for g in reads:
            if g.w is not None:
                need.append(g.w)
        for g in writes:
            if g.w is not None:
                need.append(g.w)
            need.extend(g.r)
        need.extend(waits)
        for ch, tok in need:
            self._wait(eng, ch, tok)
        tok = None
        if inc is not None:
            inc.n += inc_n
            tok = (inc, inc.n)
            self.streams[eng].append(
                lambda e, f=fn, s=inc.sem, v=inc_n: f(e).then_inc(s, v))
            for g in writes:
                g.w = tok
                g.r = []
            for g in reads:
                g.r.append(tok)
        else:
            self.streams[eng].append(lambda e, f=fn: f(e))
        self.counts[eng] += 1
        return tok


def build_nc(taps=(), phases=99, dev_heads=None, dev_quant=None):
    dev_heads = DEV_HEADS if dev_heads is None else dev_heads
    dev_quant = DEV_QUANT if dev_quant is None else dev_quant
    nc = bass.Bass("TRN2", target_bir_lowering=False)
    em = Emit()
    st = ExitStack()

    ap = lambda t: t.ap()
    x_in = ap(nc.dram_tensor("x", [S, D], F32, kind="ExternalInput"))
    xT_in = ap(nc.dram_tensor("xT", [D, S], F32, kind="ExternalInput"))
    wqT = ap(nc.dram_tensor("wqT", [D, 2 * D], F32, kind="ExternalInput"))
    wkT = ap(nc.dram_tensor("wkT", [D, 2 * D], F32, kind="ExternalInput"))
    wvT = ap(nc.dram_tensor("wvT", [D, D], F32, kind="ExternalInput"))
    woT = ap(nc.dram_tensor("woT", [D, D], F32, kind="ExternalInput"))
    w1T = ap(nc.dram_tensor("w1T", [D, HID], F32, kind="ExternalInput"))
    w2T = ap(nc.dram_tensor("w2T", [D, HID], F32, kind="ExternalInput"))
    w3T = ap(nc.dram_tensor("w3T", [HID, D], F32, kind="ExternalInput"))
    lvec = ap(nc.dram_tensor("lvec", [1, 32], F32, kind="ExternalInput"))
    mask01_in = ap(nc.dram_tensor("mask01", [128, 128], BF16,
                                  kind="ExternalInput"))
    idenb_in = ap(nc.dram_tensor("idenb", [128, 128], BF16,
                                 kind="ExternalInput"))
    idenf_in = ap(nc.dram_tensor("idenf", [128, 128], F32,
                                 kind="ExternalInput"))
    out_d = ap(nc.dram_tensor("out", [S, D], F32, kind="ExternalOutput"))

    def dscr(name, shape, dt):
        kind = "ExternalOutput" if name in taps else "Internal"
        return ap(nc.dram_tensor(name, shape, dt, kind=kind))

    stat_scr = dscr("stat_scr", [2, S], F32)
    q1s = dscr("q1s", [NH3 * 128, NCD * 128], BF16)
    q2s = dscr("q2s", [NH3 * 128, NCD * 128], BF16)
    q3s = dscr("q3s", [HID, D], BF16)
    o_scr = dscr("o_scr", [NCD * 4 * 128, 512], BF16)
    sattn = dscr("sattn", [S, D], BF16)

    tap_outs = {}
    for tname, shape, dt in [
        ("lam", [128, 1], F32), ("xn", [128, NCD * S], BF16),
        ("qf", [128, 4 * S], BF16), ("kf", [128, 4 * S], BF16),
        ("vt", [128, NST * DH], BF16), ("thr", [128, 3], F32),
        ("bias0", [128, 1], F32), ("bias1", [128, 1], F32),
        ("sc11", [1, 16], F32), ("hid", [128, NH3 * 512], BF16),
    ]:
        if tname in taps:
            tap_outs[tname] = ap(
                nc.dram_tensor("tap_" + tname, shape, dt, kind="ExternalOutput"))

    cLs = [Ch(nc, st, f"cL{i}") for i in range(8)]
    cSs = [Ch(nc, st, f"cS{i}") for i in range(4)]
    cGs = [Ch(nc, st, f"cG{i}") for i in range(4)]
    _lane = {"sync": [0], "scalar": [0], "gpsimd": [0]}
    _pools = {"sync": cLs, "scalar": cSs, "gpsimd": cGs}

    def dma_lane(queue):
        pool = _pools[queue]
        i = _lane[queue][0]
        _lane[queue][0] = (i + 1) % len(pool)
        return pool[i]

    def all_dma_toks():
        return [(c, c.n) for c in cLs + cSs + cGs if c.n > 0]
    cT = Ch(nc, st, "cT")
    cV = Ch(nc, st, "cV")
    cA = Ch(nc, st, "cA")
    cP = Ch(nc, st, "cP")
    cL = None
    cS = None

    psA = st.enter_context(nc.psum_tensor("psA", [128, 512], F32))
    psB = st.enter_context(nc.psum_tensor("psB", [128, 512], F32))
    psPV = [st.enter_context(nc.psum_tensor(f"psPV{i}", [128, 512], F32))
            for i in range(4)]
    psR = st.enter_context(nc.psum_tensor("psR", [128, 512], F32))
    psC = st.enter_context(nc.psum_tensor("psC", [128, 512], F32))
    gA, gB, gR = Guard(), Guard(), Guard()
    gCps = Guard()
    gPV = [Guard() for _ in range(4)]

    sbS = lambda name, shape, dt: st.enter_context(
        nc.sbuf_tensor(name, shape, dt))
    xn = sbS("xn", [128, NCD * S], BF16)
    gXNq = [Guard() for _ in range(4)]
    mask01 = sbS("mask01_s", [128, 128], BF16)
    idenb = sbS("idenb_s", [128, 128], BF16)
    idenf = sbS("idenf_s", [128, 128], F32)
    ones_r = sbS("ones_r", [1, 128], BF16)
    ones_c = sbS("ones_c", [128, 1], BF16)
    lam_sb = sbS("lam_sb", [128, 1], F32)
    gLAM = Guard()
    epsln = sbS("epsln", [128, 1], F32)
    epsrms = sbS("epsrms", [128, 1], F32)
    thr_sb = sbS("thr_sb", [128, 3], F32)
    thrn_sb = sbS("thrn_sb", [128, 3], F32)
    gTHR = [Guard() for _ in range(3)]
    lrow = sbS("lrow", [1, 32], F32)
    sc11 = sbS("sc11", [1, 16], F32)
    sc11b = sbS("sc11b", [1, 16], BF16)
    sc11b_f = sbS("sc11b_f", [1, 16], F32)
    biasB = [sbS(f"biasB{i}", [128, 1], F32) for i in range(2)]
    wq_stack = ExitStack()
    sbW = lambda name, shape, dt: wq_stack.enter_context(
        nc.sbuf_tensor(name, shape, dt))
    wstage = sbW("wstage", [128, NCD * 128], F32)
    gWS = Guard()
    wbf = [sbW(f"wbf{i}", [128, NCD * 256], BF16) for i in range(3)]
    gWB = [Guard(), Guard(), Guard()]
    gBB = [Guard(), Guard()]
    gC = Guard()
    gSC = Guard()

    def xnc(c, cols):
        return xn[:, c * S + cols.start: c * S + cols.stop]

    def dma(queue, out, in_, ch, reads=(), writes=(), waits=()):
        lane = dma_lane(queue)
        return em.do(queue, lambda e, o=out, i=in_: e.dma_start(out=o, in_=i),
                     reads=reads, writes=writes, waits=waits, inc=lane,
                     inc_n=16)

    def mm(out, lhsT, rhs, start, stop, reads=(), writes=(), waits=(),
           inc=None):
        return em.do("tensor",
                     lambda e, o=out, l=lhsT, r=rhs, s1=start, s2=stop:
                     e.matmul(o, l, r, start=s1, stop=s2,
                              skip_group_check=True),
                     reads=reads, writes=writes, waits=waits, inc=inc)

    def transp(out, in_, iden, reads=(), writes=(), inc=None):
        return em.do("tensor",
                     lambda e, o=out, i=in_, d=iden: e.transpose(o, i, d),
                     reads=reads, writes=writes, inc=inc)

    def act(out, in_, func, bias=None, scale=None, accum=None,
            reads=(), writes=(), waits=(), inc=None):
        def f(e, o=out, i=in_, fu=func, b=bias, s=scale, a=accum):
            kw = {}
            if b is not None:
                kw["bias"] = b
            if s is not None:
                kw["scale"] = s
            if a is not None:
                kw["accum_out"] = a
            return e.activation(o, i, fu, **kw)
        return em.do("scalar", f, reads=reads, writes=writes, waits=waits,
                     inc=inc)

    def acopy(out, in_, scale=None, reads=(), writes=(), waits=(), inc=None):
        if scale is None:
            f = lambda e, o=out, i=in_: e.copy(o, i)
        else:
            f = lambda e, o=out, i=in_, s=scale: e.mul(o, i, s)
        return em.do("scalar", f, reads=reads, writes=writes, waits=waits,
                     inc=inc)

    def vop(kind, *args, eng="vector", reads=(), writes=(), waits=(),
            inc=None):
        if kind == "tt":
            out, a, b, op = args
            f = lambda e, o=out, x=a, y=b, p=op: e.tensor_tensor(o, x, y, p)
        elif kind == "ts":
            out, a, s1, op = args
            f = lambda e, o=out, x=a, s=s1, p=op: e.tensor_scalar(
                o, x, s, None, p)
        elif kind == "red":
            out, a, op = args
            f = lambda e, o=out, x=a, p=op: e.tensor_reduce(o, x, AX.X, p)
        elif kind == "redabs":
            out, a, op = args
            f = lambda e, o=out, x=a, p=op: e.tensor_reduce(
                o, x, AX.X, p, apply_absolute_value=True)
        elif kind == "copy":
            out, a = args
            f = lambda e, o=out, x=a: e.tensor_copy(o, x)
        elif kind == "recip":
            out, a = args
            f = lambda e, o=out, x=a: e.reciprocal(o, x)
        elif kind == "memset":
            out, val = args
            f = lambda e, o=out, v=val: e.memset(o, v)
        else:
            raise ValueError(kind)
        return em.do(eng, f, reads=reads, writes=writes, waits=waits, inc=inc)

    def dma_chunked(queue, outv, inv, ch, nsplit, csize, reads=(), writes=(),
                    waits=()):
        toks = []
        per = csize // nsplit
        for i in range(nsplit):
            toks.append(dma(queue, outv[:, i * per:(i + 1) * per, :],
                            inv[:, i * per:(i + 1) * per, :], ch,
                            reads=reads, writes=writes, waits=waits))
        return toks[-1]

    dma("sync", mask01[:], mask01_in[:, :], cL, writes=[gC])
    dma("sync", idenb[:], idenb_in[:, :], cL, writes=[gC])
    dma("sync", idenf[:], idenf_in[:, :], cL, writes=[gC])
    dma("sync", lrow[:], lvec[:, :], cL, writes=[gC])
    vop("memset", ones_r[:], 1.0, writes=[gC], inc=cV)
    vop("memset", ones_c[:], 1.0, writes=[gC], inc=cV)
    vop("memset", epsln[:], EPS_LN, writes=[gC], inc=cV)
    vop("memset", epsrms[:], EPS_RMS, writes=[gC], inc=cV)

    vop("tt", sc11[0:1, 0:8], lrow[0:1, 0:8], lrow[0:1, 8:16], ALU.mult,
        reads=[gC], writes=[gSC], inc=cV)
    vop("red", sc11[0:1, 8:9], sc11[0:1, 0:8], ALU.add, writes=[gSC], inc=cV)
    vop("tt", sc11[0:1, 0:8], lrow[0:1, 16:24], lrow[0:1, 24:32], ALU.mult,
        writes=[gSC], inc=cV)
    vop("red", sc11[0:1, 9:10], sc11[0:1, 0:8], ALU.add, writes=[gSC], inc=cV)
    act(sc11[0:1, 10:12], sc11[0:1, 8:10], AF.Exp,
        reads=[gSC], writes=[gSC], inc=cA)
    vop("tt", sc11[0:1, 12:13], sc11[0:1, 10:11], sc11[0:1, 11:12],
        ALU.subtract, writes=[gSC], inc=cV)
    vop("ts", sc11[0:1, 13:14], sc11[0:1, 12:13], L_INIT, ALU.add,
        writes=[gSC], inc=cV)
    vop("copy", sc11b[0:1, 0:1], sc11[0:1, 13:14], writes=[gSC], inc=cV)
    vop("tt", sc11[0:1, 14:15], sc11[0:1, 13:14], sc11b[0:1, 0:1],
        ALU.subtract, writes=[gSC], inc=cV)
    vop("copy", sc11b[0:1, 1:2], sc11[0:1, 14:15], writes=[gSC], inc=cV)
    mm(psR[:, 16:17], ones_r[:], sc11b[0:1, 0:1], True, False,
       reads=[gSC, gC], writes=[gR], inc=cT)
    mm(psR[:, 16:17], ones_r[:], sc11b[0:1, 1:2], False, True,
       reads=[gSC], writes=[gR], inc=cT)
    vop("copy", lam_sb[:], psR[:, 16:17], reads=[gR], writes=[gLAM], inc=cV)
    if "sc11" in taps:
        dma("sync", tap_outs["sc11"], sc11[:], cL, reads=[gSC])
    if "lam" in taps:
        dma("sync", tap_outs["lam"], lam_sb[:], cL, reads=[gLAM])

    ln_stack = ExitStack()
    if phases >= 1:
        sbl = lambda name, shape, dt: ln_stack.enter_context(
            nc.sbuf_tensor(name, shape, dt))
        xstat = sbl("xstat", [128, D], F32)
        gXS = Guard()
        sqt = sbl("sqt", [128, D], F32)
        gSQ = Guard()
        rcol = sbl("rcol", [128, 16], F32)
        mcol = sbl("mcol", [128, 16], F32)
        gCOL = Guard()
        statT = sbl("statT", [16, 256], F32)
        gSTT = Guard()
        rowf = sbl("rowf", [1, 512], F32)
        gROWF = Guard()
        rhi = sbl("rhi", [1, S], BF16)
        rlo = sbl("rlo", [1, S], BF16)
        mhi = sbl("mhi", [1, S], BF16)
        mlo = sbl("mlo", [1, S], BF16)
        gROW = Guard()
        stat1 = sbl("stat1", [128, 8], F32)
        gS1 = Guard()
        xTg = sbl("xTg", [128, NCD * 256], F32)
        gXT = Guard()

        for t in range(NST):
            dma("sync", xstat[:], x_in[t * 128:(t + 1) * 128, :], cL,
                writes=[gXS])
            vop("red", stat1[:, 0:1], xstat[:], ALU.add,
                reads=[gXS], writes=[gS1], inc=cV)
            act(sqt[:], xstat[:], AF.Square, reads=[gXS], writes=[gSQ], inc=cA)
            vop("red", stat1[:, 1:2], sqt[:], ALU.add,
                reads=[gSQ], writes=[gS1], inc=cV)
            vop("ts", stat1[:, 2:3], stat1[:, 0:1], 1.0 / D, ALU.mult,
                writes=[gS1], inc=cV)
            vop("ts", stat1[:, 3:4], stat1[:, 1:2], 1.0 / D, ALU.mult,
                writes=[gS1], inc=cV)
            vop("tt", stat1[:, 4:5], stat1[:, 2:3], stat1[:, 2:3], ALU.mult,
                writes=[gS1], inc=cV)
            vop("tt", stat1[:, 5:6], stat1[:, 3:4], stat1[:, 4:5],
                ALU.subtract, writes=[gS1], inc=cV)
            act(stat1[:, 6:7], stat1[:, 5:6], AF.Ln, bias=epsln[:],
                reads=[gS1, gC], writes=[gS1], inc=cA)
            act(rcol[:, t:t + 1], stat1[:, 6:7], AF.Exp, scale=-0.5,
                reads=[gS1], writes=[gCOL], inc=cA)
            vop("tt", mcol[:, t:t + 1], stat1[:, 2:3], rcol[:, t:t + 1],
                ALU.mult, reads=[gCOL, gS1], writes=[gCOL], inc=cV)
        transp(psR[0:16, 0:128], rcol[:], idenf[:],
               reads=[gCOL, gC, gR], writes=[gR], inc=cT)
        vop("copy", statT[:, 0:128], psR[0:16, 0:128], reads=[gR],
            writes=[gSTT], inc=cV)
        transp(psR[0:16, 128:256], mcol[:], idenf[:], reads=[gCOL],
               writes=[gR], inc=cT)
        vop("copy", statT[:, 128:256], psR[0:16, 128:256], reads=[gR],
            writes=[gSTT], inc=cV)
        dma("gpsimd", stat_scr[0:1, :].rearrange("a (b c) -> (a b) c", b=16),
            statT[:, 0:128], cS, reads=[gSTT])
        dma("gpsimd", stat_scr[1:2, :].rearrange("a (b c) -> (a b) c", b=16),
            statT[:, 128:256], cS, reads=[gSTT])
        tok_stat_l = [(c, c.n) for c in cSs + cGs if c.n > 0]
        for qg in range(4):
            cols = slice(qg * 512, qg * 512 + 512)
            dma("sync", rowf[:], stat_scr[0:1, cols], cL, writes=[gROWF],
                waits=tok_stat_l)
            vop("copy", rhi[0:1, cols], rowf[:], reads=[gROWF], writes=[gROW],
                inc=cV)
            vop("tt", rowf[:], rowf[:], rhi[0:1, cols], ALU.subtract,
                reads=[gROW], writes=[gROWF], inc=cV)
            vop("copy", rlo[0:1, cols], rowf[:], reads=[gROWF], writes=[gROW],
                inc=cV)
            dma("sync", rowf[:], stat_scr[1:2, cols], cL, writes=[gROWF])
            vop("copy", mhi[0:1, cols], rowf[:], reads=[gROWF], writes=[gROW],
                inc=cV)
            vop("tt", rowf[:], rowf[:], mhi[0:1, cols], ALU.subtract,
                reads=[gROW], writes=[gROWF], inc=cV)
            vop("copy", mlo[0:1, cols], rowf[:], reads=[gROWF], writes=[gROW],
                inc=cV)
        for qg in range(4):
            cols = slice(qg * 512, (qg + 1) * 512)
            pr = psPV[2 * (qg % 2)]
            pm = psPV[2 * (qg % 2) + 1]
            gr = gPV[2 * (qg % 2)]
            gm = gPV[2 * (qg % 2) + 1]
            mm(pr[:], ones_r[:], rhi[0:1, cols], True, False,
               reads=[gROW, gC], writes=[gr], inc=cT)
            mm(pr[:], ones_r[:], rlo[0:1, cols], False, True,
               reads=[gROW], writes=[gr], inc=cT)
            mm(pm[:], ones_r[:], mhi[0:1, cols], True, False,
               reads=[gROW], writes=[gm], inc=cT)
            mm(pm[:], ones_r[:], mlo[0:1, cols], False, True,
               reads=[gROW], writes=[gm], inc=cT)
            for half in range(2):
                hc = slice(qg * 512 + half * 256, qg * 512 + half * 256 + 256)
                dma_chunked("sync",
                            xTg[:].rearrange("p (c n) -> p c n", c=NCD),
                            xT_in[:, hc].rearrange("(c p) n -> p c n", p=128),
                            cL, 4, NCD, writes=[gXT])
                for c in range(NCD):
                    xg = xTg[:, c * 256:(c + 1) * 256]
                    prh = pr[:, half * 256:(half + 1) * 256]
                    pmh = pm[:, half * 256:(half + 1) * 256]
                    vop("tt", sqt[:, 0:256], xg, prh, ALU.mult,
                        reads=[gXT, gr], writes=[gSQ], inc=cV)
                    vop("tt", xnc(c, hc), sqt[:, 0:256], pmh, ALU.subtract,
                        reads=[gSQ, gm], writes=[gXNq[qg]], inc=cV)
        if "xn" in taps:
            dma("sync", tap_outs["xn"], xn[:], cL, reads=list(gXNq))
    tok_ln = [(cV, cV.n), (cA, cA.n), (cT, cT.n)] + all_dma_toks()
    ln_stack.close()

    q_stack = ExitStack()
    quant_on = phases >= 2 and dev_quant
    if quant_on:
        sbq = lambda name, shape, dt: q_stack.enter_context(
            nc.sbuf_tensor(name, shape, dt))
        stA = sbq("stA", [128, 2 * 1408], F32)
        gSA = [Guard(), Guard()]
        accw = sbq("accw", [128, 4], F32)
        gACC = Guard()
        acchl = sbq("acchl", [128, 2], BF16)
        stB = sbq("stB", [128, 2 * 2048], F32)
        gSB = [Guard(), Guard()]
        qneg = sbq("qneg", [128, 2048], BF16)
        gQN = Guard()
        qout = sbq("qout", [128, 2 * 2048], BF16)
        gQO = [Guard(), Guard()]
        wsrcs = [(w1T, D, HID), (w2T, D, HID), (w3T, HID, D)]

    qA_jobs = []
    if quant_on:
        for wi, (wsrc, nr, ncl) in enumerate(wsrcs):
            pw = 1408 if ncl % 1408 == 0 else 1024
            assert ncl % pw == 0
            for tile in range(nr // 128):
                for pc in range(ncl // pw):
                    qA_jobs.append((wi, tile, pc, pw,
                                    tile == 0 and pc == 0))
    qA_pos = [0]

    def emit_passA(njobs):
        end = min(qA_pos[0] + njobs, len(qA_jobs))
        while qA_pos[0] < end:
            wi, tile, pc, pw, first = qA_jobs[qA_pos[0]]
            wsrc, nr, ncl = wsrcs[wi]
            b = qA_pos[0] % 2
            qA_pos[0] += 1
            stv = stA[:, b * 1408:b * 1408 + pw]
            dma("sync", stv,
                wsrc[tile * 128:(tile + 1) * 128,
                     pc * pw:(pc + 1) * pw], cL, writes=[gSA[b]])
            vop("redabs", accw[:, 3:4], stv, ALU.add,
                reads=[gSA[b]], writes=[gACC], inc=cV)
            if first:
                vop("copy", accw[:, wi:wi + 1], accw[:, 3:4],
                    reads=[gACC], writes=[gACC], inc=cV)
            else:
                vop("tt", accw[:, wi:wi + 1], accw[:, wi:wi + 1],
                    accw[:, 3:4], ALU.add,
                    reads=[gACC], writes=[gACC], inc=cV)

    def emit_gamma(wi):
        _, nr, ncl = wsrcs[wi]
        vop("copy", acchl[:, 0:1], accw[:, wi:wi + 1],
            reads=[gACC], writes=[gACC], inc=cV)
        vop("tt", accw[:, 3:4], accw[:, wi:wi + 1], acchl[:, 0:1],
            ALU.subtract, writes=[gACC], inc=cV)
        vop("copy", acchl[:, 1:2], accw[:, 3:4], writes=[gACC], inc=cV)
        mm(psR[0:1, 17:18], acchl[:, 0:1], ones_c[:], True, False,
           reads=[gACC, gC], writes=[gR], inc=cT)
        mm(psR[0:1, 17:18], acchl[:, 1:2], ones_c[:], False, True,
           reads=[gACC], writes=[gR], inc=cT)
        cnt = nr * ncl
        vop("ts", sc11[0:1, 15:16], psR[0:1, 17:18], 0.5 / cnt, ALU.mult,
            reads=[gR], writes=[gSC], inc=cV)
        vop("ts", sc11[0:1, 15:16], sc11[0:1, 15:16], 0.5 * EPS_Q, ALU.add,
            writes=[gSC], inc=cV)
        vop("copy", sc11b[0:1, 2:3], sc11[0:1, 15:16], writes=[gSC], inc=cV)
        vop("tt", sc11[0:1, 14:15], sc11[0:1, 15:16], sc11b[0:1, 2:3],
            ALU.subtract, writes=[gSC], inc=cV)
        vop("copy", sc11b[0:1, 3:4], sc11[0:1, 14:15], writes=[gSC], inc=cV)
        mm(psR[:, 18:19], ones_r[:], sc11b[0:1, 2:3], True, False,
           reads=[gSC], writes=[gR], inc=cT)
        mm(psR[:, 18:19], ones_r[:], sc11b[0:1, 3:4], False, True,
           reads=[gSC], writes=[gR], inc=cT)
        vop("copy", thr_sb[:, wi:wi + 1], psR[:, 18:19],
            reads=[gR], writes=[gTHR[wi]], inc=cV)
        vop("ts", thrn_sb[:, wi:wi + 1], thr_sb[:, wi:wi + 1], -1.0,
            ALU.mult, reads=[gTHR[wi]], writes=[gTHR[wi]], inc=cV)

    qB_jobs = []
    if quant_on:
        for j in range(NH3):
            qB_jobs.append((0,
                            w1T[:, j * 128:(j + 1) * 128].rearrange(
                                "(c p) n -> p c n", p=128),
                            q1s[j * 128:(j + 1) * 128, :], True))
            qB_jobs.append((1,
                            w2T[:, j * 128:(j + 1) * 128].rearrange(
                                "(c p) n -> p c n", p=128),
                            q2s[j * 128:(j + 1) * 128, :], True))
        for tile in range(NH3):
            qB_jobs.append((2, w3T[tile * 128:(tile + 1) * 128, :],
                            q3s[tile * 128:(tile + 1) * 128, :], False))
    qB_pos = [0]

    def emit_passB(njobs):
        end = min(qB_pos[0] + njobs, len(qB_jobs))
        while qB_pos[0] < end:
            wi, src, dst, is3d = qB_jobs[qB_pos[0]]
            b = qB_pos[0] % 2
            qB_pos[0] += 1
            stv = stB[:, b * 2048:(b + 1) * 2048]
            qo = qout[:, b * 2048:(b + 1) * 2048]
            dma("sync", stv.rearrange("p (c n) -> p c n", c=NCD) if is3d
                else stv, src, cL, writes=[gSB[b]])
            vop("ts", qo, stv, thr_sb[:, wi:wi + 1], ALU.is_ge,
                reads=[gSB[b], gTHR[wi]], writes=[gQO[b]], inc=cV)
            vop("ts", qneg[:], stv, thrn_sb[:, wi:wi + 1], ALU.is_le,
                reads=[gSB[b], gTHR[wi]], writes=[gQN], inc=cV)
            vop("tt", qo, qo, qneg[:], ALU.subtract,
                reads=[gQN], writes=[gQO[b]], inc=cV)
            dma("gpsimd", dst, qo, cS, reads=[gQO[b]])

    at_stack = ExitStack()
    o_wr_toks = {}
    if phases >= 3:
        sba = lambda name, shape, dt: at_stack.enter_context(
            nc.sbuf_tensor(name, shape, dt))
        qf = sba("qf", [128, 4 * S], BF16)
        kf = sba("kf", [128, 4 * S], BF16)
        gQF = [Guard() for _ in range(4)]
        gKF = [Guard() for _ in range(4)]
        vt = sba("vt", [128, NST * DH], BF16)
        gVT = [Guard() for _ in range(NST)]
        sqb = sba("sqb", [128, 512], BF16)
        gSQB = Guard()
        ebuf = [sba(f"ebuf{i}", [128, 512], BF16) for i in range(3)]
        gE = [Guard(), Guard(), Guard()]
        pvsb = sba("pvsb", [128, 8 * 257], F32)
        gPVS = Guard()
        rsb = sba("rsb", [128, 8], F32)
        gRS = Guard()
        onq = [sba(f"onq{i}", [128, 4 * DH], F32) for i in range(2)]
        gON = [Guard(), Guard()]
        odiff = sba("odiff", [128, 256], F32)
        gOD = Guard()
        otmp = sba("otmp", [128, 256], F32)
        gOT = Guard()
        osq, gOSQ = otmp, gOT
        rmsc = sba("rmsc", [128, 16], F32)
        gRMS = Guard()
        obf = [sba(f"obf{i}", [128, 512], BF16) for i in range(4)]
        gOBF = [Guard() for _ in range(4)]

        pend_transp = []
        tpi = [0]

        def flush_transposes():
            while pend_transp:
                onqb, head, qg, gONb = pend_transp.pop(0)
                for fc in range(2):
                    ob = obf[(qg % 2) * 2 + fc]
                    gob = gOBF[(qg % 2) * 2 + fc]
                    for sub in range(4):
                        reg = tpi[0] % 2
                        tpi[0] += 1
                        pt = (psA if reg == 0 else psB)[:, 0:128]
                        gpt = gA if reg == 0 else gB
                        transp(pt, onqb[:, sub * DH + fc * 128:
                                        sub * DH + fc * 128 + 128],
                               idenf[:], reads=[gONb, gC], writes=[gpt],
                               inc=cT)
                        vop("copy", ob[:, sub * 128:(sub + 1) * 128], pt,
                            reads=[gpt], writes=[gob], inc=cV)
                    row0 = ((head * 2 + fc) * 4 + qg) * 128
                    tok = dma("gpsimd", o_scr[row0:row0 + 128, :], ob[:],
                              cS, reads=[gob])
                    o_wr_toks[(head * 2 + fc, qg)] = tok

        def emit_wload(wsrc, colr, slot):
            ws, gws = wstage, gWS
            wb, gwb = wbf[slot % 3], gWB[slot % 3]
            for half in range(2):
                hc = slice(colr.start + half * 128, colr.start + half * 128 + 128)
                dma_chunked("sync", ws[:].rearrange("p (c n) -> p c n", c=NCD),
                            wsrc[:, hc].rearrange("(c p) n -> p c n", p=128),
                            cL, 2, NCD, writes=[gws])
                vop("copy",
                    wb[:].rearrange("p (c n) -> p c n", c=NCD)[
                        :, :, half * 128:(half + 1) * 128],
                    ws[:].rearrange("p (c n) -> p c n", c=NCD),
                    eng="gpsimd", reads=[gws], writes=[gwb], inc=cP)

        def proj_qk(dst, dstg, hh_local, slot):
            scale = 0.25
            wb, gwb = wbf[slot % 3], gWB[slot % 3]
            for m in range(2):
                r_ = 2 * hh_local + m
                for sg in range(4):
                    ps, gps = (psA, gA) if (m * 4 + sg) % 2 == 0 else (psB, gB)
                    for c in range(NCD):
                        mm(ps[:], wb[:, c * 256 + m * 128:
                                     c * 256 + m * 128 + 128],
                           xnc(c, slice(sg * 512, (sg + 1) * 512)),
                           c == 0, c == NCD - 1,
                           reads=[gwb] + gXNq, writes=[gps],
                           inc=cT if c == NCD - 1 else None)
                    acopy(dst[:, r_ * S + sg * 512:r_ * S + (sg + 1) * 512],
                          ps[:], scale=scale,
                          reads=[gps], writes=[dstg[r_]], inc=cA)

        def proj_v(head, slot):
            wb, gwb = wbf[slot % 3], gWB[slot % 3]
            for t2 in range(NST):
                ps, gps = (psA, gA) if t2 % 2 == 0 else (psB, gB)
                for c in range(NCD):
                    mm(ps[:, 0:256],
                       xnc(c, slice(t2 * 128, (t2 + 1) * 128)),
                       wb[:, c * 256:c * 256 + 256],
                       c == 0, c == NCD - 1,
                       reads=[gwb] + gXNq, writes=[gps],
                       inc=cT if c == NCD - 1 else None)
                acopy(vt[:, t2 * DH:(t2 + 1) * DH], ps[:, 0:256],
                      reads=[gps], writes=[gVT[t2]], inc=cA)

        def wload_job(head, i):
            base = head * 5
            hhA, hhB = head, head + 8
            if i == 0:
                emit_wload(wqT, slice(256 * hhA, 256 * hhA + 256), base + 0)
            elif i == 1:
                emit_wload(wqT, slice(256 * hhB, 256 * hhB + 256), base + 1)
            elif i == 2:
                emit_wload(wkT, slice(256 * hhA, 256 * hhA + 256), base + 2)
            elif i == 3:
                emit_wload(wkT, slice(256 * hhB, 256 * hhB + 256), base + 3)
            else:
                emit_wload(wvT, slice(256 * head, 256 * head + 256), base + 4)

        def emit_head(head, bg_fn=None):
            base = head * 5
            if not DEV_PREFETCH and head > 0:
                for i in range(3):
                    wload_job(head, i)
            proj_qk(qf, gQF, 0, base + 0)
            wload_job(head, 3)
            proj_qk(qf, gQF, 1, base + 1)
            wload_job(head, 4)
            proj_qk(kf, gKF, 0, base + 2)
            proj_qk(kf, gKF, 1, base + 3)
            proj_v(head, base + 4)
            nexth = head + 1 if head + 1 < min(H, dev_heads) else None
            if not DEV_PREFETCH:
                nexth = None
            if head == 0:
                if "qf" in taps:
                    dma("sync", tap_outs["qf"], qf[:], cL, reads=list(gQF))
                if "kf" in taps:
                    dma("sync", tap_outs["kf"], kf[:], cL, reads=list(gKF))
                if "vt" in taps:
                    dma("sync", tap_outs["vt"], vt[:], cL, reads=list(gVT))

            if DEV_STAGE == "proj":
                return
            for hh in range(2):
                for which, (src, gsrc) in enumerate([(qf, gQF), (kf, gKF)]):
                    for cc in range(8):
                        for ri in range(2):
                            r_ = 2 * hh + ri
                            act(sqb[:, ri * 256:(ri + 1) * 256],
                                src[:, r_ * S + cc * 256:
                                    r_ * S + (cc + 1) * 256],
                                AF.Square, reads=[gsrc[r_]], writes=[gSQB],
                                inc=cA)
                        for ri in range(2):
                            mm(psR[0:1, 256:512], ones_c[:],
                               sqb[:, ri * 256:(ri + 1) * 256],
                               ri == 0, ri == 1,
                               reads=[gSQB, gC], writes=[gR],
                               inc=cT if ri == 1 else None)
                        vop("red", sc11[0:1, which * 8 + cc:
                                        which * 8 + cc + 1],
                            psR[0:1, 256:512], ALU.max, reads=[gR],
                            writes=[gSC], inc=cV)
                vop("red", sc11b_f[0:1, 0:1], sc11[0:1, 0:8], ALU.max,
                    writes=[gSC], inc=cV)
                vop("red", sc11b_f[0:1, 1:2], sc11[0:1, 8:16], ALU.max,
                    writes=[gSC], inc=cV)
                vop("tt", sc11b_f[0:1, 2:3], sc11b_f[0:1, 0:1],
                    sc11b_f[0:1, 1:2], ALU.add, writes=[gSC], inc=cV)
                vop("ts", sc11b_f[0:1, 3:4], sc11b_f[0:1, 2:3], -0.5,
                    ALU.mult, writes=[gSC], inc=cV)
                vop("copy", sc11b[0:1, 4:5], sc11b_f[0:1, 3:4],
                    writes=[gSC], inc=cV)
                mm(psR[:, 19:20], ones_r[:], sc11b[0:1, 4:5], True, True,
                   reads=[gSC, gC], writes=[gR], inc=cT)
                vop("copy", biasB[hh][:], psR[:, 19:20], reads=[gR],
                    writes=[gBB[hh]], inc=cV)
            if head == 0 and "bias0" in taps:
                dma("sync", tap_outs["bias0"], biasB[0][:], cL, reads=[gBB[0]])
                dma("sync", tap_outs["bias1"], biasB[1][:], cL, reads=[gBB[1]])

            if DEV_STAGE == "bounds":
                return
            for qg in range(4):
                if nexth is not None and qg > 0:
                    wload_job(nexth, qg - 1)
                if bg_fn is not None and qg > 0:
                    bg_fn()
                for hh in range(2):
                    nkt = 4 * qg + 4

                    def emit_score(kt, qg=qg, hh=hh):
                        diag = kt >= 4 * qg
                        c0 = 128 * kt - 512 * qg if diag else 0
                        eb, geb = ebuf[kt % 3], gE[kt % 3]
                        ps, gps = [(psA, gA), (psB, gB), (psC, gCps)][kt % 3]
                        for ri in range(2):
                            r_ = 2 * hh + ri
                            mm(ps[:, c0:512],
                               kf[:, r_ * S + kt * 128:
                                  r_ * S + kt * 128 + 128],
                               qf[:, r_ * S + qg * 512 + c0:
                                  r_ * S + (qg + 1) * 512],
                               ri == 0, ri == 1,
                               reads=[gKF[r_], gQF[r_]], writes=[gps],
                               inc=cT if ri == 1 else None)
                        act(eb[:, c0:512], ps[:, c0:512], AF.Exp,
                            bias=biasB[hh][:],
                            reads=[gps, gBB[hh]], writes=[geb], inc=cA)
                        if diag:
                            vop("tt", eb[:, c0:c0 + 128], eb[:, c0:c0 + 128],
                                mask01[:], ALU.mult, eng="gpsimd",
                                reads=[gC], writes=[geb], inc=cP)

                    def emit_pv(kt, qg=qg, hh=hh):
                        diag = kt >= 4 * qg
                        c0 = 128 * kt - 512 * qg if diag else 0
                        eb, geb = ebuf[kt % 3], gE[kt % 3]
                        sub0 = c0 // 128
                        for sub in range(sub0, 4):
                            ecol = slice(sub * 128, sub * 128 + 128)
                            start_pv = kt == 0
                            stop_pv = kt == 4 * qg + sub
                            bank = 2 * hh + (sub // 2)
                            reg = psPV[bank][:, (sub % 2) * 256:
                                             (sub % 2) * 256 + 256]
                            mm(reg, eb[:, ecol], vt[:, kt * DH:(kt + 1) * DH],
                               start_pv and sub % 2 == 0, stop_pv,
                               reads=[geb, gVT[kt]], writes=[gPV[bank]],
                               inc=cT if stop_pv else None)
                            rcol_i = (qg % 2) * 8 + hh * 4 + sub
                            mm(psR[:, rcol_i:rcol_i + 1], eb[:, ecol],
                               ones_c[:],
                               start_pv and sub == 0, stop_pv,
                               reads=[geb, gC, gVT[kt]], writes=[gR],
                               inc=cT if sub == 3 else None)

                    for kt in range(nkt):
                        emit_score(kt)
                        if kt >= 1:
                            emit_pv(kt - 1)
                    emit_pv(nkt - 1)
                    if bg_fn is not None and hh == 0 and qg >= 2:
                        bg_fn()
                    for sub in range(4):
                        bank = 2 * hh + (sub // 2)
                        reg = psPV[bank][:, (sub % 2) * 256:
                                         (sub % 2) * 256 + 256]
                        acopy(pvsb[:, (hh * 4 + sub) * 257:
                                   (hh * 4 + sub) * 257 + 256], reg,
                              reads=[gPV[bank]], writes=[gPVS], inc=cA)
                    acopy(rsb[:, hh * 4:hh * 4 + 4],
                          psR[:, (qg % 2) * 8 + hh * 4:
                              (qg % 2) * 8 + hh * 4 + 4],
                          reads=[gR], writes=[gRS], inc=cA)
                if DEV_STAGE == "loop":
                    continue
                flush_transposes()
                onqb, gONb = onq[qg % 2], gON[qg % 2]
                for sub in range(4):
                    p1 = pvsb[:, sub * 257:sub * 257 + 256]
                    p2 = pvsb[:, (4 + sub) * 257:(4 + sub) * 257 + 256]
                    vop("recip", rmsc[:, 8:9], rsb[:, sub:sub + 1],
                        reads=[gRS], writes=[gRMS], inc=cV)
                    vop("recip", rmsc[:, 9:10], rsb[:, 4 + sub:5 + sub],
                        reads=[gRS], writes=[gRMS], inc=cV)
                    vop("tt", rmsc[:, 10:11], rmsc[:, 9:10], lam_sb[:],
                        ALU.mult, reads=[gLAM], writes=[gRMS], inc=cV)
                    vop("ts", odiff[:], p1, rmsc[:, 8:9], ALU.mult,
                        reads=[gPVS, gRMS], writes=[gOD], inc=cV)
                    vop("ts", otmp[:], p2, rmsc[:, 10:11], ALU.mult,
                        reads=[gPVS, gRMS], writes=[gOT], inc=cV)
                    vop("tt", odiff[:], odiff[:], otmp[:], ALU.subtract,
                        reads=[gOT], writes=[gOD], inc=cV)
                    act(osq[:], odiff[:], AF.Square,
                        reads=[gOD], writes=[gOSQ], inc=cA)
                    vop("red", rmsc[:, 0:8],
                        osq[:].rearrange("p (a b) -> p a b", a=8), ALU.add,
                        reads=[gOSQ], writes=[gRMS], inc=cV)
                    act(rmsc[:, 0:8], rmsc[:, 0:8], AF.Ln, bias=epsrms[:],
                        scale=1.0 / 32, reads=[gRMS, gC], writes=[gRMS],
                        inc=cA)
                    act(rmsc[:, 0:8], rmsc[:, 0:8], AF.Exp, scale=-0.5,
                        reads=[gRMS], writes=[gRMS], inc=cA)
                    for g8 in range(8):
                        vop("ts", onqb[:, sub * DH + g8 * 32:
                                       sub * DH + g8 * 32 + 32],
                            odiff[:, g8 * 32:(g8 + 1) * 32],
                            rmsc[:, g8:g8 + 1], ALU.mult,
                            reads=[gOD, gRMS], writes=[gONb], inc=cV)
                pend_transp.append((onqb, head, qg, gONb))

        for i in range(3):
            wload_job(0, i)
        em.do("sync", lambda e: e.nop(), waits=tok_ln)
        npA = (len(qA_jobs) + 11) // 12 if quant_on else 0
        bgA = (lambda: emit_passA(npA)) if quant_on else None
        for head in range(min(4, dev_heads)):
            emit_head(head, bgA)
            if quant_on:
                emit_passA(npA)
        if dev_heads > 4:
            emit_head(4, bgA)
        if quant_on:
            emit_passA(len(qA_jobs))
            emit_gamma(0)
            emit_gamma(1)
            emit_gamma(2)
            if "thr" in taps:
                dma("sync", tap_outs["thr"], thr_sb[:], cL, reads=[gTHR[2]])
        bgB = (lambda: emit_passB(7)) if quant_on else None
        for head in range(5, min(H, dev_heads)):
            emit_head(head, bgB)
            if quant_on:
                emit_passB(7)
        flush_transposes()
        if quant_on:
            emit_passB(len(qB_jobs))
    tok_attn = [(cV, cV.n), (cA, cA.n), (cT, cT.n), (cP, cP.n)] + all_dma_toks()
    at_stack.close()

    wo_stack = ExitStack()
    if phases >= 4:
        sbw = lambda name, shape, dt: wo_stack.enter_context(
            nc.sbuf_tensor(name, shape, dt))
        wostage = sbw("wostage", [128, NCD * 256], F32)
        gWOS = Guard()
        wobf = sbw("wobf", [128, NCD * 512], BF16)
        gWOB = Guard()
        oslice = [sbw(f"oslice{i}", [128, NCD * 512], BF16) for i in range(2)]
        gOSL = [Guard(), Guard()]
        sstage = sbw("sstage", [128, 512], BF16)
        gSST = Guard()

        for ds in range(4):
            dcols = slice(ds * 512, (ds + 1) * 512)
            for half in range(2):
                hcols = slice(ds * 512 + half * 256, ds * 512 + half * 256 + 256)
                dma_chunked("sync",
                            wostage[:].rearrange("p (c n) -> p c n", c=NCD),
                            woT[:, hcols].rearrange("(c p) n -> p c n", p=128),
                            cL, 2, NCD, writes=[gWOS],
                            waits=tok_attn if ds == 0 and half == 0 else ())
                vop("ts",
                    wobf[:].rearrange("p (c n) -> p c n", c=NCD)[
                        :, :, half * 256:(half + 1) * 256],
                    wostage[:].rearrange("p (c n) -> p c n", c=NCD),
                    OML, ALU.mult,
                    reads=[gWOS], writes=[gWOB], inc=cV)
            for qg in range(4):
                osl, gosl = oslice[qg % 2], gOSL[qg % 2]
                for c in range(NCD):
                    row0 = (c * 4 + qg) * 128
                    dma("sync", osl[:, c * 512:(c + 1) * 512],
                        o_scr[row0:row0 + 128, :], cL, writes=[gosl],
                        waits=[o_wr_toks[(c, qg)]] if phases >= 3 else ())
                for sub in range(4):
                    ps, gps = (psA, gA) if sub % 2 == 0 else (psB, gB)
                    for c in range(NCD):
                        mm(ps[:], osl[:, c * 512 + sub * 128:
                                      c * 512 + sub * 128 + 128],
                           wobf[:, c * 512:(c + 1) * 512],
                           c == 0, c == NCD - 1,
                           reads=[gosl, gWOB], writes=[gps],
                           inc=cT if c == NCD - 1 else None)
                    acopy(sstage[:], ps[:], reads=[gps], writes=[gSST],
                          inc=cA)
                    t_ = qg * 4 + sub
                    dma("scalar", sattn[t_ * 128:(t_ + 1) * 128, dcols],
                        sstage[:], cS, reads=[gSST])
    tok_wo = [(cV, cV.n), (cA, cA.n), (cT, cT.n), (cP, cP.n)] + all_dma_toks()
    tok_sattn = [(c, c.n) for c in cSs if c.n > 0]
    wo_stack.close()
    q_stack.close()
    wq_stack.close()

    ffn_stack = ExitStack()
    if phases >= 5:
        sbf = lambda name, shape, dt: ffn_stack.enter_context(
            nc.sbuf_tensor(name, shape, dt))
        hid = sbf("hid", [128, NH3 * 512], BF16)
        gHID = [Guard() for _ in range(NH3)]
        q12 = [sbf(f"q12_{i}", [128, NCD * 128], BF16) for i in range(4)]
        gQ12 = [Guard() for _ in range(4)]
        q3sl = [sbf(f"q3sl{i}", [128, NH3 * 256], BF16) for i in range(2)]
        gQ3 = [Guard(), Guard()]
        sigb = sbf("sigb", [128, 512], F32)
        gSIG = Guard()
        xsl = sbf("xsl", [128, 512], F32)
        gXSL = [Guard(), Guard()]
        ssl = sbf("ssl", [128, 512], BF16)
        gSSL = [Guard(), Guard()]
        sfup = sbf("sfup", [128, 256], F32)
        gSF = Guard()
        ost = sbf("ost", [128, 512], F32)
        gOST = [Guard(), Guard()]

        tok_q = tok_wo
        q3v = q3s.rearrange("(j p) n -> p j n", p=128)
        for blk in range(4):
            bcols = slice(blk * 512, (blk + 1) * 512)
            for j in range(NH3):
                b1, g1 = q12[(2 * j) % 4], gQ12[(2 * j) % 4]
                b2, g2 = q12[(2 * j + 1) % 4], gQ12[(2 * j + 1) % 4]
                dma("sync", b1[:], q1s[j * 128:(j + 1) * 128, :], cL,
                    writes=[g1], waits=tok_q if blk == 0 and j < 2 else ())
                dma("sync", b2[:], q2s[j * 128:(j + 1) * 128, :], cL,
                    writes=[g2], waits=tok_q if blk == 0 and j < 2 else ())
                for c in range(NCD):
                    mm(psA[:], b1[:, c * 128:(c + 1) * 128], xnc(c, bcols),
                       c == 0, c == NCD - 1, reads=[g1, gXNq[blk]],
                       writes=[gA], inc=cT if c == NCD - 1 else None)
                for c in range(NCD):
                    mm(psB[:], b2[:, c * 128:(c + 1) * 128], xnc(c, bcols),
                       c == 0, c == NCD - 1, reads=[g2, gXNq[blk]],
                       writes=[gB], inc=cT if c == NCD - 1 else None)
                act(sigb[:], psA[:], AF.Sigmoid,
                    reads=[gA], writes=[gSIG], inc=cA)
                vop("tt", hid[:, j * 512:(j + 1) * 512], sigb[:], psB[:],
                    ALU.mult, reads=[gSIG, gB], writes=[gHID[j]], inc=cV)
            if blk == 0 and "hid" in taps:
                dma("sync", tap_outs["hid"], hid[:], cL,
                    reads=[gHID[jj] for jj in range(NH3)])
            for dch in range(8):
                q3b, gq3 = q3sl[dch % 2], gQ3[dch % 2]
                for piece in range(4):
                    dma("sync",
                        q3b[:].rearrange("p (j n) -> p j n", j=NH3)[
                            :, piece * 11:(piece + 1) * 11, :],
                        q3v[:, piece * 11:(piece + 1) * 11,
                            dch * 256:(dch + 1) * 256],
                        cL, writes=[gq3],
                        waits=tok_q if blk == 0 and dch < 2 else ())
                dc = slice(dch * 256, (dch + 1) * 256)
                for t_ in range(4):
                    trow = blk * 4 + t_
                    nb = (dch * 4 + t_) % 2
                    xsl_v = xsl[:, nb * 256:(nb + 1) * 256]
                    ssl_v = ssl[:, nb * 256:(nb + 1) * 256]
                    ost_v = ost[:, nb * 256:(nb + 1) * 256]
                    dma("sync", xsl_v,
                        x_in[trow * 128:(trow + 1) * 128, dc], cL,
                        writes=[gXSL[nb]])
                    dma("sync", ssl_v,
                        sattn[trow * 128:(trow + 1) * 128, dc], cL,
                        writes=[gSSL[nb]],
                        waits=tok_sattn if phases >= 4 else ())
                    idx = (dch * 4 + t_) % 8
                    bank = idx % 4
                    reg = psPV[bank][:, (idx // 4) * 256:
                                     (idx // 4) * 256 + 256]
                    tok = None
                    for j in range(NH3):
                        tok = mm(reg,
                                 hid[:, j * 512 + t_ * 128:
                                     j * 512 + t_ * 128 + 128],
                                 q3b[:, j * 256:(j + 1) * 256],
                                 j == 0, j == NH3 - 1,
                                 reads=[gHID[j], gq3], writes=[gPV[bank]],
                                 inc=cT if j == NH3 - 1 else None)
                    for j in range(NH3):
                        gHID[j].r.append(tok)
                    gq3.r.append(tok)
                    act(sigb[:, 0:256], reg, AF.Silu,
                        reads=[gPV[bank]], writes=[gSIG], inc=cA)
                    act(sfup[:], ssl_v, AF.Identity,
                        reads=[gSSL[nb]], writes=[gSF], inc=cA)
                    vop("tt", sigb[:, 256:512], sigb[:, 0:256], sfup[:],
                        ALU.mult, reads=[gSIG, gSF], writes=[gSIG], inc=cV)
                    vop("tt", ost_v, sigb[:, 256:512], xsl_v,
                        ALU.add, reads=[gSIG, gXSL[nb]], writes=[gOST[nb]],
                        inc=cV)
                    dma("gpsimd", out_d[trow * 128:(trow + 1) * 128, dc],
                        ost_v, cS, reads=[gOST[nb]])

    em.do("sync", lambda e: e.nop(), waits=all_dma_toks())

    with nc.Block() as block:
        for eng in ENGINES:
            stream = em.streams[eng]

            def runner(engine, _s=stream):
                for f in _s:
                    f(engine)
            getattr(block, eng)(runner)

    ffn_stack.close()
    st.close()
    return nc, em.counts


def kernel(L0_attn_x, L0_attn_l_q1, L0_attn_l_k1, L0_attn_l_q2, L0_attn_l_k2,
           L0_attn_wq, L0_attn_wk, L0_attn_wv, L0_attn_wo,
           L0_ffn_w1, L0_ffn_w2, L0_ffn_w3):
    x = np.ascontiguousarray(np.asarray(L0_attn_x, dtype=np.float32))
    f32 = lambda a: np.asarray(a, dtype=np.float32)
    tr = lambda a: np.ascontiguousarray(np.asarray(a, dtype=np.float32).T)
    shared = {
        "wqT": tr(L0_attn_wq), "wkT": tr(L0_attn_wk), "wvT": tr(L0_attn_wv),
        "woT": tr(L0_attn_wo), "w1T": tr(L0_ffn_w1), "w2T": tr(L0_ffn_w2),
        "w3T": tr(L0_ffn_w3),
        "lvec": np.ascontiguousarray(np.concatenate(
            [f32(L0_attn_l_q1), f32(L0_attn_l_k1),
             f32(L0_attn_l_q2), f32(L0_attn_l_k2)]).reshape(1, 32)),
        "mask01": np.triu(np.ones((128, 128))).astype(ml_dtypes.bfloat16),
        "idenb": np.eye(128).astype(ml_dtypes.bfloat16),
        "idenf": np.eye(128, dtype=np.float32),
    }
    in_maps = []
    for c in range(8):
        m = dict(shared)
        m["x"] = x[c]
        m["xT"] = np.ascontiguousarray(x[c].T)
        in_maps.append(m)
    nc, _ = build_nc(taps=TAPS, phases=PHASES)
    res = run_bass_kernel_spmd(nc, in_maps, core_ids=list(range(8)))
    global LAST_RESULTS
    LAST_RESULTS = res.results
    return np.stack([np.asarray(res.results[c]["out"]) for c in range(8)], 0)


LAST_RESULTS = None


# version 19
# speedup vs baseline: 1.0247x; 1.0247x over previous
import math
from contextlib import ExitStack

import numpy as np
import ml_dtypes
import concourse.bass as bass
import concourse.mybir as mybir
from concourse.bass_utils import run_bass_kernel_spmd

F32 = mybir.dt.float32
BF16 = mybir.dt.bfloat16
AF = mybir.ActivationFunctionType
ALU = mybir.AluOpType
AX = mybir.AxisListType

S = 2048
D = 2048
H = 8
DH = 256
HID = 5632
NCD = D // 128
NST = S // 128
NH3 = HID // 128
L_INIT = 0.8 - 0.6 * math.exp(-0.3)
OML = 1.0 - L_INIT
EPS_LN = 1e-5
EPS_RMS = 1e-6
EPS_Q = 1e-8

TAPS = ()
DEV_HEADS = 8
DEV_QUANT = True
DEV_STAGE = 'full'
DEV_PREFETCH = True
PHASES = 99

ENGINES = ("sync", "tensor", "vector", "scalar", "gpsimd")


class Ch:
    def __init__(self, nc, ctx, name):
        self.sem = ctx.enter_context(nc.semaphore(name))
        self.n = 0
        self.name = name


class Guard:
    __slots__ = ("w", "r")

    def __init__(self):
        self.w = None
        self.r = []


class Emit:
    OWN = {"tensor": "cT"}

    def __init__(self):
        self.streams = {e: [] for e in ENGINES}
        self.waited = {e: {} for e in ENGINES}
        self.counts = {e: 0 for e in ENGINES}

    def _wait(self, eng, ch, tok):
        if tok is None or tok <= 0:
            return
        if self.OWN.get(eng) == ch.name:
            return
        w = self.waited[eng]
        if w.get(ch.name, 0) >= tok:
            return
        w[ch.name] = tok
        self.streams[eng].append(lambda e, s=ch.sem, t=tok: e.wait_ge(s, t))

    def do(self, eng, fn, reads=(), writes=(), waits=(), inc=None, inc_n=1):
        need = []
        for g in reads:
            if g.w is not None:
                need.append(g.w)
        for g in writes:
            if g.w is not None:
                need.append(g.w)
            need.extend(g.r)
        need.extend(waits)
        for ch, tok in need:
            self._wait(eng, ch, tok)
        tok = None
        if inc is not None:
            inc.n += inc_n
            tok = (inc, inc.n)
            self.streams[eng].append(
                lambda e, f=fn, s=inc.sem, v=inc_n: f(e).then_inc(s, v))
            for g in writes:
                g.w = tok
                g.r = []
            for g in reads:
                g.r.append(tok)
        else:
            self.streams[eng].append(lambda e, f=fn: f(e))
        self.counts[eng] += 1
        return tok


def build_nc(taps=(), phases=99, dev_heads=None, dev_quant=None):
    dev_heads = DEV_HEADS if dev_heads is None else dev_heads
    dev_quant = DEV_QUANT if dev_quant is None else dev_quant
    nc = bass.Bass("TRN2", target_bir_lowering=False)
    em = Emit()
    st = ExitStack()

    ap = lambda t: t.ap()
    x_in = ap(nc.dram_tensor("x", [S, D], F32, kind="ExternalInput"))
    xT_in = ap(nc.dram_tensor("xT", [D, S], F32, kind="ExternalInput"))
    wqT = ap(nc.dram_tensor("wqT", [D, 2 * D], F32, kind="ExternalInput"))
    wkT = ap(nc.dram_tensor("wkT", [D, 2 * D], F32, kind="ExternalInput"))
    wvT = ap(nc.dram_tensor("wvT", [D, D], F32, kind="ExternalInput"))
    woT = ap(nc.dram_tensor("woT", [D, D], F32, kind="ExternalInput"))
    w1T = ap(nc.dram_tensor("w1T", [D, HID], F32, kind="ExternalInput"))
    w2T = ap(nc.dram_tensor("w2T", [D, HID], F32, kind="ExternalInput"))
    w3T = ap(nc.dram_tensor("w3T", [HID, D], F32, kind="ExternalInput"))
    lvec = ap(nc.dram_tensor("lvec", [1, 32], F32, kind="ExternalInput"))
    mask01_in = ap(nc.dram_tensor("mask01", [128, 128], BF16,
                                  kind="ExternalInput"))
    idenb_in = ap(nc.dram_tensor("idenb", [128, 128], BF16,
                                 kind="ExternalInput"))
    idenf_in = ap(nc.dram_tensor("idenf", [128, 128], F32,
                                 kind="ExternalInput"))
    out_d = ap(nc.dram_tensor("out", [S, D], F32, kind="ExternalOutput"))

    def dscr(name, shape, dt):
        kind = "ExternalOutput" if name in taps else "Internal"
        return ap(nc.dram_tensor(name, shape, dt, kind=kind))

    stat_scr = dscr("stat_scr", [2, S], F32)
    q1s = dscr("q1s", [NH3 * 128, NCD * 128], BF16)
    q2s = dscr("q2s", [NH3 * 128, NCD * 128], BF16)
    q3s = dscr("q3s", [HID, D], BF16)
    o_scr = dscr("o_scr", [NCD * 4 * 128, 512], BF16)
    sattn = dscr("sattn", [S, D], BF16)

    tap_outs = {}
    for tname, shape, dt in [
        ("lam", [128, 1], F32), ("xn", [128, NCD * S], BF16),
        ("qf", [128, 4 * S], BF16), ("kf", [128, 4 * S], BF16),
        ("vt", [128, NST * DH], BF16), ("thr", [128, 3], F32),
        ("bias0", [128, 1], F32), ("bias1", [128, 1], F32),
        ("sc11", [1, 16], F32), ("hid", [128, NH3 * 512], BF16),
    ]:
        if tname in taps:
            tap_outs[tname] = ap(
                nc.dram_tensor("tap_" + tname, shape, dt, kind="ExternalOutput"))

    cLs = [Ch(nc, st, f"cL{i}") for i in range(8)]
    cSs = [Ch(nc, st, f"cS{i}") for i in range(4)]
    cGs = [Ch(nc, st, f"cG{i}") for i in range(4)]
    _lane = {"sync": [0], "scalar": [0], "gpsimd": [0]}
    _pools = {"sync": cLs, "scalar": cSs, "gpsimd": cGs}

    def dma_lane(queue):
        pool = _pools[queue]
        i = _lane[queue][0]
        _lane[queue][0] = (i + 1) % len(pool)
        return pool[i]

    def all_dma_toks():
        return [(c, c.n) for c in cLs + cSs + cGs if c.n > 0]
    cT = Ch(nc, st, "cT")
    cV = Ch(nc, st, "cV")
    cA = Ch(nc, st, "cA")
    cP = Ch(nc, st, "cP")
    cL = None
    cS = None

    psA = st.enter_context(nc.psum_tensor("psA", [128, 512], F32))
    psB = st.enter_context(nc.psum_tensor("psB", [128, 512], F32))
    psPV = [st.enter_context(nc.psum_tensor(f"psPV{i}", [128, 512], F32))
            for i in range(4)]
    psR = st.enter_context(nc.psum_tensor("psR", [128, 512], F32))
    psC = st.enter_context(nc.psum_tensor("psC", [128, 512], F32))
    gA, gB, gR = Guard(), Guard(), Guard()
    gCps = Guard()
    gPV = [Guard() for _ in range(4)]

    sbS = lambda name, shape, dt: st.enter_context(
        nc.sbuf_tensor(name, shape, dt))
    xn = sbS("xn", [128, NCD * S], BF16)
    gXNq = [Guard() for _ in range(4)]
    mask01 = sbS("mask01_s", [128, 128], BF16)
    idenb = sbS("idenb_s", [128, 128], BF16)
    idenf = sbS("idenf_s", [128, 128], F32)
    ones_r = sbS("ones_r", [1, 128], BF16)
    ones_c = sbS("ones_c", [128, 1], BF16)
    lam_sb = sbS("lam_sb", [128, 1], F32)
    gLAM = Guard()
    epsln = sbS("epsln", [128, 1], F32)
    epsrms = sbS("epsrms", [128, 1], F32)
    thr_sb = sbS("thr_sb", [128, 3], F32)
    thrn_sb = sbS("thrn_sb", [128, 3], F32)
    gTHR = [Guard() for _ in range(3)]
    lrow = sbS("lrow", [1, 32], F32)
    sc11 = sbS("sc11", [1, 16], F32)
    sc11b = sbS("sc11b", [1, 16], BF16)
    sc11b_f = sbS("sc11b_f", [1, 16], F32)
    biasB = [sbS(f"biasB{i}", [128, 1], F32) for i in range(2)]
    wq_stack = ExitStack()
    sbW = lambda name, shape, dt: wq_stack.enter_context(
        nc.sbuf_tensor(name, shape, dt))
    wstage = sbW("wstage", [128, NCD * 128], F32)
    gWS = Guard()
    wbf = [sbW(f"wbf{i}", [128, NCD * 256], BF16) for i in range(3)]
    gWB = [Guard(), Guard(), Guard()]
    gBB = [Guard(), Guard()]
    gC = Guard()
    gSC = Guard()

    def xnc(c, cols):
        return xn[:, c * S + cols.start: c * S + cols.stop]

    def dma(queue, out, in_, ch, reads=(), writes=(), waits=()):
        lane = dma_lane(queue)
        waits = list(waits) + [(lane, lane.n)]
        return em.do(queue, lambda e, o=out, i=in_: e.dma_start(out=o, in_=i),
                     reads=reads, writes=writes, waits=waits, inc=lane,
                     inc_n=16)

    def mm(out, lhsT, rhs, start, stop, reads=(), writes=(), waits=(),
           inc=None):
        return em.do("tensor",
                     lambda e, o=out, l=lhsT, r=rhs, s1=start, s2=stop:
                     e.matmul(o, l, r, start=s1, stop=s2,
                              skip_group_check=True),
                     reads=reads, writes=writes, waits=waits, inc=inc)

    def transp(out, in_, iden, reads=(), writes=(), inc=None):
        return em.do("tensor",
                     lambda e, o=out, i=in_, d=iden: e.transpose(o, i, d),
                     reads=reads, writes=writes, inc=inc)

    def act(out, in_, func, bias=None, scale=None, accum=None,
            reads=(), writes=(), waits=(), inc=None):
        def f(e, o=out, i=in_, fu=func, b=bias, s=scale, a=accum):
            kw = {}
            if b is not None:
                kw["bias"] = b
            if s is not None:
                kw["scale"] = s
            if a is not None:
                kw["accum_out"] = a
            return e.activation(o, i, fu, **kw)
        return em.do("scalar", f, reads=reads, writes=writes, waits=waits,
                     inc=inc)

    def acopy(out, in_, scale=None, reads=(), writes=(), waits=(), inc=None):
        if scale is None:
            f = lambda e, o=out, i=in_: e.copy(o, i)
        else:
            f = lambda e, o=out, i=in_, s=scale: e.mul(o, i, s)
        return em.do("scalar", f, reads=reads, writes=writes, waits=waits,
                     inc=inc)

    def vop(kind, *args, eng="vector", reads=(), writes=(), waits=(),
            inc=None):
        if kind == "tt":
            out, a, b, op = args
            f = lambda e, o=out, x=a, y=b, p=op: e.tensor_tensor(o, x, y, p)
        elif kind == "ts":
            out, a, s1, op = args
            f = lambda e, o=out, x=a, s=s1, p=op: e.tensor_scalar(
                o, x, s, None, p)
        elif kind == "red":
            out, a, op = args
            f = lambda e, o=out, x=a, p=op: e.tensor_reduce(o, x, AX.X, p)
        elif kind == "redabs":
            out, a, op = args
            f = lambda e, o=out, x=a, p=op: e.tensor_reduce(
                o, x, AX.X, p, apply_absolute_value=True)
        elif kind == "copy":
            out, a = args
            f = lambda e, o=out, x=a: e.tensor_copy(o, x)
        elif kind == "recip":
            out, a = args
            f = lambda e, o=out, x=a: e.reciprocal(o, x)
        elif kind == "memset":
            out, val = args
            f = lambda e, o=out, v=val: e.memset(o, v)
        else:
            raise ValueError(kind)
        return em.do(eng, f, reads=reads, writes=writes, waits=waits, inc=inc)

    def dma_chunked(queue, outv, inv, ch, nsplit, csize, reads=(), writes=(),
                    waits=()):
        toks = []
        per = csize // nsplit
        for i in range(nsplit):
            toks.append(dma(queue, outv[:, i * per:(i + 1) * per, :],
                            inv[:, i * per:(i + 1) * per, :], ch,
                            reads=reads, writes=writes, waits=waits))
        return toks[-1]

    dma("sync", mask01[:], mask01_in[:, :], cL, writes=[gC])
    dma("sync", idenb[:], idenb_in[:, :], cL, writes=[gC])
    dma("sync", idenf[:], idenf_in[:, :], cL, writes=[gC])
    dma("sync", lrow[:], lvec[:, :], cL, writes=[gC])
    vop("memset", ones_r[:], 1.0, writes=[gC], inc=cV)
    vop("memset", ones_c[:], 1.0, writes=[gC], inc=cV)
    vop("memset", epsln[:], EPS_LN, writes=[gC], inc=cV)
    vop("memset", epsrms[:], EPS_RMS, writes=[gC], inc=cV)

    vop("tt", sc11[0:1, 0:8], lrow[0:1, 0:8], lrow[0:1, 8:16], ALU.mult,
        reads=[gC], writes=[gSC], inc=cV)
    vop("red", sc11[0:1, 8:9], sc11[0:1, 0:8], ALU.add, writes=[gSC], inc=cV)
    vop("tt", sc11[0:1, 0:8], lrow[0:1, 16:24], lrow[0:1, 24:32], ALU.mult,
        writes=[gSC], inc=cV)
    vop("red", sc11[0:1, 9:10], sc11[0:1, 0:8], ALU.add, writes=[gSC], inc=cV)
    act(sc11[0:1, 10:12], sc11[0:1, 8:10], AF.Exp,
        reads=[gSC], writes=[gSC], inc=cA)
    vop("tt", sc11[0:1, 12:13], sc11[0:1, 10:11], sc11[0:1, 11:12],
        ALU.subtract, writes=[gSC], inc=cV)
    vop("ts", sc11[0:1, 13:14], sc11[0:1, 12:13], L_INIT, ALU.add,
        writes=[gSC], inc=cV)
    vop("copy", sc11b[0:1, 0:1], sc11[0:1, 13:14], writes=[gSC], inc=cV)
    vop("tt", sc11[0:1, 14:15], sc11[0:1, 13:14], sc11b[0:1, 0:1],
        ALU.subtract, writes=[gSC], inc=cV)
    vop("copy", sc11b[0:1, 1:2], sc11[0:1, 14:15], writes=[gSC], inc=cV)
    mm(psR[:, 16:17], ones_r[:], sc11b[0:1, 0:1], True, False,
       reads=[gSC, gC], writes=[gR], inc=cT)
    mm(psR[:, 16:17], ones_r[:], sc11b[0:1, 1:2], False, True,
       reads=[gSC], writes=[gR], inc=cT)
    vop("copy", lam_sb[:], psR[:, 16:17], reads=[gR], writes=[gLAM], inc=cV)
    if "sc11" in taps:
        dma("sync", tap_outs["sc11"], sc11[:], cL, reads=[gSC])
    if "lam" in taps:
        dma("sync", tap_outs["lam"], lam_sb[:], cL, reads=[gLAM])

    ln_stack = ExitStack()
    if phases >= 1:
        sbl = lambda name, shape, dt: ln_stack.enter_context(
            nc.sbuf_tensor(name, shape, dt))
        xstat = sbl("xstat", [128, D], F32)
        gXS = Guard()
        sqt = sbl("sqt", [128, D], F32)
        gSQ = Guard()
        rcol = sbl("rcol", [128, 16], F32)
        mcol = sbl("mcol", [128, 16], F32)
        gCOL = Guard()
        statT = sbl("statT", [16, 256], F32)
        gSTT = Guard()
        rowf = sbl("rowf", [1, 512], F32)
        gROWF = Guard()
        rhi = sbl("rhi", [1, S], BF16)
        rlo = sbl("rlo", [1, S], BF16)
        mhi = sbl("mhi", [1, S], BF16)
        mlo = sbl("mlo", [1, S], BF16)
        gROW = Guard()
        stat1 = sbl("stat1", [128, 8], F32)
        gS1 = Guard()
        xTg = sbl("xTg", [128, NCD * 256], F32)
        gXT = Guard()

        for t in range(NST):
            dma("sync", xstat[:], x_in[t * 128:(t + 1) * 128, :], cL,
                writes=[gXS])
            vop("red", stat1[:, 0:1], xstat[:], ALU.add,
                reads=[gXS], writes=[gS1], inc=cV)
            act(sqt[:], xstat[:], AF.Square, reads=[gXS], writes=[gSQ], inc=cA)
            vop("red", stat1[:, 1:2], sqt[:], ALU.add,
                reads=[gSQ], writes=[gS1], inc=cV)
            vop("ts", stat1[:, 2:3], stat1[:, 0:1], 1.0 / D, ALU.mult,
                writes=[gS1], inc=cV)
            vop("ts", stat1[:, 3:4], stat1[:, 1:2], 1.0 / D, ALU.mult,
                writes=[gS1], inc=cV)
            vop("tt", stat1[:, 4:5], stat1[:, 2:3], stat1[:, 2:3], ALU.mult,
                writes=[gS1], inc=cV)
            vop("tt", stat1[:, 5:6], stat1[:, 3:4], stat1[:, 4:5],
                ALU.subtract, writes=[gS1], inc=cV)
            act(stat1[:, 6:7], stat1[:, 5:6], AF.Ln, bias=epsln[:],
                reads=[gS1, gC], writes=[gS1], inc=cA)
            act(rcol[:, t:t + 1], stat1[:, 6:7], AF.Exp, scale=-0.5,
                reads=[gS1], writes=[gCOL], inc=cA)
            vop("tt", mcol[:, t:t + 1], stat1[:, 2:3], rcol[:, t:t + 1],
                ALU.mult, reads=[gCOL, gS1], writes=[gCOL], inc=cV)
        transp(psR[0:16, 0:128], rcol[:], idenf[:],
               reads=[gCOL, gC, gR], writes=[gR], inc=cT)
        vop("copy", statT[:, 0:128], psR[0:16, 0:128], reads=[gR],
            writes=[gSTT], inc=cV)
        transp(psR[0:16, 128:256], mcol[:], idenf[:], reads=[gCOL],
               writes=[gR], inc=cT)
        vop("copy", statT[:, 128:256], psR[0:16, 128:256], reads=[gR],
            writes=[gSTT], inc=cV)
        dma("gpsimd", stat_scr[0:1, :].rearrange("a (b c) -> (a b) c", b=16),
            statT[:, 0:128], cS, reads=[gSTT])
        dma("gpsimd", stat_scr[1:2, :].rearrange("a (b c) -> (a b) c", b=16),
            statT[:, 128:256], cS, reads=[gSTT])
        tok_stat_l = [(c, c.n) for c in cSs + cGs if c.n > 0]
        for qg in range(4):
            cols = slice(qg * 512, qg * 512 + 512)
            dma("sync", rowf[:], stat_scr[0:1, cols], cL, writes=[gROWF],
                waits=tok_stat_l)
            vop("copy", rhi[0:1, cols], rowf[:], reads=[gROWF], writes=[gROW],
                inc=cV)
            vop("tt", rowf[:], rowf[:], rhi[0:1, cols], ALU.subtract,
                reads=[gROW], writes=[gROWF], inc=cV)
            vop("copy", rlo[0:1, cols], rowf[:], reads=[gROWF], writes=[gROW],
                inc=cV)
            dma("sync", rowf[:], stat_scr[1:2, cols], cL, writes=[gROWF])
            vop("copy", mhi[0:1, cols], rowf[:], reads=[gROWF], writes=[gROW],
                inc=cV)
            vop("tt", rowf[:], rowf[:], mhi[0:1, cols], ALU.subtract,
                reads=[gROW], writes=[gROWF], inc=cV)
            vop("copy", mlo[0:1, cols], rowf[:], reads=[gROWF], writes=[gROW],
                inc=cV)
        for qg in range(4):
            cols = slice(qg * 512, (qg + 1) * 512)
            pr = psPV[2 * (qg % 2)]
            pm = psPV[2 * (qg % 2) + 1]
            gr = gPV[2 * (qg % 2)]
            gm = gPV[2 * (qg % 2) + 1]
            mm(pr[:], ones_r[:], rhi[0:1, cols], True, False,
               reads=[gROW, gC], writes=[gr], inc=cT)
            mm(pr[:], ones_r[:], rlo[0:1, cols], False, True,
               reads=[gROW], writes=[gr], inc=cT)
            mm(pm[:], ones_r[:], mhi[0:1, cols], True, False,
               reads=[gROW], writes=[gm], inc=cT)
            mm(pm[:], ones_r[:], mlo[0:1, cols], False, True,
               reads=[gROW], writes=[gm], inc=cT)
            for half in range(2):
                hc = slice(qg * 512 + half * 256, qg * 512 + half * 256 + 256)
                dma_chunked("sync",
                            xTg[:].rearrange("p (c n) -> p c n", c=NCD),
                            xT_in[:, hc].rearrange("(c p) n -> p c n", p=128),
                            cL, 4, NCD, writes=[gXT])
                for c in range(NCD):
                    xg = xTg[:, c * 256:(c + 1) * 256]
                    prh = pr[:, half * 256:(half + 1) * 256]
                    pmh = pm[:, half * 256:(half + 1) * 256]
                    vop("tt", sqt[:, 0:256], xg, prh, ALU.mult,
                        reads=[gXT, gr], writes=[gSQ], inc=cV)
                    vop("tt", xnc(c, hc), sqt[:, 0:256], pmh, ALU.subtract,
                        reads=[gSQ, gm], writes=[gXNq[qg]], inc=cV)
        if "xn" in taps:
            dma("sync", tap_outs["xn"], xn[:], cL, reads=list(gXNq))
    tok_ln = [(cV, cV.n), (cA, cA.n), (cT, cT.n)] + all_dma_toks()
    ln_stack.close()

    q_stack = ExitStack()
    quant_on = phases >= 2 and dev_quant
    if quant_on:
        sbq = lambda name, shape, dt: q_stack.enter_context(
            nc.sbuf_tensor(name, shape, dt))
        stA = sbq("stA", [128, 2 * 1408], F32)
        gSA = [Guard(), Guard()]
        accw = sbq("accw", [128, 4], F32)
        gACC = Guard()
        acchl = sbq("acchl", [128, 2], BF16)
        stB = sbq("stB", [128, 2 * 2048], F32)
        gSB = [Guard(), Guard()]
        qneg = sbq("qneg", [128, 2048], BF16)
        gQN = Guard()
        qout = sbq("qout", [128, 2 * 2048], BF16)
        gQO = [Guard(), Guard()]
        wsrcs = [(w1T, D, HID), (w2T, D, HID), (w3T, HID, D)]

    qA_jobs = []
    if quant_on:
        for wi, (wsrc, nr, ncl) in enumerate(wsrcs):
            pw = 1408 if ncl % 1408 == 0 else 1024
            assert ncl % pw == 0
            for tile in range(nr // 128):
                for pc in range(ncl // pw):
                    qA_jobs.append((wi, tile, pc, pw,
                                    tile == 0 and pc == 0))
    qA_pos = [0]

    def emit_passA(njobs):
        end = min(qA_pos[0] + njobs, len(qA_jobs))
        while qA_pos[0] < end:
            wi, tile, pc, pw, first = qA_jobs[qA_pos[0]]
            wsrc, nr, ncl = wsrcs[wi]
            b = qA_pos[0] % 2
            qA_pos[0] += 1
            stv = stA[:, b * 1408:b * 1408 + pw]
            dma("sync", stv,
                wsrc[tile * 128:(tile + 1) * 128,
                     pc * pw:(pc + 1) * pw], cL, writes=[gSA[b]])
            vop("redabs", accw[:, 3:4], stv, ALU.add,
                reads=[gSA[b]], writes=[gACC], inc=cV)
            if first:
                vop("copy", accw[:, wi:wi + 1], accw[:, 3:4],
                    reads=[gACC], writes=[gACC], inc=cV)
            else:
                vop("tt", accw[:, wi:wi + 1], accw[:, wi:wi + 1],
                    accw[:, 3:4], ALU.add,
                    reads=[gACC], writes=[gACC], inc=cV)

    def emit_gamma(wi):
        _, nr, ncl = wsrcs[wi]
        vop("copy", acchl[:, 0:1], accw[:, wi:wi + 1],
            reads=[gACC], writes=[gACC], inc=cV)
        vop("tt", accw[:, 3:4], accw[:, wi:wi + 1], acchl[:, 0:1],
            ALU.subtract, writes=[gACC], inc=cV)
        vop("copy", acchl[:, 1:2], accw[:, 3:4], writes=[gACC], inc=cV)
        mm(psR[0:1, 17:18], acchl[:, 0:1], ones_c[:], True, False,
           reads=[gACC, gC], writes=[gR], inc=cT)
        mm(psR[0:1, 17:18], acchl[:, 1:2], ones_c[:], False, True,
           reads=[gACC], writes=[gR], inc=cT)
        cnt = nr * ncl
        vop("ts", sc11[0:1, 15:16], psR[0:1, 17:18], 0.5 / cnt, ALU.mult,
            reads=[gR], writes=[gSC], inc=cV)
        vop("ts", sc11[0:1, 15:16], sc11[0:1, 15:16], 0.5 * EPS_Q, ALU.add,
            writes=[gSC], inc=cV)
        vop("copy", sc11b[0:1, 2:3], sc11[0:1, 15:16], writes=[gSC], inc=cV)
        vop("tt", sc11[0:1, 14:15], sc11[0:1, 15:16], sc11b[0:1, 2:3],
            ALU.subtract, writes=[gSC], inc=cV)
        vop("copy", sc11b[0:1, 3:4], sc11[0:1, 14:15], writes=[gSC], inc=cV)
        mm(psR[:, 18:19], ones_r[:], sc11b[0:1, 2:3], True, False,
           reads=[gSC], writes=[gR], inc=cT)
        mm(psR[:, 18:19], ones_r[:], sc11b[0:1, 3:4], False, True,
           reads=[gSC], writes=[gR], inc=cT)
        vop("copy", thr_sb[:, wi:wi + 1], psR[:, 18:19],
            reads=[gR], writes=[gTHR[wi]], inc=cV)
        vop("ts", thrn_sb[:, wi:wi + 1], thr_sb[:, wi:wi + 1], -1.0,
            ALU.mult, reads=[gTHR[wi]], writes=[gTHR[wi]], inc=cV)

    qB_jobs = []
    if quant_on:
        for j in range(NH3):
            qB_jobs.append((0,
                            w1T[:, j * 128:(j + 1) * 128].rearrange(
                                "(c p) n -> p c n", p=128),
                            q1s[j * 128:(j + 1) * 128, :], True))
            qB_jobs.append((1,
                            w2T[:, j * 128:(j + 1) * 128].rearrange(
                                "(c p) n -> p c n", p=128),
                            q2s[j * 128:(j + 1) * 128, :], True))
        for tile in range(NH3):
            qB_jobs.append((2, w3T[tile * 128:(tile + 1) * 128, :],
                            q3s[tile * 128:(tile + 1) * 128, :], False))
    qB_pos = [0]

    def emit_passB(njobs):
        end = min(qB_pos[0] + njobs, len(qB_jobs))
        while qB_pos[0] < end:
            wi, src, dst, is3d = qB_jobs[qB_pos[0]]
            b = qB_pos[0] % 2
            qB_pos[0] += 1
            stv = stB[:, b * 2048:(b + 1) * 2048]
            qo = qout[:, b * 2048:(b + 1) * 2048]
            dma("sync", stv.rearrange("p (c n) -> p c n", c=NCD) if is3d
                else stv, src, cL, writes=[gSB[b]])
            vop("ts", qo, stv, thr_sb[:, wi:wi + 1], ALU.is_ge,
                reads=[gSB[b], gTHR[wi]], writes=[gQO[b]], inc=cV)
            vop("ts", qneg[:], stv, thrn_sb[:, wi:wi + 1], ALU.is_le,
                reads=[gSB[b], gTHR[wi]], writes=[gQN], inc=cV)
            vop("tt", qo, qo, qneg[:], ALU.subtract,
                reads=[gQN], writes=[gQO[b]], inc=cV)
            dma("gpsimd", dst, qo, cS, reads=[gQO[b]])

    at_stack = ExitStack()
    o_wr_toks = {}
    if phases >= 3:
        sba = lambda name, shape, dt: at_stack.enter_context(
            nc.sbuf_tensor(name, shape, dt))
        qf = sba("qf", [128, 4 * S], BF16)
        kf = sba("kf", [128, 4 * S], BF16)
        gQF = [Guard() for _ in range(4)]
        gKF = [Guard() for _ in range(4)]
        vt = sba("vt", [128, NST * DH], BF16)
        gVT = [Guard() for _ in range(NST)]
        sqb = sba("sqb", [128, 512], BF16)
        gSQB = Guard()
        ebuf = [sba(f"ebuf{i}", [128, 512], BF16) for i in range(3)]
        gE = [Guard(), Guard(), Guard()]
        pvsb = sba("pvsb", [128, 8 * 257], F32)
        gPVS = Guard()
        rsb = sba("rsb", [128, 8], F32)
        gRS = Guard()
        onq = [sba(f"onq{i}", [128, 4 * DH], F32) for i in range(2)]
        gON = [Guard(), Guard()]
        odiff = sba("odiff", [128, 256], F32)
        gOD = Guard()
        otmp = sba("otmp", [128, 256], F32)
        gOT = Guard()
        osq, gOSQ = otmp, gOT
        rmsc = sba("rmsc", [128, 16], F32)
        gRMS = Guard()
        obf = [sba(f"obf{i}", [128, 512], BF16) for i in range(4)]
        gOBF = [Guard() for _ in range(4)]

        pend_transp = []
        tpi = [0]

        def flush_transposes():
            while pend_transp:
                onqb, head, qg, gONb = pend_transp.pop(0)
                for fc in range(2):
                    ob = obf[(qg % 2) * 2 + fc]
                    gob = gOBF[(qg % 2) * 2 + fc]
                    for sub in range(4):
                        reg = tpi[0] % 2
                        tpi[0] += 1
                        pt = (psA if reg == 0 else psB)[:, 0:128]
                        gpt = gA if reg == 0 else gB
                        transp(pt, onqb[:, sub * DH + fc * 128:
                                        sub * DH + fc * 128 + 128],
                               idenf[:], reads=[gONb, gC], writes=[gpt],
                               inc=cT)
                        vop("copy", ob[:, sub * 128:(sub + 1) * 128], pt,
                            reads=[gpt], writes=[gob], inc=cV)
                    row0 = ((head * 2 + fc) * 4 + qg) * 128
                    tok = dma("gpsimd", o_scr[row0:row0 + 128, :], ob[:],
                              cS, reads=[gob])
                    o_wr_toks[(head * 2 + fc, qg)] = tok

        def emit_wload(wsrc, colr, slot):
            ws, gws = wstage, gWS
            wb, gwb = wbf[slot % 3], gWB[slot % 3]
            for half in range(2):
                hc = slice(colr.start + half * 128, colr.start + half * 128 + 128)
                dma_chunked("sync", ws[:].rearrange("p (c n) -> p c n", c=NCD),
                            wsrc[:, hc].rearrange("(c p) n -> p c n", p=128),
                            cL, 2, NCD, writes=[gws])
                vop("copy",
                    wb[:].rearrange("p (c n) -> p c n", c=NCD)[
                        :, :, half * 128:(half + 1) * 128],
                    ws[:].rearrange("p (c n) -> p c n", c=NCD),
                    eng="gpsimd", reads=[gws], writes=[gwb], inc=cP)

        def proj_qk(dst, dstg, hh_local, slot):
            scale = 0.25
            wb, gwb = wbf[slot % 3], gWB[slot % 3]
            for m in range(2):
                r_ = 2 * hh_local + m
                for sg in range(4):
                    ps, gps = (psA, gA) if (m * 4 + sg) % 2 == 0 else (psB, gB)
                    for c in range(NCD):
                        mm(ps[:], wb[:, c * 256 + m * 128:
                                     c * 256 + m * 128 + 128],
                           xnc(c, slice(sg * 512, (sg + 1) * 512)),
                           c == 0, c == NCD - 1,
                           reads=[gwb] + gXNq, writes=[gps],
                           inc=cT if c == NCD - 1 else None)
                    acopy(dst[:, r_ * S + sg * 512:r_ * S + (sg + 1) * 512],
                          ps[:], scale=scale,
                          reads=[gps], writes=[dstg[r_]], inc=cA)

        def proj_v(head, slot):
            wb, gwb = wbf[slot % 3], gWB[slot % 3]
            for t2 in range(NST):
                ps, gps = (psA, gA) if t2 % 2 == 0 else (psB, gB)
                for c in range(NCD):
                    mm(ps[:, 0:256],
                       xnc(c, slice(t2 * 128, (t2 + 1) * 128)),
                       wb[:, c * 256:c * 256 + 256],
                       c == 0, c == NCD - 1,
                       reads=[gwb] + gXNq, writes=[gps],
                       inc=cT if c == NCD - 1 else None)
                acopy(vt[:, t2 * DH:(t2 + 1) * DH], ps[:, 0:256],
                      reads=[gps], writes=[gVT[t2]], inc=cA)

        def wload_job(head, i):
            base = head * 5
            hhA, hhB = head, head + 8
            if i == 0:
                emit_wload(wqT, slice(256 * hhA, 256 * hhA + 256), base + 0)
            elif i == 1:
                emit_wload(wqT, slice(256 * hhB, 256 * hhB + 256), base + 1)
            elif i == 2:
                emit_wload(wkT, slice(256 * hhA, 256 * hhA + 256), base + 2)
            elif i == 3:
                emit_wload(wkT, slice(256 * hhB, 256 * hhB + 256), base + 3)
            else:
                emit_wload(wvT, slice(256 * head, 256 * head + 256), base + 4)

        def emit_head(head, bg_fn=None):
            base = head * 5
            if not DEV_PREFETCH and head > 0:
                for i in range(3):
                    wload_job(head, i)
            proj_qk(qf, gQF, 0, base + 0)
            wload_job(head, 3)
            proj_qk(qf, gQF, 1, base + 1)
            wload_job(head, 4)
            proj_qk(kf, gKF, 0, base + 2)
            proj_qk(kf, gKF, 1, base + 3)
            proj_v(head, base + 4)
            nexth = head + 1 if head + 1 < min(H, dev_heads) else None
            if not DEV_PREFETCH:
                nexth = None
            if head == 0:
                if "qf" in taps:
                    dma("sync", tap_outs["qf"], qf[:], cL, reads=list(gQF))
                if "kf" in taps:
                    dma("sync", tap_outs["kf"], kf[:], cL, reads=list(gKF))
                if "vt" in taps:
                    dma("sync", tap_outs["vt"], vt[:], cL, reads=list(gVT))

            if DEV_STAGE == "proj":
                return
            for hh in range(2):
                for which, (src, gsrc) in enumerate([(qf, gQF), (kf, gKF)]):
                    for cc in range(8):
                        for ri in range(2):
                            r_ = 2 * hh + ri
                            act(sqb[:, ri * 256:(ri + 1) * 256],
                                src[:, r_ * S + cc * 256:
                                    r_ * S + (cc + 1) * 256],
                                AF.Square, reads=[gsrc[r_]], writes=[gSQB],
                                inc=cA)
                        for ri in range(2):
                            mm(psR[0:1, 256:512], ones_c[:],
                               sqb[:, ri * 256:(ri + 1) * 256],
                               ri == 0, ri == 1,
                               reads=[gSQB, gC], writes=[gR],
                               inc=cT if ri == 1 else None)
                        vop("red", sc11[0:1, which * 8 + cc:
                                        which * 8 + cc + 1],
                            psR[0:1, 256:512], ALU.max, reads=[gR],
                            writes=[gSC], inc=cV)
                vop("red", sc11b_f[0:1, 0:1], sc11[0:1, 0:8], ALU.max,
                    writes=[gSC], inc=cV)
                vop("red", sc11b_f[0:1, 1:2], sc11[0:1, 8:16], ALU.max,
                    writes=[gSC], inc=cV)
                vop("tt", sc11b_f[0:1, 2:3], sc11b_f[0:1, 0:1],
                    sc11b_f[0:1, 1:2], ALU.add, writes=[gSC], inc=cV)
                vop("ts", sc11b_f[0:1, 3:4], sc11b_f[0:1, 2:3], -0.5,
                    ALU.mult, writes=[gSC], inc=cV)
                vop("copy", sc11b[0:1, 4:5], sc11b_f[0:1, 3:4],
                    writes=[gSC], inc=cV)
                mm(psR[:, 19:20], ones_r[:], sc11b[0:1, 4:5], True, True,
                   reads=[gSC, gC], writes=[gR], inc=cT)
                vop("copy", biasB[hh][:], psR[:, 19:20], reads=[gR],
                    writes=[gBB[hh]], inc=cV)
            if head == 0 and "bias0" in taps:
                dma("sync", tap_outs["bias0"], biasB[0][:], cL, reads=[gBB[0]])
                dma("sync", tap_outs["bias1"], biasB[1][:], cL, reads=[gBB[1]])

            if DEV_STAGE == "bounds":
                return
            for qg in range(4):
                if nexth is not None and qg > 0:
                    wload_job(nexth, qg - 1)
                if bg_fn is not None and qg > 0:
                    bg_fn()
                for hh in range(2):
                    nkt = 4 * qg + 4

                    def emit_score(kt, qg=qg, hh=hh):
                        diag = kt >= 4 * qg
                        c0 = 128 * kt - 512 * qg if diag else 0
                        eb, geb = ebuf[kt % 3], gE[kt % 3]
                        ps, gps = [(psA, gA), (psB, gB), (psC, gCps)][kt % 3]
                        for ri in range(2):
                            r_ = 2 * hh + ri
                            mm(ps[:, c0:512],
                               kf[:, r_ * S + kt * 128:
                                  r_ * S + kt * 128 + 128],
                               qf[:, r_ * S + qg * 512 + c0:
                                  r_ * S + (qg + 1) * 512],
                               ri == 0, ri == 1,
                               reads=[gKF[r_], gQF[r_]], writes=[gps],
                               inc=cT if ri == 1 else None)
                        act(eb[:, c0:512], ps[:, c0:512], AF.Exp,
                            bias=biasB[hh][:],
                            reads=[gps, gBB[hh]], writes=[geb], inc=cA)
                        if diag:
                            vop("tt", eb[:, c0:c0 + 128], eb[:, c0:c0 + 128],
                                mask01[:], ALU.mult, eng="gpsimd",
                                reads=[gC], writes=[geb], inc=cP)

                    def emit_pv(kt, qg=qg, hh=hh):
                        diag = kt >= 4 * qg
                        c0 = 128 * kt - 512 * qg if diag else 0
                        eb, geb = ebuf[kt % 3], gE[kt % 3]
                        sub0 = c0 // 128
                        for sub in range(sub0, 4):
                            ecol = slice(sub * 128, sub * 128 + 128)
                            start_pv = kt == 0
                            stop_pv = kt == 4 * qg + sub
                            bank = 2 * hh + (sub // 2)
                            reg = psPV[bank][:, (sub % 2) * 256:
                                             (sub % 2) * 256 + 256]
                            mm(reg, eb[:, ecol], vt[:, kt * DH:(kt + 1) * DH],
                               start_pv and sub % 2 == 0, stop_pv,
                               reads=[geb, gVT[kt]], writes=[gPV[bank]],
                               inc=cT if stop_pv else None)
                            rcol_i = (qg % 2) * 8 + hh * 4 + sub
                            mm(psR[:, rcol_i:rcol_i + 1], eb[:, ecol],
                               ones_c[:],
                               start_pv and sub == 0, stop_pv,
                               reads=[geb, gC, gVT[kt]], writes=[gR],
                               inc=cT if sub == 3 else None)

                    for kt in range(nkt):
                        emit_score(kt)
                        if kt >= 1:
                            emit_pv(kt - 1)
                    emit_pv(nkt - 1)
                    if bg_fn is not None and hh == 0 and qg >= 2:
                        bg_fn()
                    for sub in range(4):
                        bank = 2 * hh + (sub // 2)
                        reg = psPV[bank][:, (sub % 2) * 256:
                                         (sub % 2) * 256 + 256]
                        acopy(pvsb[:, (hh * 4 + sub) * 257:
                                   (hh * 4 + sub) * 257 + 256], reg,
                              reads=[gPV[bank]], writes=[gPVS], inc=cA)
                    acopy(rsb[:, hh * 4:hh * 4 + 4],
                          psR[:, (qg % 2) * 8 + hh * 4:
                              (qg % 2) * 8 + hh * 4 + 4],
                          reads=[gR], writes=[gRS], inc=cA)
                if DEV_STAGE == "loop":
                    continue
                flush_transposes()
                onqb, gONb = onq[qg % 2], gON[qg % 2]
                for sub in range(4):
                    p1 = pvsb[:, sub * 257:sub * 257 + 256]
                    p2 = pvsb[:, (4 + sub) * 257:(4 + sub) * 257 + 256]
                    vop("recip", rmsc[:, 8:9], rsb[:, sub:sub + 1],
                        reads=[gRS], writes=[gRMS], inc=cV)
                    vop("recip", rmsc[:, 9:10], rsb[:, 4 + sub:5 + sub],
                        reads=[gRS], writes=[gRMS], inc=cV)
                    vop("tt", rmsc[:, 10:11], rmsc[:, 9:10], lam_sb[:],
                        ALU.mult, reads=[gLAM], writes=[gRMS], inc=cV)
                    vop("ts", odiff[:], p1, rmsc[:, 8:9], ALU.mult,
                        reads=[gPVS, gRMS], writes=[gOD], inc=cV)
                    vop("ts", otmp[:], p2, rmsc[:, 10:11], ALU.mult,
                        reads=[gPVS, gRMS], writes=[gOT], inc=cV)
                    vop("tt", odiff[:], odiff[:], otmp[:], ALU.subtract,
                        reads=[gOT], writes=[gOD], inc=cV)
                    act(osq[:], odiff[:], AF.Square,
                        reads=[gOD], writes=[gOSQ], inc=cA)
                    vop("red", rmsc[:, 0:8],
                        osq[:].rearrange("p (a b) -> p a b", a=8), ALU.add,
                        reads=[gOSQ], writes=[gRMS], inc=cV)
                    act(rmsc[:, 0:8], rmsc[:, 0:8], AF.Ln, bias=epsrms[:],
                        scale=1.0 / 32, reads=[gRMS, gC], writes=[gRMS],
                        inc=cA)
                    act(rmsc[:, 0:8], rmsc[:, 0:8], AF.Exp, scale=-0.5,
                        reads=[gRMS], writes=[gRMS], inc=cA)
                    for g8 in range(8):
                        vop("ts", onqb[:, sub * DH + g8 * 32:
                                       sub * DH + g8 * 32 + 32],
                            odiff[:, g8 * 32:(g8 + 1) * 32],
                            rmsc[:, g8:g8 + 1], ALU.mult,
                            reads=[gOD, gRMS], writes=[gONb], inc=cV)
                pend_transp.append((onqb, head, qg, gONb))

        for i in range(3):
            wload_job(0, i)
        em.do("sync", lambda e: e.nop(), waits=tok_ln)
        npA = (len(qA_jobs) + 11) // 12 if quant_on else 0
        bgA = (lambda: emit_passA(npA)) if quant_on else None
        for head in range(min(4, dev_heads)):
            emit_head(head, bgA)
            if quant_on:
                emit_passA(npA)
        if dev_heads > 4:
            emit_head(4, bgA)
        if quant_on:
            emit_passA(len(qA_jobs))
            emit_gamma(0)
            emit_gamma(1)
            emit_gamma(2)
            if "thr" in taps:
                dma("sync", tap_outs["thr"], thr_sb[:], cL, reads=[gTHR[2]])
        bgB = (lambda: emit_passB(7)) if quant_on else None
        for head in range(5, min(H, dev_heads)):
            emit_head(head, bgB)
            if quant_on:
                emit_passB(7)
        flush_transposes()
        if quant_on:
            emit_passB(len(qB_jobs))
    tok_attn = [(cV, cV.n), (cA, cA.n), (cT, cT.n), (cP, cP.n)] + all_dma_toks()
    at_stack.close()

    wo_stack = ExitStack()
    if phases >= 4:
        sbw = lambda name, shape, dt: wo_stack.enter_context(
            nc.sbuf_tensor(name, shape, dt))
        wostage = sbw("wostage", [128, NCD * 256], F32)
        gWOS = Guard()
        wobf = sbw("wobf", [128, NCD * 512], BF16)
        gWOB = Guard()
        oslice = [sbw(f"oslice{i}", [128, NCD * 512], BF16) for i in range(2)]
        gOSL = [Guard(), Guard()]
        sstage = sbw("sstage", [128, 512], BF16)
        gSST = Guard()

        for ds in range(4):
            dcols = slice(ds * 512, (ds + 1) * 512)
            for half in range(2):
                hcols = slice(ds * 512 + half * 256, ds * 512 + half * 256 + 256)
                dma_chunked("sync",
                            wostage[:].rearrange("p (c n) -> p c n", c=NCD),
                            woT[:, hcols].rearrange("(c p) n -> p c n", p=128),
                            cL, 2, NCD, writes=[gWOS],
                            waits=tok_attn if ds == 0 and half == 0 else ())
                vop("ts",
                    wobf[:].rearrange("p (c n) -> p c n", c=NCD)[
                        :, :, half * 256:(half + 1) * 256],
                    wostage[:].rearrange("p (c n) -> p c n", c=NCD),
                    OML, ALU.mult,
                    reads=[gWOS], writes=[gWOB], inc=cV)
            for qg in range(4):
                osl, gosl = oslice[qg % 2], gOSL[qg % 2]
                for c in range(NCD):
                    row0 = (c * 4 + qg) * 128
                    dma("sync", osl[:, c * 512:(c + 1) * 512],
                        o_scr[row0:row0 + 128, :], cL, writes=[gosl],
                        waits=[o_wr_toks[(c, qg)]] if phases >= 3 else ())
                for sub in range(4):
                    ps, gps = (psA, gA) if sub % 2 == 0 else (psB, gB)
                    for c in range(NCD):
                        mm(ps[:], osl[:, c * 512 + sub * 128:
                                      c * 512 + sub * 128 + 128],
                           wobf[:, c * 512:(c + 1) * 512],
                           c == 0, c == NCD - 1,
                           reads=[gosl, gWOB], writes=[gps],
                           inc=cT if c == NCD - 1 else None)
                    acopy(sstage[:], ps[:], reads=[gps], writes=[gSST],
                          inc=cA)
                    t_ = qg * 4 + sub
                    dma("scalar", sattn[t_ * 128:(t_ + 1) * 128, dcols],
                        sstage[:], cS, reads=[gSST])
    tok_wo = [(cV, cV.n), (cA, cA.n), (cT, cT.n), (cP, cP.n)] + all_dma_toks()
    tok_sattn = [(c, c.n) for c in cSs if c.n > 0]
    wo_stack.close()
    q_stack.close()
    wq_stack.close()

    ffn_stack = ExitStack()
    if phases >= 5:
        sbf = lambda name, shape, dt: ffn_stack.enter_context(
            nc.sbuf_tensor(name, shape, dt))
        hid = sbf("hid", [128, NH3 * 512], BF16)
        gHID = [Guard() for _ in range(NH3)]
        q12 = [sbf(f"q12_{i}", [128, NCD * 128], BF16) for i in range(4)]
        gQ12 = [Guard() for _ in range(4)]
        q3sl = [sbf(f"q3sl{i}", [128, NH3 * 256], BF16) for i in range(2)]
        gQ3 = [Guard(), Guard()]
        sigb = sbf("sigb", [128, 512], F32)
        gSIG = Guard()
        xsl = sbf("xsl", [128, 512], F32)
        gXSL = [Guard(), Guard()]
        ssl = sbf("ssl", [128, 512], BF16)
        gSSL = [Guard(), Guard()]
        sfup = sbf("sfup", [128, 256], F32)
        gSF = Guard()
        ost = sbf("ost", [128, 512], F32)
        gOST = [Guard(), Guard()]

        tok_q = tok_wo
        q3v = q3s.rearrange("(j p) n -> p j n", p=128)
        for blk in range(4):
            bcols = slice(blk * 512, (blk + 1) * 512)
            for j in range(NH3):
                b1, g1 = q12[(2 * j) % 4], gQ12[(2 * j) % 4]
                b2, g2 = q12[(2 * j + 1) % 4], gQ12[(2 * j + 1) % 4]
                dma("sync", b1[:], q1s[j * 128:(j + 1) * 128, :], cL,
                    writes=[g1], waits=tok_q if blk == 0 and j < 2 else ())
                dma("sync", b2[:], q2s[j * 128:(j + 1) * 128, :], cL,
                    writes=[g2], waits=tok_q if blk == 0 and j < 2 else ())
                for c in range(NCD):
                    mm(psA[:], b1[:, c * 128:(c + 1) * 128], xnc(c, bcols),
                       c == 0, c == NCD - 1, reads=[g1, gXNq[blk]],
                       writes=[gA], inc=cT if c == NCD - 1 else None)
                for c in range(NCD):
                    mm(psB[:], b2[:, c * 128:(c + 1) * 128], xnc(c, bcols),
                       c == 0, c == NCD - 1, reads=[g2, gXNq[blk]],
                       writes=[gB], inc=cT if c == NCD - 1 else None)
                act(sigb[:], psA[:], AF.Sigmoid,
                    reads=[gA], writes=[gSIG], inc=cA)
                vop("tt", hid[:, j * 512:(j + 1) * 512], sigb[:], psB[:],
                    ALU.mult, reads=[gSIG, gB], writes=[gHID[j]], inc=cV)
            if blk == 0 and "hid" in taps:
                dma("sync", tap_outs["hid"], hid[:], cL,
                    reads=[gHID[jj] for jj in range(NH3)])
            for dch in range(8):
                q3b, gq3 = q3sl[dch % 2], gQ3[dch % 2]
                for piece in range(4):
                    dma("sync",
                        q3b[:].rearrange("p (j n) -> p j n", j=NH3)[
                            :, piece * 11:(piece + 1) * 11, :],
                        q3v[:, piece * 11:(piece + 1) * 11,
                            dch * 256:(dch + 1) * 256],
                        cL, writes=[gq3],
                        waits=tok_q if blk == 0 and dch < 2 else ())
                dc = slice(dch * 256, (dch + 1) * 256)
                for t_ in range(4):
                    trow = blk * 4 + t_
                    nb = (dch * 4 + t_) % 2
                    xsl_v = xsl[:, nb * 256:(nb + 1) * 256]
                    ssl_v = ssl[:, nb * 256:(nb + 1) * 256]
                    ost_v = ost[:, nb * 256:(nb + 1) * 256]
                    dma("sync", xsl_v,
                        x_in[trow * 128:(trow + 1) * 128, dc], cL,
                        writes=[gXSL[nb]])
                    dma("sync", ssl_v,
                        sattn[trow * 128:(trow + 1) * 128, dc], cL,
                        writes=[gSSL[nb]],
                        waits=tok_sattn if phases >= 4 else ())
                    idx = (dch * 4 + t_) % 8
                    bank = idx % 4
                    reg = psPV[bank][:, (idx // 4) * 256:
                                     (idx // 4) * 256 + 256]
                    tok = None
                    for j in range(NH3):
                        tok = mm(reg,
                                 hid[:, j * 512 + t_ * 128:
                                     j * 512 + t_ * 128 + 128],
                                 q3b[:, j * 256:(j + 1) * 256],
                                 j == 0, j == NH3 - 1,
                                 reads=[gHID[j], gq3], writes=[gPV[bank]],
                                 inc=cT if j == NH3 - 1 else None)
                    for j in range(NH3):
                        gHID[j].r.append(tok)
                    gq3.r.append(tok)
                    act(sigb[:, 0:256], reg, AF.Silu,
                        reads=[gPV[bank]], writes=[gSIG], inc=cA)
                    act(sfup[:], ssl_v, AF.Identity,
                        reads=[gSSL[nb]], writes=[gSF], inc=cA)
                    vop("tt", sigb[:, 256:512], sigb[:, 0:256], sfup[:],
                        ALU.mult, reads=[gSIG, gSF], writes=[gSIG], inc=cV)
                    vop("tt", ost_v, sigb[:, 256:512], xsl_v,
                        ALU.add, reads=[gSIG, gXSL[nb]], writes=[gOST[nb]],
                        inc=cV)
                    dma("gpsimd", out_d[trow * 128:(trow + 1) * 128, dc],
                        ost_v, cS, reads=[gOST[nb]])

    em.do("sync", lambda e: e.nop(), waits=all_dma_toks())

    with nc.Block() as block:
        for eng in ENGINES:
            stream = em.streams[eng]

            def runner(engine, _s=stream):
                for f in _s:
                    f(engine)
            getattr(block, eng)(runner)

    ffn_stack.close()
    st.close()
    return nc, em.counts


def kernel(L0_attn_x, L0_attn_l_q1, L0_attn_l_k1, L0_attn_l_q2, L0_attn_l_k2,
           L0_attn_wq, L0_attn_wk, L0_attn_wv, L0_attn_wo,
           L0_ffn_w1, L0_ffn_w2, L0_ffn_w3):
    x = np.ascontiguousarray(np.asarray(L0_attn_x, dtype=np.float32))
    f32 = lambda a: np.asarray(a, dtype=np.float32)
    tr = lambda a: np.ascontiguousarray(np.asarray(a, dtype=np.float32).T)
    shared = {
        "wqT": tr(L0_attn_wq), "wkT": tr(L0_attn_wk), "wvT": tr(L0_attn_wv),
        "woT": tr(L0_attn_wo), "w1T": tr(L0_ffn_w1), "w2T": tr(L0_ffn_w2),
        "w3T": tr(L0_ffn_w3),
        "lvec": np.ascontiguousarray(np.concatenate(
            [f32(L0_attn_l_q1), f32(L0_attn_l_k1),
             f32(L0_attn_l_q2), f32(L0_attn_l_k2)]).reshape(1, 32)),
        "mask01": np.triu(np.ones((128, 128))).astype(ml_dtypes.bfloat16),
        "idenb": np.eye(128).astype(ml_dtypes.bfloat16),
        "idenf": np.eye(128, dtype=np.float32),
    }
    in_maps = []
    for c in range(8):
        m = dict(shared)
        m["x"] = x[c]
        m["xT"] = np.ascontiguousarray(x[c].T)
        in_maps.append(m)
    nc, _ = build_nc(taps=TAPS, phases=PHASES)
    res = run_bass_kernel_spmd(nc, in_maps, core_ids=list(range(8)))
    global LAST_RESULTS
    LAST_RESULTS = res.results
    return np.stack([np.asarray(res.results[c]["out"]) for c in range(8)], 0)


LAST_RESULTS = None


# version 21
# speedup vs baseline: 1.0272x; 1.0024x over previous
import math
from contextlib import ExitStack

import numpy as np
import ml_dtypes
import concourse.bass as bass
import concourse.mybir as mybir
from concourse.bass_utils import run_bass_kernel_spmd

F32 = mybir.dt.float32
BF16 = mybir.dt.bfloat16
AF = mybir.ActivationFunctionType
ALU = mybir.AluOpType
AX = mybir.AxisListType

S = 2048
D = 2048
H = 8
DH = 256
HID = 5632
NCD = D // 128
NST = S // 128
NH3 = HID // 128
L_INIT = 0.8 - 0.6 * math.exp(-0.3)
OML = 1.0 - L_INIT
EPS_LN = 1e-5
EPS_RMS = 1e-6
EPS_Q = 1e-8

TAPS = ()
DEV_HEADS = 8
DEV_QUANT = True
DEV_STAGE = 'full'
DEV_PREFETCH = True
PHASES = 99

ENGINES = ("sync", "tensor", "vector", "scalar", "gpsimd")


class Ch:
    def __init__(self, nc, ctx, name):
        self.sem = ctx.enter_context(nc.semaphore(name))
        self.n = 0
        self.name = name


class Guard:
    __slots__ = ("w", "r")

    def __init__(self):
        self.w = None
        self.r = []


class Emit:
    OWN = {"tensor": "cT"}

    def __init__(self):
        self.streams = {e: [] for e in ENGINES}
        self.waited = {e: {} for e in ENGINES}
        self.counts = {e: 0 for e in ENGINES}

    def _wait(self, eng, ch, tok):
        if tok is None or tok <= 0:
            return
        if self.OWN.get(eng) == ch.name:
            return
        w = self.waited[eng]
        if w.get(ch.name, 0) >= tok:
            return
        w[ch.name] = tok
        self.streams[eng].append(lambda e, s=ch.sem, t=tok: e.wait_ge(s, t))

    def do(self, eng, fn, reads=(), writes=(), waits=(), inc=None, inc_n=1):
        need = []
        for g in reads:
            if g.w is not None:
                need.append(g.w)
        for g in writes:
            if g.w is not None:
                need.append(g.w)
            need.extend(g.r)
        need.extend(waits)
        for ch, tok in need:
            self._wait(eng, ch, tok)
        tok = None
        if inc is not None:
            inc.n += inc_n
            tok = (inc, inc.n)
            self.streams[eng].append(
                lambda e, f=fn, s=inc.sem, v=inc_n: f(e).then_inc(s, v))
            for g in writes:
                g.w = tok
                g.r = []
            for g in reads:
                g.r.append(tok)
        else:
            self.streams[eng].append(lambda e, f=fn: f(e))
        self.counts[eng] += 1
        return tok


def build_nc(taps=(), phases=99, dev_heads=None, dev_quant=None):
    dev_heads = DEV_HEADS if dev_heads is None else dev_heads
    dev_quant = DEV_QUANT if dev_quant is None else dev_quant
    nc = bass.Bass("TRN2", target_bir_lowering=False)
    em = Emit()
    st = ExitStack()

    ap = lambda t: t.ap()
    x_in = ap(nc.dram_tensor("x", [S, D], F32, kind="ExternalInput"))
    xT_in = ap(nc.dram_tensor("xT", [D, S], F32, kind="ExternalInput"))
    wqT = ap(nc.dram_tensor("wqT", [D, 2 * D], F32, kind="ExternalInput"))
    wkT = ap(nc.dram_tensor("wkT", [D, 2 * D], F32, kind="ExternalInput"))
    wvT = ap(nc.dram_tensor("wvT", [D, D], F32, kind="ExternalInput"))
    woT = ap(nc.dram_tensor("woT", [D, D], F32, kind="ExternalInput"))
    w1T = ap(nc.dram_tensor("w1T", [D, HID], F32, kind="ExternalInput"))
    w2T = ap(nc.dram_tensor("w2T", [D, HID], F32, kind="ExternalInput"))
    w3T = ap(nc.dram_tensor("w3T", [HID, D], F32, kind="ExternalInput"))
    lvec = ap(nc.dram_tensor("lvec", [1, 32], F32, kind="ExternalInput"))
    mask01_in = ap(nc.dram_tensor("mask01", [128, 128], BF16,
                                  kind="ExternalInput"))
    idenb_in = ap(nc.dram_tensor("idenb", [128, 128], BF16,
                                 kind="ExternalInput"))
    idenf_in = ap(nc.dram_tensor("idenf", [128, 128], F32,
                                 kind="ExternalInput"))
    out_d = ap(nc.dram_tensor("out", [S, D], F32, kind="ExternalOutput"))

    def dscr(name, shape, dt):
        kind = "ExternalOutput" if name in taps else "Internal"
        return ap(nc.dram_tensor(name, shape, dt, kind=kind))

    stat_scr = dscr("stat_scr", [2, S], F32)
    q1s = dscr("q1s", [NH3 * 128, NCD * 128], BF16)
    q2s = dscr("q2s", [NH3 * 128, NCD * 128], BF16)
    q3s = dscr("q3s", [HID, D], BF16)
    o_scr = dscr("o_scr", [NCD * 4 * 128, 512], BF16)
    sattn = dscr("sattn", [S, D], BF16)

    tap_outs = {}
    for tname, shape, dt in [
        ("lam", [128, 1], F32), ("xn", [128, NCD * S], BF16),
        ("qf", [128, 4 * S], BF16), ("kf", [128, 4 * S], BF16),
        ("vt", [128, NST * DH], BF16), ("thr", [128, 3], F32),
        ("bias0", [128, 1], F32), ("bias1", [128, 1], F32),
        ("sc11", [1, 16], F32), ("hid", [128, NH3 * 512], BF16),
    ]:
        if tname in taps:
            tap_outs[tname] = ap(
                nc.dram_tensor("tap_" + tname, shape, dt, kind="ExternalOutput"))

    cLs = [Ch(nc, st, f"cL{i}") for i in range(14)]
    cSs = [Ch(nc, st, f"cS{i}") for i in range(6)]
    cGs = [Ch(nc, st, f"cG{i}") for i in range(6)]
    _lane = {"sync": [0], "scalar": [0], "gpsimd": [0]}
    _pools = {"sync": cLs, "scalar": cSs, "gpsimd": cGs}

    def dma_lane(queue):
        pool = _pools[queue]
        i = _lane[queue][0]
        _lane[queue][0] = (i + 1) % len(pool)
        return pool[i]

    def all_dma_toks():
        return [(c, c.n) for c in cLs + cSs + cGs if c.n > 0]
    cT = Ch(nc, st, "cT")
    cV = Ch(nc, st, "cV")
    cA = Ch(nc, st, "cA")
    cP = Ch(nc, st, "cP")
    cL = None
    cS = None

    psA = st.enter_context(nc.psum_tensor("psA", [128, 512], F32))
    psB = st.enter_context(nc.psum_tensor("psB", [128, 512], F32))
    psPV = [st.enter_context(nc.psum_tensor(f"psPV{i}", [128, 512], F32))
            for i in range(4)]
    psR = st.enter_context(nc.psum_tensor("psR", [128, 512], F32))
    psC = st.enter_context(nc.psum_tensor("psC", [128, 512], F32))
    gA, gB, gR = Guard(), Guard(), Guard()
    gCps = Guard()
    gPV = [Guard() for _ in range(4)]

    sbS = lambda name, shape, dt: st.enter_context(
        nc.sbuf_tensor(name, shape, dt))
    xn = sbS("xn", [128, NCD * S], BF16)
    gXNq = [Guard() for _ in range(4)]
    mask01 = sbS("mask01_s", [128, 128], BF16)
    idenb = sbS("idenb_s", [128, 128], BF16)
    idenf = sbS("idenf_s", [128, 128], F32)
    ones_r = sbS("ones_r", [1, 128], BF16)
    ones_c = sbS("ones_c", [128, 1], BF16)
    lam_sb = sbS("lam_sb", [128, 1], F32)
    gLAM = Guard()
    epsln = sbS("epsln", [128, 1], F32)
    epsrms = sbS("epsrms", [128, 1], F32)
    thr_sb = sbS("thr_sb", [128, 3], F32)
    thrn_sb = sbS("thrn_sb", [128, 3], F32)
    gTHR = [Guard() for _ in range(3)]
    lrow = sbS("lrow", [1, 32], F32)
    sc11 = sbS("sc11", [1, 16], F32)
    sc11b = sbS("sc11b", [1, 16], BF16)
    sc11b_f = sbS("sc11b_f", [1, 16], F32)
    biasB = [sbS(f"biasB{i}", [128, 1], F32) for i in range(2)]
    wq_stack = ExitStack()
    sbW = lambda name, shape, dt: wq_stack.enter_context(
        nc.sbuf_tensor(name, shape, dt))
    wstage = sbW("wstage", [128, NCD * 128], F32)
    gWS = Guard()
    wbf = [sbW(f"wbf{i}", [128, NCD * 256], BF16) for i in range(3)]
    gWB = [Guard(), Guard(), Guard()]
    gBB = [Guard(), Guard()]
    gC = Guard()
    gSC = Guard()

    def xnc(c, cols):
        return xn[:, c * S + cols.start: c * S + cols.stop]

    def dma(queue, out, in_, ch, reads=(), writes=(), waits=()):
        lane = dma_lane(queue)
        waits = list(waits) + [(lane, lane.n)]
        return em.do(queue, lambda e, o=out, i=in_: e.dma_start(out=o, in_=i),
                     reads=reads, writes=writes, waits=waits, inc=lane,
                     inc_n=16)

    def mm(out, lhsT, rhs, start, stop, reads=(), writes=(), waits=(),
           inc=None):
        return em.do("tensor",
                     lambda e, o=out, l=lhsT, r=rhs, s1=start, s2=stop:
                     e.matmul(o, l, r, start=s1, stop=s2,
                              skip_group_check=True),
                     reads=reads, writes=writes, waits=waits, inc=inc)

    def transp(out, in_, iden, reads=(), writes=(), inc=None):
        return em.do("tensor",
                     lambda e, o=out, i=in_, d=iden: e.transpose(o, i, d),
                     reads=reads, writes=writes, inc=inc)

    def act(out, in_, func, bias=None, scale=None, accum=None,
            reads=(), writes=(), waits=(), inc=None):
        def f(e, o=out, i=in_, fu=func, b=bias, s=scale, a=accum):
            kw = {}
            if b is not None:
                kw["bias"] = b
            if s is not None:
                kw["scale"] = s
            if a is not None:
                kw["accum_out"] = a
            return e.activation(o, i, fu, **kw)
        return em.do("scalar", f, reads=reads, writes=writes, waits=waits,
                     inc=inc)

    def acopy(out, in_, scale=None, reads=(), writes=(), waits=(), inc=None):
        if scale is None:
            f = lambda e, o=out, i=in_: e.copy(o, i)
        else:
            f = lambda e, o=out, i=in_, s=scale: e.mul(o, i, s)
        return em.do("scalar", f, reads=reads, writes=writes, waits=waits,
                     inc=inc)

    def vop(kind, *args, eng="vector", reads=(), writes=(), waits=(),
            inc=None):
        if kind == "tt":
            out, a, b, op = args
            f = lambda e, o=out, x=a, y=b, p=op: e.tensor_tensor(o, x, y, p)
        elif kind == "ts":
            out, a, s1, op = args
            f = lambda e, o=out, x=a, s=s1, p=op: e.tensor_scalar(
                o, x, s, None, p)
        elif kind == "red":
            out, a, op = args
            f = lambda e, o=out, x=a, p=op: e.tensor_reduce(o, x, AX.X, p)
        elif kind == "redabs":
            out, a, op = args
            f = lambda e, o=out, x=a, p=op: e.tensor_reduce(
                o, x, AX.X, p, apply_absolute_value=True)
        elif kind == "copy":
            out, a = args
            f = lambda e, o=out, x=a: e.tensor_copy(o, x)
        elif kind == "recip":
            out, a = args
            f = lambda e, o=out, x=a: e.reciprocal(o, x)
        elif kind == "memset":
            out, val = args
            f = lambda e, o=out, v=val: e.memset(o, v)
        else:
            raise ValueError(kind)
        return em.do(eng, f, reads=reads, writes=writes, waits=waits, inc=inc)

    def dma_chunked(queue, outv, inv, ch, nsplit, csize, reads=(), writes=(),
                    waits=()):
        toks = []
        per = csize // nsplit
        for i in range(nsplit):
            toks.append(dma(queue, outv[:, i * per:(i + 1) * per, :],
                            inv[:, i * per:(i + 1) * per, :], ch,
                            reads=reads, writes=writes, waits=waits))
        return toks[-1]

    dma("sync", mask01[:], mask01_in[:, :], cL, writes=[gC])
    dma("sync", idenb[:], idenb_in[:, :], cL, writes=[gC])
    dma("sync", idenf[:], idenf_in[:, :], cL, writes=[gC])
    dma("sync", lrow[:], lvec[:, :], cL, writes=[gC])
    vop("memset", ones_r[:], 1.0, writes=[gC], inc=cV)
    vop("memset", ones_c[:], 1.0, writes=[gC], inc=cV)
    vop("memset", epsln[:], EPS_LN, writes=[gC], inc=cV)
    vop("memset", epsrms[:], EPS_RMS, writes=[gC], inc=cV)

    vop("tt", sc11[0:1, 0:8], lrow[0:1, 0:8], lrow[0:1, 8:16], ALU.mult,
        reads=[gC], writes=[gSC], inc=cV)
    vop("red", sc11[0:1, 8:9], sc11[0:1, 0:8], ALU.add, writes=[gSC], inc=cV)
    vop("tt", sc11[0:1, 0:8], lrow[0:1, 16:24], lrow[0:1, 24:32], ALU.mult,
        writes=[gSC], inc=cV)
    vop("red", sc11[0:1, 9:10], sc11[0:1, 0:8], ALU.add, writes=[gSC], inc=cV)
    act(sc11[0:1, 10:12], sc11[0:1, 8:10], AF.Exp,
        reads=[gSC], writes=[gSC], inc=cA)
    vop("tt", sc11[0:1, 12:13], sc11[0:1, 10:11], sc11[0:1, 11:12],
        ALU.subtract, writes=[gSC], inc=cV)
    vop("ts", sc11[0:1, 13:14], sc11[0:1, 12:13], L_INIT, ALU.add,
        writes=[gSC], inc=cV)
    vop("copy", sc11b[0:1, 0:1], sc11[0:1, 13:14], writes=[gSC], inc=cV)
    vop("tt", sc11[0:1, 14:15], sc11[0:1, 13:14], sc11b[0:1, 0:1],
        ALU.subtract, writes=[gSC], inc=cV)
    vop("copy", sc11b[0:1, 1:2], sc11[0:1, 14:15], writes=[gSC], inc=cV)
    mm(psR[:, 16:17], ones_r[:], sc11b[0:1, 0:1], True, False,
       reads=[gSC, gC], writes=[gR], inc=cT)
    mm(psR[:, 16:17], ones_r[:], sc11b[0:1, 1:2], False, True,
       reads=[gSC], writes=[gR], inc=cT)
    vop("copy", lam_sb[:], psR[:, 16:17], reads=[gR], writes=[gLAM], inc=cV)
    if "sc11" in taps:
        dma("sync", tap_outs["sc11"], sc11[:], cL, reads=[gSC])
    if "lam" in taps:
        dma("sync", tap_outs["lam"], lam_sb[:], cL, reads=[gLAM])

    ln_stack = ExitStack()
    if phases >= 1:
        sbl = lambda name, shape, dt: ln_stack.enter_context(
            nc.sbuf_tensor(name, shape, dt))
        xstat = sbl("xstat", [128, 2 * D], F32)
        gXS = [Guard(), Guard()]
        sqt = sbl("sqt", [128, D], F32)
        gSQ = Guard()
        rcol = sbl("rcol", [128, 16], F32)
        mcol = sbl("mcol", [128, 16], F32)
        gCOL = Guard()
        statT = sbl("statT", [16, 256], F32)
        gSTT = Guard()
        rowf = sbl("rowf", [1, 512], F32)
        gROWF = Guard()
        rhi = sbl("rhi", [1, S], BF16)
        rlo = sbl("rlo", [1, S], BF16)
        mhi = sbl("mhi", [1, S], BF16)
        mlo = sbl("mlo", [1, S], BF16)
        gROW = Guard()
        stat1 = sbl("stat1", [128, 8], F32)
        gS1 = Guard()
        xTg = sbl("xTg", [128, NCD * 256], F32)
        gXT = Guard()

        for t in range(NST):
            xv = xstat[:, (t % 2) * D:(t % 2) * D + D]
            gxv = gXS[t % 2]
            dma("sync", xv, x_in[t * 128:(t + 1) * 128, :], cL,
                writes=[gxv])
            vop("red", stat1[:, 0:1], xv, ALU.add,
                reads=[gxv], writes=[gS1], inc=cV)
            act(sqt[:], xv, AF.Square, reads=[gxv], writes=[gSQ], inc=cA)
            vop("red", stat1[:, 1:2], sqt[:], ALU.add,
                reads=[gSQ], writes=[gS1], inc=cV)
            vop("ts", stat1[:, 2:3], stat1[:, 0:1], 1.0 / D, ALU.mult,
                writes=[gS1], inc=cV)
            vop("ts", stat1[:, 3:4], stat1[:, 1:2], 1.0 / D, ALU.mult,
                writes=[gS1], inc=cV)
            vop("tt", stat1[:, 4:5], stat1[:, 2:3], stat1[:, 2:3], ALU.mult,
                writes=[gS1], inc=cV)
            vop("tt", stat1[:, 5:6], stat1[:, 3:4], stat1[:, 4:5],
                ALU.subtract, writes=[gS1], inc=cV)
            act(stat1[:, 6:7], stat1[:, 5:6], AF.Ln, bias=epsln[:],
                reads=[gS1, gC], writes=[gS1], inc=cA)
            act(rcol[:, t:t + 1], stat1[:, 6:7], AF.Exp, scale=-0.5,
                reads=[gS1], writes=[gCOL], inc=cA)
            vop("tt", mcol[:, t:t + 1], stat1[:, 2:3], rcol[:, t:t + 1],
                ALU.mult, reads=[gCOL, gS1], writes=[gCOL], inc=cV)
        transp(psR[0:16, 0:128], rcol[:], idenf[:],
               reads=[gCOL, gC, gR], writes=[gR], inc=cT)
        vop("copy", statT[:, 0:128], psR[0:16, 0:128], reads=[gR],
            writes=[gSTT], inc=cV)
        transp(psR[0:16, 128:256], mcol[:], idenf[:], reads=[gCOL],
               writes=[gR], inc=cT)
        vop("copy", statT[:, 128:256], psR[0:16, 128:256], reads=[gR],
            writes=[gSTT], inc=cV)
        dma("gpsimd", stat_scr[0:1, :].rearrange("a (b c) -> (a b) c", b=16),
            statT[:, 0:128], cS, reads=[gSTT])
        dma("gpsimd", stat_scr[1:2, :].rearrange("a (b c) -> (a b) c", b=16),
            statT[:, 128:256], cS, reads=[gSTT])
        tok_stat_l = [(c, c.n) for c in cSs + cGs if c.n > 0]
        for qg in range(4):
            cols = slice(qg * 512, qg * 512 + 512)
            dma("sync", rowf[:], stat_scr[0:1, cols], cL, writes=[gROWF],
                waits=tok_stat_l)
            vop("copy", rhi[0:1, cols], rowf[:], reads=[gROWF], writes=[gROW],
                inc=cV)
            vop("tt", rowf[:], rowf[:], rhi[0:1, cols], ALU.subtract,
                reads=[gROW], writes=[gROWF], inc=cV)
            vop("copy", rlo[0:1, cols], rowf[:], reads=[gROWF], writes=[gROW],
                inc=cV)
            dma("sync", rowf[:], stat_scr[1:2, cols], cL, writes=[gROWF])
            vop("copy", mhi[0:1, cols], rowf[:], reads=[gROWF], writes=[gROW],
                inc=cV)
            vop("tt", rowf[:], rowf[:], mhi[0:1, cols], ALU.subtract,
                reads=[gROW], writes=[gROWF], inc=cV)
            vop("copy", mlo[0:1, cols], rowf[:], reads=[gROWF], writes=[gROW],
                inc=cV)
        for qg in range(4):
            cols = slice(qg * 512, (qg + 1) * 512)
            pr = psPV[2 * (qg % 2)]
            pm = psPV[2 * (qg % 2) + 1]
            gr = gPV[2 * (qg % 2)]
            gm = gPV[2 * (qg % 2) + 1]
            mm(pr[:], ones_r[:], rhi[0:1, cols], True, False,
               reads=[gROW, gC], writes=[gr], inc=cT)
            mm(pr[:], ones_r[:], rlo[0:1, cols], False, True,
               reads=[gROW], writes=[gr], inc=cT)
            mm(pm[:], ones_r[:], mhi[0:1, cols], True, False,
               reads=[gROW], writes=[gm], inc=cT)
            mm(pm[:], ones_r[:], mlo[0:1, cols], False, True,
               reads=[gROW], writes=[gm], inc=cT)
            for half in range(2):
                hc = slice(qg * 512 + half * 256, qg * 512 + half * 256 + 256)
                dma_chunked("sync",
                            xTg[:].rearrange("p (c n) -> p c n", c=NCD),
                            xT_in[:, hc].rearrange("(c p) n -> p c n", p=128),
                            cL, 4, NCD, writes=[gXT])
                for c in range(NCD):
                    xg = xTg[:, c * 256:(c + 1) * 256]
                    prh = pr[:, half * 256:(half + 1) * 256]
                    pmh = pm[:, half * 256:(half + 1) * 256]
                    vop("tt", sqt[:, 0:256], xg, prh, ALU.mult,
                        reads=[gXT, gr], writes=[gSQ], inc=cV)
                    vop("tt", xnc(c, hc), sqt[:, 0:256], pmh, ALU.subtract,
                        reads=[gSQ, gm], writes=[gXNq[qg]], inc=cV)
        if "xn" in taps:
            dma("sync", tap_outs["xn"], xn[:], cL, reads=list(gXNq))
    tok_ln = [(cV, cV.n), (cA, cA.n), (cT, cT.n)] + all_dma_toks()
    ln_stack.close()

    q_stack = ExitStack()
    quant_on = phases >= 2 and dev_quant
    if quant_on:
        sbq = lambda name, shape, dt: q_stack.enter_context(
            nc.sbuf_tensor(name, shape, dt))
        stA = sbq("stA", [128, 2 * 1408], F32)
        gSA = [Guard(), Guard()]
        accw = sbq("accw", [128, 4], F32)
        gACC = Guard()
        acchl = sbq("acchl", [128, 2], BF16)
        stB = sbq("stB", [128, 2 * 2048], F32)
        gSB = [Guard(), Guard()]
        qneg = sbq("qneg", [128, 2048], BF16)
        gQN = Guard()
        qout = sbq("qout", [128, 2 * 2048], BF16)
        gQO = [Guard(), Guard()]
        wsrcs = [(w1T, D, HID), (w2T, D, HID), (w3T, HID, D)]

    qA_jobs = []
    if quant_on:
        for wi, (wsrc, nr, ncl) in enumerate(wsrcs):
            pw = 1408 if ncl % 1408 == 0 else 1024
            assert ncl % pw == 0
            for tile in range(nr // 128):
                for pc in range(ncl // pw):
                    qA_jobs.append((wi, tile, pc, pw,
                                    tile == 0 and pc == 0))
    qA_pos = [0]

    def emit_passA(njobs):
        end = min(qA_pos[0] + njobs, len(qA_jobs))
        while qA_pos[0] < end:
            wi, tile, pc, pw, first = qA_jobs[qA_pos[0]]
            wsrc, nr, ncl = wsrcs[wi]
            b = qA_pos[0] % 2
            qA_pos[0] += 1
            stv = stA[:, b * 1408:b * 1408 + pw]
            dma("sync", stv,
                wsrc[tile * 128:(tile + 1) * 128,
                     pc * pw:(pc + 1) * pw], cL, writes=[gSA[b]])
            vop("redabs", accw[:, 3:4], stv, ALU.add,
                reads=[gSA[b]], writes=[gACC], inc=cV)
            if first:
                vop("copy", accw[:, wi:wi + 1], accw[:, 3:4],
                    reads=[gACC], writes=[gACC], inc=cV)
            else:
                vop("tt", accw[:, wi:wi + 1], accw[:, wi:wi + 1],
                    accw[:, 3:4], ALU.add,
                    reads=[gACC], writes=[gACC], inc=cV)

    def emit_gamma(wi):
        _, nr, ncl = wsrcs[wi]
        vop("copy", acchl[:, 0:1], accw[:, wi:wi + 1],
            reads=[gACC], writes=[gACC], inc=cV)
        vop("tt", accw[:, 3:4], accw[:, wi:wi + 1], acchl[:, 0:1],
            ALU.subtract, writes=[gACC], inc=cV)
        vop("copy", acchl[:, 1:2], accw[:, 3:4], writes=[gACC], inc=cV)
        mm(psR[0:1, 17:18], acchl[:, 0:1], ones_c[:], True, False,
           reads=[gACC, gC], writes=[gR], inc=cT)
        mm(psR[0:1, 17:18], acchl[:, 1:2], ones_c[:], False, True,
           reads=[gACC], writes=[gR], inc=cT)
        cnt = nr * ncl
        vop("ts", sc11[0:1, 15:16], psR[0:1, 17:18], 0.5 / cnt, ALU.mult,
            reads=[gR], writes=[gSC], inc=cV)
        vop("ts", sc11[0:1, 15:16], sc11[0:1, 15:16], 0.5 * EPS_Q, ALU.add,
            writes=[gSC], inc=cV)
        vop("copy", sc11b[0:1, 2:3], sc11[0:1, 15:16], writes=[gSC], inc=cV)
        vop("tt", sc11[0:1, 14:15], sc11[0:1, 15:16], sc11b[0:1, 2:3],
            ALU.subtract, writes=[gSC], inc=cV)
        vop("copy", sc11b[0:1, 3:4], sc11[0:1, 14:15], writes=[gSC], inc=cV)
        mm(psR[:, 18:19], ones_r[:], sc11b[0:1, 2:3], True, False,
           reads=[gSC], writes=[gR], inc=cT)
        mm(psR[:, 18:19], ones_r[:], sc11b[0:1, 3:4], False, True,
           reads=[gSC], writes=[gR], inc=cT)
        vop("copy", thr_sb[:, wi:wi + 1], psR[:, 18:19],
            reads=[gR], writes=[gTHR[wi]], inc=cV)
        vop("ts", thrn_sb[:, wi:wi + 1], thr_sb[:, wi:wi + 1], -1.0,
            ALU.mult, reads=[gTHR[wi]], writes=[gTHR[wi]], inc=cV)

    qB_jobs = []
    if quant_on:
        for j in range(NH3):
            qB_jobs.append((0,
                            w1T[:, j * 128:(j + 1) * 128].rearrange(
                                "(c p) n -> p c n", p=128),
                            q1s[j * 128:(j + 1) * 128, :], True))
            qB_jobs.append((1,
                            w2T[:, j * 128:(j + 1) * 128].rearrange(
                                "(c p) n -> p c n", p=128),
                            q2s[j * 128:(j + 1) * 128, :], True))
        for tile in range(NH3):
            qB_jobs.append((2, w3T[tile * 128:(tile + 1) * 128, :],
                            q3s[tile * 128:(tile + 1) * 128, :], False))
    qB_pos = [0]

    def emit_passB(njobs):
        end = min(qB_pos[0] + njobs, len(qB_jobs))
        while qB_pos[0] < end:
            wi, src, dst, is3d = qB_jobs[qB_pos[0]]
            b = qB_pos[0] % 2
            qB_pos[0] += 1
            stv = stB[:, b * 2048:(b + 1) * 2048]
            qo = qout[:, b * 2048:(b + 1) * 2048]
            dma("sync", stv.rearrange("p (c n) -> p c n", c=NCD) if is3d
                else stv, src, cL, writes=[gSB[b]])
            vop("ts", qo, stv, thr_sb[:, wi:wi + 1], ALU.is_ge,
                reads=[gSB[b], gTHR[wi]], writes=[gQO[b]], inc=cV)
            vop("ts", qneg[:], stv, thrn_sb[:, wi:wi + 1], ALU.is_le,
                reads=[gSB[b], gTHR[wi]], writes=[gQN], inc=cV)
            vop("tt", qo, qo, qneg[:], ALU.subtract,
                reads=[gQN], writes=[gQO[b]], inc=cV)
            dma("gpsimd", dst, qo, cS, reads=[gQO[b]])

    at_stack = ExitStack()
    o_wr_toks = {}
    if phases >= 3:
        sba = lambda name, shape, dt: at_stack.enter_context(
            nc.sbuf_tensor(name, shape, dt))
        qf = sba("qf", [128, 4 * S], BF16)
        kf = sba("kf", [128, 4 * S], BF16)
        gQF = [Guard() for _ in range(4)]
        gKF = [Guard() for _ in range(4)]
        vt = sba("vt", [128, NST * DH], BF16)
        gVT = [Guard() for _ in range(NST)]
        sqb = sba("sqb", [128, 512], BF16)
        gSQB = Guard()
        ebuf = [sba(f"ebuf{i}", [128, 512], BF16) for i in range(3)]
        gE = [Guard(), Guard(), Guard()]
        pvsb = sba("pvsb", [128, 8 * 257], F32)
        gPVS = Guard()
        rsb = sba("rsb", [128, 8], F32)
        gRS = Guard()
        onq = [sba(f"onq{i}", [128, 4 * DH], F32) for i in range(2)]
        gON = [Guard(), Guard()]
        odiff = sba("odiff", [128, 256], F32)
        gOD = Guard()
        otmp = sba("otmp", [128, 256], F32)
        gOT = Guard()
        osq, gOSQ = otmp, gOT
        rmsc = sba("rmsc", [128, 16], F32)
        gRMS = Guard()
        obf = [sba(f"obf{i}", [128, 512], BF16) for i in range(4)]
        gOBF = [Guard() for _ in range(4)]

        pend_transp = []
        tpi = [0]

        def flush_transposes():
            while pend_transp:
                onqb, head, qg, gONb = pend_transp.pop(0)
                for fc in range(2):
                    ob = obf[(qg % 2) * 2 + fc]
                    gob = gOBF[(qg % 2) * 2 + fc]
                    for sub in range(4):
                        reg = tpi[0] % 2
                        tpi[0] += 1
                        pt = (psA if reg == 0 else psB)[:, 0:128]
                        gpt = gA if reg == 0 else gB
                        transp(pt, onqb[:, sub * DH + fc * 128:
                                        sub * DH + fc * 128 + 128],
                               idenf[:], reads=[gONb, gC], writes=[gpt],
                               inc=cT)
                        vop("copy", ob[:, sub * 128:(sub + 1) * 128], pt,
                            reads=[gpt], writes=[gob], inc=cV)
                    row0 = ((head * 2 + fc) * 4 + qg) * 128
                    tok = dma("gpsimd", o_scr[row0:row0 + 128, :], ob[:],
                              cS, reads=[gob])
                    o_wr_toks[(head * 2 + fc, qg)] = tok

        def emit_wload(wsrc, colr, slot):
            ws, gws = wstage, gWS
            wb, gwb = wbf[slot % 3], gWB[slot % 3]
            for half in range(2):
                hc = slice(colr.start + half * 128, colr.start + half * 128 + 128)
                dma_chunked("sync", ws[:].rearrange("p (c n) -> p c n", c=NCD),
                            wsrc[:, hc].rearrange("(c p) n -> p c n", p=128),
                            cL, 2, NCD, writes=[gws])
                vop("copy",
                    wb[:].rearrange("p (c n) -> p c n", c=NCD)[
                        :, :, half * 128:(half + 1) * 128],
                    ws[:].rearrange("p (c n) -> p c n", c=NCD),
                    eng="gpsimd", reads=[gws], writes=[gwb], inc=cP)

        def proj_qk(dst, dstg, hh_local, slot):
            scale = 0.25
            wb, gwb = wbf[slot % 3], gWB[slot % 3]
            for m in range(2):
                r_ = 2 * hh_local + m
                for sg in range(4):
                    ps, gps = (psA, gA) if (m * 4 + sg) % 2 == 0 else (psB, gB)
                    for c in range(NCD):
                        mm(ps[:], wb[:, c * 256 + m * 128:
                                     c * 256 + m * 128 + 128],
                           xnc(c, slice(sg * 512, (sg + 1) * 512)),
                           c == 0, c == NCD - 1,
                           reads=[gwb] + gXNq, writes=[gps],
                           inc=cT if c == NCD - 1 else None)
                    acopy(dst[:, r_ * S + sg * 512:r_ * S + (sg + 1) * 512],
                          ps[:], scale=scale,
                          reads=[gps], writes=[dstg[r_]], inc=cA)

        def proj_v(head, slot):
            wb, gwb = wbf[slot % 3], gWB[slot % 3]
            for t2 in range(NST):
                ps, gps = (psA, gA) if t2 % 2 == 0 else (psB, gB)
                for c in range(NCD):
                    mm(ps[:, 0:256],
                       xnc(c, slice(t2 * 128, (t2 + 1) * 128)),
                       wb[:, c * 256:c * 256 + 256],
                       c == 0, c == NCD - 1,
                       reads=[gwb] + gXNq, writes=[gps],
                       inc=cT if c == NCD - 1 else None)
                acopy(vt[:, t2 * DH:(t2 + 1) * DH], ps[:, 0:256],
                      reads=[gps], writes=[gVT[t2]], inc=cA)

        def wload_job(head, i):
            base = head * 5
            hhA, hhB = head, head + 8
            if i == 0:
                emit_wload(wqT, slice(256 * hhA, 256 * hhA + 256), base + 0)
            elif i == 1:
                emit_wload(wqT, slice(256 * hhB, 256 * hhB + 256), base + 1)
            elif i == 2:
                emit_wload(wkT, slice(256 * hhA, 256 * hhA + 256), base + 2)
            elif i == 3:
                emit_wload(wkT, slice(256 * hhB, 256 * hhB + 256), base + 3)
            else:
                emit_wload(wvT, slice(256 * head, 256 * head + 256), base + 4)

        def emit_head(head, bg_fn=None):
            base = head * 5
            if not DEV_PREFETCH and head > 0:
                for i in range(3):
                    wload_job(head, i)
            proj_qk(qf, gQF, 0, base + 0)
            wload_job(head, 3)
            proj_qk(qf, gQF, 1, base + 1)
            wload_job(head, 4)
            proj_qk(kf, gKF, 0, base + 2)
            proj_qk(kf, gKF, 1, base + 3)
            proj_v(head, base + 4)
            nexth = head + 1 if head + 1 < min(H, dev_heads) else None
            if not DEV_PREFETCH:
                nexth = None
            if head == 0:
                if "qf" in taps:
                    dma("sync", tap_outs["qf"], qf[:], cL, reads=list(gQF))
                if "kf" in taps:
                    dma("sync", tap_outs["kf"], kf[:], cL, reads=list(gKF))
                if "vt" in taps:
                    dma("sync", tap_outs["vt"], vt[:], cL, reads=list(gVT))

            if DEV_STAGE == "proj":
                return
            for hh in range(2):
                for which, (src, gsrc) in enumerate([(qf, gQF), (kf, gKF)]):
                    for cc in range(8):
                        for ri in range(2):
                            r_ = 2 * hh + ri
                            act(sqb[:, ri * 256:(ri + 1) * 256],
                                src[:, r_ * S + cc * 256:
                                    r_ * S + (cc + 1) * 256],
                                AF.Square, reads=[gsrc[r_]], writes=[gSQB],
                                inc=cA)
                        for ri in range(2):
                            mm(psR[0:1, 256:512], ones_c[:],
                               sqb[:, ri * 256:(ri + 1) * 256],
                               ri == 0, ri == 1,
                               reads=[gSQB, gC], writes=[gR],
                               inc=cT if ri == 1 else None)
                        vop("red", sc11[0:1, which * 8 + cc:
                                        which * 8 + cc + 1],
                            psR[0:1, 256:512], ALU.max, reads=[gR],
                            writes=[gSC], inc=cV)
                vop("red", sc11b_f[0:1, 0:1], sc11[0:1, 0:8], ALU.max,
                    writes=[gSC], inc=cV)
                vop("red", sc11b_f[0:1, 1:2], sc11[0:1, 8:16], ALU.max,
                    writes=[gSC], inc=cV)
                vop("tt", sc11b_f[0:1, 2:3], sc11b_f[0:1, 0:1],
                    sc11b_f[0:1, 1:2], ALU.add, writes=[gSC], inc=cV)
                vop("ts", sc11b_f[0:1, 3:4], sc11b_f[0:1, 2:3], -0.5,
                    ALU.mult, writes=[gSC], inc=cV)
                vop("copy", sc11b[0:1, 4:5], sc11b_f[0:1, 3:4],
                    writes=[gSC], inc=cV)
                mm(psR[:, 19:20], ones_r[:], sc11b[0:1, 4:5], True, True,
                   reads=[gSC, gC], writes=[gR], inc=cT)
                vop("copy", biasB[hh][:], psR[:, 19:20], reads=[gR],
                    writes=[gBB[hh]], inc=cV)
            if head == 0 and "bias0" in taps:
                dma("sync", tap_outs["bias0"], biasB[0][:], cL, reads=[gBB[0]])
                dma("sync", tap_outs["bias1"], biasB[1][:], cL, reads=[gBB[1]])

            if DEV_STAGE == "bounds":
                return
            for qg in range(4):
                if nexth is not None and qg > 0:
                    wload_job(nexth, qg - 1)
                if bg_fn is not None and qg > 0:
                    bg_fn()
                for hh in range(2):
                    nkt = 4 * qg + 4

                    def emit_score(kt, qg=qg, hh=hh):
                        diag = kt >= 4 * qg
                        c0 = 128 * kt - 512 * qg if diag else 0
                        eb, geb = ebuf[kt % 3], gE[kt % 3]
                        ps, gps = [(psA, gA), (psB, gB), (psC, gCps)][kt % 3]
                        for ri in range(2):
                            r_ = 2 * hh + ri
                            mm(ps[:, c0:512],
                               kf[:, r_ * S + kt * 128:
                                  r_ * S + kt * 128 + 128],
                               qf[:, r_ * S + qg * 512 + c0:
                                  r_ * S + (qg + 1) * 512],
                               ri == 0, ri == 1,
                               reads=[gKF[r_], gQF[r_]], writes=[gps],
                               inc=cT if ri == 1 else None)
                        act(eb[:, c0:512], ps[:, c0:512], AF.Exp,
                            bias=biasB[hh][:],
                            reads=[gps, gBB[hh]], writes=[geb], inc=cA)
                        if diag:
                            vop("tt", eb[:, c0:c0 + 128], eb[:, c0:c0 + 128],
                                mask01[:], ALU.mult, eng="gpsimd",
                                reads=[gC], writes=[geb], inc=cP)

                    def emit_pv(kt, qg=qg, hh=hh):
                        diag = kt >= 4 * qg
                        c0 = 128 * kt - 512 * qg if diag else 0
                        eb, geb = ebuf[kt % 3], gE[kt % 3]
                        sub0 = c0 // 128
                        for sub in range(sub0, 4):
                            ecol = slice(sub * 128, sub * 128 + 128)
                            start_pv = kt == 0
                            stop_pv = kt == 4 * qg + sub
                            bank = 2 * hh + (sub // 2)
                            reg = psPV[bank][:, (sub % 2) * 256:
                                             (sub % 2) * 256 + 256]
                            mm(reg, eb[:, ecol], vt[:, kt * DH:(kt + 1) * DH],
                               start_pv and sub % 2 == 0, stop_pv,
                               reads=[geb, gVT[kt]], writes=[gPV[bank]],
                               inc=cT if stop_pv else None)
                            rcol_i = (qg % 2) * 8 + hh * 4 + sub
                            mm(psR[:, rcol_i:rcol_i + 1], eb[:, ecol],
                               ones_c[:],
                               start_pv and sub == 0, stop_pv,
                               reads=[geb, gC, gVT[kt]], writes=[gR],
                               inc=cT if sub == 3 else None)

                    for kt in range(nkt):
                        emit_score(kt)
                        if kt >= 2:
                            emit_pv(kt - 2)
                    emit_pv(nkt - 2)
                    emit_pv(nkt - 1)
                    if bg_fn is not None and hh == 0 and qg >= 2:
                        bg_fn()
                    for sub in range(4):
                        bank = 2 * hh + (sub // 2)
                        reg = psPV[bank][:, (sub % 2) * 256:
                                         (sub % 2) * 256 + 256]
                        acopy(pvsb[:, (hh * 4 + sub) * 257:
                                   (hh * 4 + sub) * 257 + 256], reg,
                              reads=[gPV[bank]], writes=[gPVS], inc=cA)
                    acopy(rsb[:, hh * 4:hh * 4 + 4],
                          psR[:, (qg % 2) * 8 + hh * 4:
                              (qg % 2) * 8 + hh * 4 + 4],
                          reads=[gR], writes=[gRS], inc=cA)
                if DEV_STAGE == "loop":
                    continue
                flush_transposes()
                onqb, gONb = onq[qg % 2], gON[qg % 2]
                for sub in range(4):
                    p1 = pvsb[:, sub * 257:sub * 257 + 256]
                    p2 = pvsb[:, (4 + sub) * 257:(4 + sub) * 257 + 256]
                    vop("recip", rmsc[:, 8:9], rsb[:, sub:sub + 1],
                        reads=[gRS], writes=[gRMS], inc=cV)
                    vop("recip", rmsc[:, 9:10], rsb[:, 4 + sub:5 + sub],
                        reads=[gRS], writes=[gRMS], inc=cV)
                    vop("tt", rmsc[:, 10:11], rmsc[:, 9:10], lam_sb[:],
                        ALU.mult, reads=[gLAM], writes=[gRMS], inc=cV)
                    vop("ts", odiff[:], p1, rmsc[:, 8:9], ALU.mult,
                        reads=[gPVS, gRMS], writes=[gOD], inc=cV)
                    vop("ts", otmp[:], p2, rmsc[:, 10:11], ALU.mult,
                        reads=[gPVS, gRMS], writes=[gOT], inc=cV)
                    vop("tt", odiff[:], odiff[:], otmp[:], ALU.subtract,
                        reads=[gOT], writes=[gOD], inc=cV)
                    act(osq[:], odiff[:], AF.Square,
                        reads=[gOD], writes=[gOSQ], inc=cA)
                    vop("red", rmsc[:, 0:8],
                        osq[:].rearrange("p (a b) -> p a b", a=8), ALU.add,
                        reads=[gOSQ], writes=[gRMS], inc=cV)
                    act(rmsc[:, 0:8], rmsc[:, 0:8], AF.Ln, bias=epsrms[:],
                        scale=1.0 / 32, reads=[gRMS, gC], writes=[gRMS],
                        inc=cA)
                    act(rmsc[:, 0:8], rmsc[:, 0:8], AF.Exp, scale=-0.5,
                        reads=[gRMS], writes=[gRMS], inc=cA)
                    for g8 in range(8):
                        vop("ts", onqb[:, sub * DH + g8 * 32:
                                       sub * DH + g8 * 32 + 32],
                            odiff[:, g8 * 32:(g8 + 1) * 32],
                            rmsc[:, g8:g8 + 1], ALU.mult,
                            reads=[gOD, gRMS], writes=[gONb], inc=cV)
                pend_transp.append((onqb, head, qg, gONb))

        for i in range(3):
            wload_job(0, i)
        em.do("sync", lambda e: e.nop(), waits=tok_ln)
        npA = (len(qA_jobs) + 11) // 12 if quant_on else 0
        bgA = (lambda: emit_passA(npA)) if quant_on else None
        for head in range(min(4, dev_heads)):
            emit_head(head, bgA)
            if quant_on:
                emit_passA(npA)
        if dev_heads > 4:
            emit_head(4, bgA)
        if quant_on:
            emit_passA(len(qA_jobs))
            emit_gamma(0)
            emit_gamma(1)
            emit_gamma(2)
            if "thr" in taps:
                dma("sync", tap_outs["thr"], thr_sb[:], cL, reads=[gTHR[2]])
        bgB = (lambda: emit_passB(7)) if quant_on else None
        for head in range(5, min(H, dev_heads)):
            emit_head(head, bgB)
            if quant_on:
                emit_passB(7)
        flush_transposes()
        if quant_on:
            emit_passB(len(qB_jobs))
    tok_attn = [(cV, cV.n), (cA, cA.n), (cT, cT.n), (cP, cP.n)] + all_dma_toks()
    at_stack.close()

    wo_stack = ExitStack()
    if phases >= 4:
        sbw = lambda name, shape, dt: wo_stack.enter_context(
            nc.sbuf_tensor(name, shape, dt))
        wostage = sbw("wostage", [128, NCD * 256], F32)
        gWOS = Guard()
        wobf = sbw("wobf", [128, NCD * 512], BF16)
        gWOB = Guard()
        oslice = [sbw(f"oslice{i}", [128, NCD * 512], BF16) for i in range(2)]
        gOSL = [Guard(), Guard()]
        sstage = sbw("sstage", [128, 512], BF16)
        gSST = Guard()

        for ds in range(4):
            dcols = slice(ds * 512, (ds + 1) * 512)
            for half in range(2):
                hcols = slice(ds * 512 + half * 256, ds * 512 + half * 256 + 256)
                dma_chunked("sync",
                            wostage[:].rearrange("p (c n) -> p c n", c=NCD),
                            woT[:, hcols].rearrange("(c p) n -> p c n", p=128),
                            cL, 2, NCD, writes=[gWOS],
                            waits=tok_attn if ds == 0 and half == 0 else ())
                vop("ts",
                    wobf[:].rearrange("p (c n) -> p c n", c=NCD)[
                        :, :, half * 256:(half + 1) * 256],
                    wostage[:].rearrange("p (c n) -> p c n", c=NCD),
                    OML, ALU.mult,
                    reads=[gWOS], writes=[gWOB], inc=cV)
            for qg in range(4):
                osl, gosl = oslice[qg % 2], gOSL[qg % 2]
                for c in range(NCD):
                    row0 = (c * 4 + qg) * 128
                    dma("sync", osl[:, c * 512:(c + 1) * 512],
                        o_scr[row0:row0 + 128, :], cL, writes=[gosl],
                        waits=[o_wr_toks[(c, qg)]] if phases >= 3 else ())
                for sub in range(4):
                    ps, gps = (psA, gA) if sub % 2 == 0 else (psB, gB)
                    for c in range(NCD):
                        mm(ps[:], osl[:, c * 512 + sub * 128:
                                      c * 512 + sub * 128 + 128],
                           wobf[:, c * 512:(c + 1) * 512],
                           c == 0, c == NCD - 1,
                           reads=[gosl, gWOB], writes=[gps],
                           inc=cT if c == NCD - 1 else None)
                    acopy(sstage[:], ps[:], reads=[gps], writes=[gSST],
                          inc=cA)
                    t_ = qg * 4 + sub
                    dma("scalar", sattn[t_ * 128:(t_ + 1) * 128, dcols],
                        sstage[:], cS, reads=[gSST])
    tok_wo = [(cV, cV.n), (cA, cA.n), (cT, cT.n), (cP, cP.n)] + all_dma_toks()
    tok_sattn = [(c, c.n) for c in cSs if c.n > 0]
    wo_stack.close()
    q_stack.close()
    wq_stack.close()

    ffn_stack = ExitStack()
    if phases >= 5:
        sbf = lambda name, shape, dt: ffn_stack.enter_context(
            nc.sbuf_tensor(name, shape, dt))
        hid = sbf("hid", [128, NH3 * 512], BF16)
        gHID = [Guard() for _ in range(NH3)]
        q12 = [sbf(f"q12_{i}", [128, NCD * 128], BF16) for i in range(4)]
        gQ12 = [Guard() for _ in range(4)]
        q3sl = [sbf(f"q3sl{i}", [128, NH3 * 256], BF16) for i in range(2)]
        gQ3 = [Guard(), Guard()]
        sigb = sbf("sigb", [128, 512], F32)
        gSIG = Guard()
        xsl = sbf("xsl", [128, 512], F32)
        gXSL = [Guard(), Guard()]
        ssl = sbf("ssl", [128, 512], BF16)
        gSSL = [Guard(), Guard()]
        sfup = sbf("sfup", [128, 256], F32)
        gSF = Guard()
        ost = sbf("ost", [128, 512], F32)
        gOST = [Guard(), Guard()]

        tok_q = tok_wo
        q3v = q3s.rearrange("(j p) n -> p j n", p=128)
        for blk in range(4):
            bcols = slice(blk * 512, (blk + 1) * 512)
            for j in range(NH3):
                b1, g1 = q12[(2 * j) % 4], gQ12[(2 * j) % 4]
                b2, g2 = q12[(2 * j + 1) % 4], gQ12[(2 * j + 1) % 4]
                dma("sync", b1[:], q1s[j * 128:(j + 1) * 128, :], cL,
                    writes=[g1], waits=tok_q if blk == 0 and j < 2 else ())
                dma("sync", b2[:], q2s[j * 128:(j + 1) * 128, :], cL,
                    writes=[g2], waits=tok_q if blk == 0 and j < 2 else ())
                for c in range(NCD):
                    mm(psA[:], b1[:, c * 128:(c + 1) * 128], xnc(c, bcols),
                       c == 0, c == NCD - 1, reads=[g1, gXNq[blk]],
                       writes=[gA], inc=cT if c == NCD - 1 else None)
                for c in range(NCD):
                    mm(psB[:], b2[:, c * 128:(c + 1) * 128], xnc(c, bcols),
                       c == 0, c == NCD - 1, reads=[g2, gXNq[blk]],
                       writes=[gB], inc=cT if c == NCD - 1 else None)
                act(sigb[:], psA[:], AF.Sigmoid,
                    reads=[gA], writes=[gSIG], inc=cA)
                vop("tt", hid[:, j * 512:(j + 1) * 512], sigb[:], psB[:],
                    ALU.mult, reads=[gSIG, gB], writes=[gHID[j]], inc=cV)
            if blk == 0 and "hid" in taps:
                dma("sync", tap_outs["hid"], hid[:], cL,
                    reads=[gHID[jj] for jj in range(NH3)])
            for dch in range(8):
                q3b, gq3 = q3sl[dch % 2], gQ3[dch % 2]
                for piece in range(4):
                    dma("sync",
                        q3b[:].rearrange("p (j n) -> p j n", j=NH3)[
                            :, piece * 11:(piece + 1) * 11, :],
                        q3v[:, piece * 11:(piece + 1) * 11,
                            dch * 256:(dch + 1) * 256],
                        cL, writes=[gq3],
                        waits=tok_q if blk == 0 and dch < 2 else ())
                dc = slice(dch * 256, (dch + 1) * 256)
                for t_ in range(4):
                    trow = blk * 4 + t_
                    nb = (dch * 4 + t_) % 2
                    xsl_v = xsl[:, nb * 256:(nb + 1) * 256]
                    ssl_v = ssl[:, nb * 256:(nb + 1) * 256]
                    ost_v = ost[:, nb * 256:(nb + 1) * 256]
                    dma("sync", xsl_v,
                        x_in[trow * 128:(trow + 1) * 128, dc], cL,
                        writes=[gXSL[nb]])
                    dma("sync", ssl_v,
                        sattn[trow * 128:(trow + 1) * 128, dc], cL,
                        writes=[gSSL[nb]],
                        waits=tok_sattn if phases >= 4 else ())
                    idx = (dch * 4 + t_) % 8
                    bank = idx % 4
                    reg = psPV[bank][:, (idx // 4) * 256:
                                     (idx // 4) * 256 + 256]
                    tok = None
                    for j in range(NH3):
                        tok = mm(reg,
                                 hid[:, j * 512 + t_ * 128:
                                     j * 512 + t_ * 128 + 128],
                                 q3b[:, j * 256:(j + 1) * 256],
                                 j == 0, j == NH3 - 1,
                                 reads=[gHID[j], gq3], writes=[gPV[bank]],
                                 inc=cT if j == NH3 - 1 else None)
                    for j in range(NH3):
                        gHID[j].r.append(tok)
                    gq3.r.append(tok)
                    act(sigb[:, 0:256], reg, AF.Silu,
                        reads=[gPV[bank]], writes=[gSIG], inc=cA)
                    act(sfup[:], ssl_v, AF.Identity,
                        reads=[gSSL[nb]], writes=[gSF], inc=cA)
                    vop("tt", sigb[:, 256:512], sigb[:, 0:256], sfup[:],
                        ALU.mult, reads=[gSIG, gSF], writes=[gSIG], inc=cV)
                    vop("tt", ost_v, sigb[:, 256:512], xsl_v,
                        ALU.add, reads=[gSIG, gXSL[nb]], writes=[gOST[nb]],
                        inc=cV)
                    dma("gpsimd", out_d[trow * 128:(trow + 1) * 128, dc],
                        ost_v, cS, reads=[gOST[nb]])

    em.do("sync", lambda e: e.nop(), waits=all_dma_toks())

    with nc.Block() as block:
        for eng in ENGINES:
            stream = em.streams[eng]

            def runner(engine, _s=stream):
                for f in _s:
                    f(engine)
            getattr(block, eng)(runner)

    ffn_stack.close()
    st.close()
    return nc, em.counts


def kernel(L0_attn_x, L0_attn_l_q1, L0_attn_l_k1, L0_attn_l_q2, L0_attn_l_k2,
           L0_attn_wq, L0_attn_wk, L0_attn_wv, L0_attn_wo,
           L0_ffn_w1, L0_ffn_w2, L0_ffn_w3):
    x = np.ascontiguousarray(np.asarray(L0_attn_x, dtype=np.float32))
    f32 = lambda a: np.asarray(a, dtype=np.float32)
    tr = lambda a: np.ascontiguousarray(np.asarray(a, dtype=np.float32).T)
    shared = {
        "wqT": tr(L0_attn_wq), "wkT": tr(L0_attn_wk), "wvT": tr(L0_attn_wv),
        "woT": tr(L0_attn_wo), "w1T": tr(L0_ffn_w1), "w2T": tr(L0_ffn_w2),
        "w3T": tr(L0_ffn_w3),
        "lvec": np.ascontiguousarray(np.concatenate(
            [f32(L0_attn_l_q1), f32(L0_attn_l_k1),
             f32(L0_attn_l_q2), f32(L0_attn_l_k2)]).reshape(1, 32)),
        "mask01": np.triu(np.ones((128, 128))).astype(ml_dtypes.bfloat16),
        "idenb": np.eye(128).astype(ml_dtypes.bfloat16),
        "idenf": np.eye(128, dtype=np.float32),
    }
    in_maps = []
    for c in range(8):
        m = dict(shared)
        m["x"] = x[c]
        m["xT"] = np.ascontiguousarray(x[c].T)
        in_maps.append(m)
    nc, _ = build_nc(taps=TAPS, phases=PHASES)
    res = run_bass_kernel_spmd(nc, in_maps, core_ids=list(range(8)))
    global LAST_RESULTS
    LAST_RESULTS = res.results
    return np.stack([np.asarray(res.results[c]["out"]) for c in range(8)], 0)


LAST_RESULTS = None


# version 22 (final)
# speedup vs baseline: 1.0407x; 1.0132x over previous
import math
from contextlib import ExitStack

import numpy as np
import ml_dtypes
import concourse.bass as bass
import concourse.mybir as mybir
from concourse.bass_utils import run_bass_kernel_spmd

F32 = mybir.dt.float32
BF16 = mybir.dt.bfloat16
AF = mybir.ActivationFunctionType
ALU = mybir.AluOpType
AX = mybir.AxisListType

S = 2048
D = 2048
H = 8
DH = 256
HID = 5632
NCD = D // 128
NST = S // 128
NH3 = HID // 128
L_INIT = 0.8 - 0.6 * math.exp(-0.3)
OML = 1.0 - L_INIT
EPS_LN = 1e-5
EPS_RMS = 1e-6
EPS_Q = 1e-8

TAPS = ()
DEV_HEADS = 8
DEV_QUANT = True
DEV_STAGE = 'full'
DEV_PREFETCH = True
PHASES = 99

ENGINES = ("sync", "tensor", "vector", "scalar", "gpsimd")


class Ch:
    def __init__(self, nc, ctx, name):
        self.sem = ctx.enter_context(nc.semaphore(name))
        self.n = 0
        self.name = name


class Guard:
    __slots__ = ("w", "r")

    def __init__(self):
        self.w = None
        self.r = []


class Emit:
    OWN = {"tensor": "cT"}

    def __init__(self):
        self.streams = {e: [] for e in ENGINES}
        self.waited = {e: {} for e in ENGINES}
        self.counts = {e: 0 for e in ENGINES}

    def _wait(self, eng, ch, tok):
        if tok is None or tok <= 0:
            return
        if self.OWN.get(eng) == ch.name:
            return
        w = self.waited[eng]
        if w.get(ch.name, 0) >= tok:
            return
        w[ch.name] = tok
        self.streams[eng].append(lambda e, s=ch.sem, t=tok: e.wait_ge(s, t))

    def do(self, eng, fn, reads=(), writes=(), waits=(), inc=None, inc_n=1):
        need = []
        for g in reads:
            if g.w is not None:
                need.append(g.w)
        for g in writes:
            if g.w is not None:
                need.append(g.w)
            need.extend(g.r)
        need.extend(waits)
        for ch, tok in need:
            self._wait(eng, ch, tok)
        tok = None
        if inc is not None:
            inc.n += inc_n
            tok = (inc, inc.n)
            self.streams[eng].append(
                lambda e, f=fn, s=inc.sem, v=inc_n: f(e).then_inc(s, v))
            for g in writes:
                g.w = tok
                g.r = []
            for g in reads:
                g.r.append(tok)
        else:
            self.streams[eng].append(lambda e, f=fn: f(e))
        self.counts[eng] += 1
        return tok


def build_nc(taps=(), phases=99, dev_heads=None, dev_quant=None):
    dev_heads = DEV_HEADS if dev_heads is None else dev_heads
    dev_quant = DEV_QUANT if dev_quant is None else dev_quant
    nc = bass.Bass("TRN2", target_bir_lowering=False)
    em = Emit()
    st = ExitStack()

    ap = lambda t: t.ap()
    x_in = ap(nc.dram_tensor("x", [S, D], F32, kind="ExternalInput"))
    xT_in = ap(nc.dram_tensor("xT", [D, S], F32, kind="ExternalInput"))
    wqT = ap(nc.dram_tensor("wqT", [D, 2 * D], F32, kind="ExternalInput"))
    wkT = ap(nc.dram_tensor("wkT", [D, 2 * D], F32, kind="ExternalInput"))
    wvT = ap(nc.dram_tensor("wvT", [D, D], F32, kind="ExternalInput"))
    woT = ap(nc.dram_tensor("woT", [D, D], F32, kind="ExternalInput"))
    w1T = ap(nc.dram_tensor("w1T", [D, HID], F32, kind="ExternalInput"))
    w2T = ap(nc.dram_tensor("w2T", [D, HID], F32, kind="ExternalInput"))
    w3T = ap(nc.dram_tensor("w3T", [HID, D], F32, kind="ExternalInput"))
    lvec = ap(nc.dram_tensor("lvec", [1, 32], F32, kind="ExternalInput"))
    mask01_in = ap(nc.dram_tensor("mask01", [128, 128], BF16,
                                  kind="ExternalInput"))
    idenb_in = ap(nc.dram_tensor("idenb", [128, 128], BF16,
                                 kind="ExternalInput"))
    idenf_in = ap(nc.dram_tensor("idenf", [128, 128], F32,
                                 kind="ExternalInput"))
    out_d = ap(nc.dram_tensor("out", [S, D], F32, kind="ExternalOutput"))

    def dscr(name, shape, dt):
        kind = "ExternalOutput" if name in taps else "Internal"
        return ap(nc.dram_tensor(name, shape, dt, kind=kind))

    stat_scr = dscr("stat_scr", [2, S], F32)
    q1s = dscr("q1s", [NH3 * 128, NCD * 128], BF16)
    q2s = dscr("q2s", [NH3 * 128, NCD * 128], BF16)
    q3s = dscr("q3s", [HID, D], BF16)
    o_scr = dscr("o_scr", [NCD * 4 * 128, 512], BF16)
    sattn = dscr("sattn", [S, D], BF16)

    tap_outs = {}
    for tname, shape, dt in [
        ("lam", [128, 1], F32), ("xn", [128, NCD * S], BF16),
        ("qf", [128, 4 * S], BF16), ("kf", [128, 4 * S], BF16),
        ("vt", [128, NST * DH], BF16), ("thr", [128, 3], F32),
        ("bias0", [128, 1], F32), ("bias1", [128, 1], F32),
        ("sc11", [1, 16], F32), ("hid", [128, NH3 * 512], BF16),
    ]:
        if tname in taps:
            tap_outs[tname] = ap(
                nc.dram_tensor("tap_" + tname, shape, dt, kind="ExternalOutput"))

    cLs = [Ch(nc, st, f"cL{i}") for i in range(14)]
    cSs = [Ch(nc, st, f"cS{i}") for i in range(6)]
    cGs = [Ch(nc, st, f"cG{i}") for i in range(6)]
    _lane = {"sync": [0], "scalar": [0], "gpsimd": [0]}
    _pools = {"sync": cLs, "scalar": cSs, "gpsimd": cGs}

    def dma_lane(queue):
        pool = _pools[queue]
        i = _lane[queue][0]
        _lane[queue][0] = (i + 1) % len(pool)
        return pool[i]

    def all_dma_toks():
        return [(c, c.n) for c in cLs + cSs + cGs if c.n > 0]
    cT = Ch(nc, st, "cT")
    cV = Ch(nc, st, "cV")
    cA = Ch(nc, st, "cA")
    cP = Ch(nc, st, "cP")
    cL = None
    cS = None

    psA = st.enter_context(nc.psum_tensor("psA", [128, 512], F32))
    psB = st.enter_context(nc.psum_tensor("psB", [128, 512], F32))
    psPV = [st.enter_context(nc.psum_tensor(f"psPV{i}", [128, 512], F32))
            for i in range(4)]
    psR = st.enter_context(nc.psum_tensor("psR", [128, 512], F32))
    psC = st.enter_context(nc.psum_tensor("psC", [128, 512], F32))
    gA, gB, gR = Guard(), Guard(), Guard()
    gCps = Guard()
    gPV = [Guard() for _ in range(4)]

    sbS = lambda name, shape, dt: st.enter_context(
        nc.sbuf_tensor(name, shape, dt))
    xn = sbS("xn", [128, NCD * S], BF16)
    gXNq = [Guard() for _ in range(4)]
    mask01 = sbS("mask01_s", [128, 128], BF16)
    idenb = sbS("idenb_s", [128, 128], BF16)
    idenf = sbS("idenf_s", [128, 128], F32)
    ones_r = sbS("ones_r", [1, 128], BF16)
    ones_c = sbS("ones_c", [128, 1], BF16)
    lam_sb = sbS("lam_sb", [128, 1], F32)
    gLAM = Guard()
    epsln = sbS("epsln", [128, 1], F32)
    epsrms = sbS("epsrms", [128, 1], F32)
    thr_sb = sbS("thr_sb", [128, 3], F32)
    thrn_sb = sbS("thrn_sb", [128, 3], F32)
    gTHR = [Guard() for _ in range(3)]
    lrow = sbS("lrow", [1, 32], F32)
    sc11 = sbS("sc11", [1, 16], F32)
    sc11b = sbS("sc11b", [1, 16], BF16)
    sc11b_f = sbS("sc11b_f", [1, 16], F32)
    biasB = [sbS(f"biasB{i}", [128, 1], F32) for i in range(2)]
    wq_stack = ExitStack()
    sbW = lambda name, shape, dt: wq_stack.enter_context(
        nc.sbuf_tensor(name, shape, dt))
    wstage = sbW("wstage", [128, NCD * 128], F32)
    gWS = Guard()
    wbf = [sbW(f"wbf{i}", [128, NCD * 256], BF16) for i in range(3)]
    gWB = [Guard(), Guard(), Guard()]
    gBB = [Guard(), Guard()]
    gC = Guard()
    gSC = Guard()

    def xnc(c, cols):
        return xn[:, c * S + cols.start: c * S + cols.stop]

    def dma(queue, out, in_, ch, reads=(), writes=(), waits=()):
        lane = dma_lane(queue)
        waits = list(waits) + [(lane, lane.n)]
        return em.do(queue, lambda e, o=out, i=in_: e.dma_start(out=o, in_=i),
                     reads=reads, writes=writes, waits=waits, inc=lane,
                     inc_n=16)

    def mm(out, lhsT, rhs, start, stop, reads=(), writes=(), waits=(),
           inc=None):
        return em.do("tensor",
                     lambda e, o=out, l=lhsT, r=rhs, s1=start, s2=stop:
                     e.matmul(o, l, r, start=s1, stop=s2,
                              skip_group_check=True),
                     reads=reads, writes=writes, waits=waits, inc=inc)

    def transp(out, in_, iden, reads=(), writes=(), inc=None):
        return em.do("tensor",
                     lambda e, o=out, i=in_, d=iden: e.transpose(o, i, d),
                     reads=reads, writes=writes, inc=inc)

    def act(out, in_, func, bias=None, scale=None, accum=None,
            reads=(), writes=(), waits=(), inc=None):
        def f(e, o=out, i=in_, fu=func, b=bias, s=scale, a=accum):
            kw = {}
            if b is not None:
                kw["bias"] = b
            if s is not None:
                kw["scale"] = s
            if a is not None:
                kw["accum_out"] = a
            return e.activation(o, i, fu, **kw)
        return em.do("scalar", f, reads=reads, writes=writes, waits=waits,
                     inc=inc)

    def acopy(out, in_, scale=None, reads=(), writes=(), waits=(), inc=None):
        if scale is None:
            f = lambda e, o=out, i=in_: e.copy(o, i)
        else:
            f = lambda e, o=out, i=in_, s=scale: e.mul(o, i, s)
        return em.do("scalar", f, reads=reads, writes=writes, waits=waits,
                     inc=inc)

    def vop(kind, *args, eng="vector", reads=(), writes=(), waits=(),
            inc=None):
        if kind == "tt":
            out, a, b, op = args
            f = lambda e, o=out, x=a, y=b, p=op: e.tensor_tensor(o, x, y, p)
        elif kind == "ts":
            out, a, s1, op = args
            f = lambda e, o=out, x=a, s=s1, p=op: e.tensor_scalar(
                o, x, s, None, p)
        elif kind == "red":
            out, a, op = args
            f = lambda e, o=out, x=a, p=op: e.tensor_reduce(o, x, AX.X, p)
        elif kind == "redabs":
            out, a, op = args
            f = lambda e, o=out, x=a, p=op: e.tensor_reduce(
                o, x, AX.X, p, apply_absolute_value=True)
        elif kind == "copy":
            out, a = args
            f = lambda e, o=out, x=a: e.tensor_copy(o, x)
        elif kind == "recip":
            out, a = args
            f = lambda e, o=out, x=a: e.reciprocal(o, x)
        elif kind == "memset":
            out, val = args
            f = lambda e, o=out, v=val: e.memset(o, v)
        else:
            raise ValueError(kind)
        return em.do(eng, f, reads=reads, writes=writes, waits=waits, inc=inc)

    def dma_chunked(queue, outv, inv, ch, nsplit, csize, reads=(), writes=(),
                    waits=()):
        toks = []
        per = csize // nsplit
        for i in range(nsplit):
            toks.append(dma(queue, outv[:, i * per:(i + 1) * per, :],
                            inv[:, i * per:(i + 1) * per, :], ch,
                            reads=reads, writes=writes, waits=waits))
        return toks[-1]

    dma("sync", mask01[:], mask01_in[:, :], cL, writes=[gC])
    dma("sync", idenb[:], idenb_in[:, :], cL, writes=[gC])
    dma("sync", idenf[:], idenf_in[:, :], cL, writes=[gC])
    dma("sync", lrow[:], lvec[:, :], cL, writes=[gC])
    vop("memset", ones_r[:], 1.0, writes=[gC], inc=cV)
    vop("memset", ones_c[:], 1.0, writes=[gC], inc=cV)
    vop("memset", epsln[:], EPS_LN, writes=[gC], inc=cV)
    vop("memset", epsrms[:], EPS_RMS, writes=[gC], inc=cV)

    vop("tt", sc11[0:1, 0:8], lrow[0:1, 0:8], lrow[0:1, 8:16], ALU.mult,
        reads=[gC], writes=[gSC], inc=cV)
    vop("red", sc11[0:1, 8:9], sc11[0:1, 0:8], ALU.add, writes=[gSC], inc=cV)
    vop("tt", sc11[0:1, 0:8], lrow[0:1, 16:24], lrow[0:1, 24:32], ALU.mult,
        writes=[gSC], inc=cV)
    vop("red", sc11[0:1, 9:10], sc11[0:1, 0:8], ALU.add, writes=[gSC], inc=cV)
    act(sc11[0:1, 10:12], sc11[0:1, 8:10], AF.Exp,
        reads=[gSC], writes=[gSC], inc=cA)
    vop("tt", sc11[0:1, 12:13], sc11[0:1, 10:11], sc11[0:1, 11:12],
        ALU.subtract, writes=[gSC], inc=cV)
    vop("ts", sc11[0:1, 13:14], sc11[0:1, 12:13], L_INIT, ALU.add,
        writes=[gSC], inc=cV)
    vop("copy", sc11b[0:1, 0:1], sc11[0:1, 13:14], writes=[gSC], inc=cV)
    vop("tt", sc11[0:1, 14:15], sc11[0:1, 13:14], sc11b[0:1, 0:1],
        ALU.subtract, writes=[gSC], inc=cV)
    vop("copy", sc11b[0:1, 1:2], sc11[0:1, 14:15], writes=[gSC], inc=cV)
    mm(psR[:, 16:17], ones_r[:], sc11b[0:1, 0:1], True, False,
       reads=[gSC, gC], writes=[gR], inc=cT)
    mm(psR[:, 16:17], ones_r[:], sc11b[0:1, 1:2], False, True,
       reads=[gSC], writes=[gR], inc=cT)
    vop("copy", lam_sb[:], psR[:, 16:17], reads=[gR], writes=[gLAM], inc=cV)
    if "sc11" in taps:
        dma("sync", tap_outs["sc11"], sc11[:], cL, reads=[gSC])
    if "lam" in taps:
        dma("sync", tap_outs["lam"], lam_sb[:], cL, reads=[gLAM])

    ln_stack = ExitStack()
    if phases >= 1:
        sbl = lambda name, shape, dt: ln_stack.enter_context(
            nc.sbuf_tensor(name, shape, dt))
        xstat = sbl("xstat", [128, 2 * D], F32)
        gXS = [Guard(), Guard()]
        sqt = sbl("sqt", [128, D], F32)
        gSQ = Guard()
        rcol = sbl("rcol", [128, 16], F32)
        mcol = sbl("mcol", [128, 16], F32)
        gCOL = Guard()
        statT = sbl("statT", [16, 256], F32)
        gSTT = Guard()
        rowf = sbl("rowf", [1, 512], F32)
        gROWF = Guard()
        rhi = sbl("rhi", [1, S], BF16)
        rlo = sbl("rlo", [1, S], BF16)
        mhi = sbl("mhi", [1, S], BF16)
        mlo = sbl("mlo", [1, S], BF16)
        gROW = Guard()
        stat1 = sbl("stat1", [128, 8], F32)
        gS1 = Guard()
        xTg = sbl("xTg", [128, NCD * 256], F32)
        gXT = Guard()

        for t in range(NST):
            xv = xstat[:, (t % 2) * D:(t % 2) * D + D]
            gxv = gXS[t % 2]
            dma("sync", xv, x_in[t * 128:(t + 1) * 128, :], cL,
                writes=[gxv])
            vop("red", stat1[:, 0:1], xv, ALU.add,
                reads=[gxv], writes=[gS1], inc=cV)
            act(sqt[:], xv, AF.Square, reads=[gxv], writes=[gSQ], inc=cA)
            vop("red", stat1[:, 1:2], sqt[:], ALU.add,
                reads=[gSQ], writes=[gS1], inc=cV)
            vop("ts", stat1[:, 2:3], stat1[:, 0:1], 1.0 / D, ALU.mult,
                writes=[gS1], inc=cV)
            vop("ts", stat1[:, 3:4], stat1[:, 1:2], 1.0 / D, ALU.mult,
                writes=[gS1], inc=cV)
            vop("tt", stat1[:, 4:5], stat1[:, 2:3], stat1[:, 2:3], ALU.mult,
                writes=[gS1], inc=cV)
            vop("tt", stat1[:, 5:6], stat1[:, 3:4], stat1[:, 4:5],
                ALU.subtract, writes=[gS1], inc=cV)
            act(stat1[:, 6:7], stat1[:, 5:6], AF.Ln, bias=epsln[:],
                reads=[gS1, gC], writes=[gS1], inc=cA)
            act(rcol[:, t:t + 1], stat1[:, 6:7], AF.Exp, scale=-0.5,
                reads=[gS1], writes=[gCOL], inc=cA)
            vop("tt", mcol[:, t:t + 1], stat1[:, 2:3], rcol[:, t:t + 1],
                ALU.mult, reads=[gCOL, gS1], writes=[gCOL], inc=cV)
        transp(psR[0:16, 0:128], rcol[:], idenf[:],
               reads=[gCOL, gC, gR], writes=[gR], inc=cT)
        vop("copy", statT[:, 0:128], psR[0:16, 0:128], reads=[gR],
            writes=[gSTT], inc=cV)
        transp(psR[0:16, 128:256], mcol[:], idenf[:], reads=[gCOL],
               writes=[gR], inc=cT)
        vop("copy", statT[:, 128:256], psR[0:16, 128:256], reads=[gR],
            writes=[gSTT], inc=cV)
        dma("gpsimd", stat_scr[0:1, :].rearrange("a (b c) -> (a b) c", b=16),
            statT[:, 0:128], cS, reads=[gSTT])
        dma("gpsimd", stat_scr[1:2, :].rearrange("a (b c) -> (a b) c", b=16),
            statT[:, 128:256], cS, reads=[gSTT])
        tok_stat_l = [(c, c.n) for c in cSs + cGs if c.n > 0]
        for qg in range(4):
            cols = slice(qg * 512, qg * 512 + 512)
            dma("sync", rowf[:], stat_scr[0:1, cols], cL, writes=[gROWF],
                waits=tok_stat_l)
            vop("copy", rhi[0:1, cols], rowf[:], reads=[gROWF], writes=[gROW],
                inc=cV)
            vop("tt", rowf[:], rowf[:], rhi[0:1, cols], ALU.subtract,
                reads=[gROW], writes=[gROWF], inc=cV)
            vop("copy", rlo[0:1, cols], rowf[:], reads=[gROWF], writes=[gROW],
                inc=cV)
            dma("sync", rowf[:], stat_scr[1:2, cols], cL, writes=[gROWF])
            vop("copy", mhi[0:1, cols], rowf[:], reads=[gROWF], writes=[gROW],
                inc=cV)
            vop("tt", rowf[:], rowf[:], mhi[0:1, cols], ALU.subtract,
                reads=[gROW], writes=[gROWF], inc=cV)
            vop("copy", mlo[0:1, cols], rowf[:], reads=[gROWF], writes=[gROW],
                inc=cV)
        for qg in range(4):
            cols = slice(qg * 512, (qg + 1) * 512)
            pr = psPV[2 * (qg % 2)]
            pm = psPV[2 * (qg % 2) + 1]
            gr = gPV[2 * (qg % 2)]
            gm = gPV[2 * (qg % 2) + 1]
            mm(pr[:], ones_r[:], rhi[0:1, cols], True, False,
               reads=[gROW, gC], writes=[gr], inc=cT)
            mm(pr[:], ones_r[:], rlo[0:1, cols], False, True,
               reads=[gROW], writes=[gr], inc=cT)
            mm(pm[:], ones_r[:], mhi[0:1, cols], True, False,
               reads=[gROW], writes=[gm], inc=cT)
            mm(pm[:], ones_r[:], mlo[0:1, cols], False, True,
               reads=[gROW], writes=[gm], inc=cT)
            for half in range(2):
                hc = slice(qg * 512 + half * 256, qg * 512 + half * 256 + 256)
                dma_chunked("sync",
                            xTg[:].rearrange("p (c n) -> p c n", c=NCD),
                            xT_in[:, hc].rearrange("(c p) n -> p c n", p=128),
                            cL, 4, NCD, writes=[gXT])
                for c in range(NCD):
                    xg = xTg[:, c * 256:(c + 1) * 256]
                    prh = pr[:, half * 256:(half + 1) * 256]
                    pmh = pm[:, half * 256:(half + 1) * 256]
                    vop("tt", sqt[:, 0:256], xg, prh, ALU.mult,
                        reads=[gXT, gr], writes=[gSQ], inc=cV)
                    vop("tt", xnc(c, hc), sqt[:, 0:256], pmh, ALU.subtract,
                        reads=[gSQ, gm], writes=[gXNq[qg]], inc=cV)
        if "xn" in taps:
            dma("sync", tap_outs["xn"], xn[:], cL, reads=list(gXNq))
    tok_ln = [(cV, cV.n), (cA, cA.n), (cT, cT.n)] + all_dma_toks()
    ln_stack.close()

    q_stack = ExitStack()
    quant_on = phases >= 2 and dev_quant
    if quant_on:
        sbq = lambda name, shape, dt: q_stack.enter_context(
            nc.sbuf_tensor(name, shape, dt))
        stA = sbq("stA", [128, 2 * 1408], F32)
        gSA = [Guard(), Guard()]
        accw = sbq("accw", [128, 4], F32)
        gACC = Guard()
        acchl = sbq("acchl", [128, 2], BF16)
        stB = sbq("stB", [128, 2 * 2048], F32)
        gSB = [Guard(), Guard()]
        qneg = sbq("qneg", [128, 2048], BF16)
        gQN = Guard()
        qout = sbq("qout", [128, 2 * 2048], BF16)
        gQO = [Guard(), Guard()]
        wsrcs = [(w1T, D, HID), (w2T, D, HID), (w3T, HID, D)]

    qA_jobs = []
    if quant_on:
        for wi, (wsrc, nr, ncl) in enumerate(wsrcs):
            pw = 1408 if ncl % 1408 == 0 else 1024
            assert ncl % pw == 0
            for tile in range(nr // 128):
                for pc in range(ncl // pw):
                    qA_jobs.append((wi, tile, pc, pw,
                                    tile == 0 and pc == 0))
    qA_pos = [0]

    def emit_passA(njobs):
        end = min(qA_pos[0] + njobs, len(qA_jobs))
        while qA_pos[0] < end:
            wi, tile, pc, pw, first = qA_jobs[qA_pos[0]]
            wsrc, nr, ncl = wsrcs[wi]
            b = qA_pos[0] % 2
            qA_pos[0] += 1
            stv = stA[:, b * 1408:b * 1408 + pw]
            dma("sync", stv,
                wsrc[tile * 128:(tile + 1) * 128,
                     pc * pw:(pc + 1) * pw], cL, writes=[gSA[b]])
            vop("redabs", accw[:, 3:4], stv, ALU.add,
                reads=[gSA[b]], writes=[gACC], inc=cV)
            if first:
                vop("copy", accw[:, wi:wi + 1], accw[:, 3:4],
                    reads=[gACC], writes=[gACC], inc=cV)
            else:
                vop("tt", accw[:, wi:wi + 1], accw[:, wi:wi + 1],
                    accw[:, 3:4], ALU.add,
                    reads=[gACC], writes=[gACC], inc=cV)

    def emit_gamma(wi):
        _, nr, ncl = wsrcs[wi]
        vop("copy", acchl[:, 0:1], accw[:, wi:wi + 1],
            reads=[gACC], writes=[gACC], inc=cV)
        vop("tt", accw[:, 3:4], accw[:, wi:wi + 1], acchl[:, 0:1],
            ALU.subtract, writes=[gACC], inc=cV)
        vop("copy", acchl[:, 1:2], accw[:, 3:4], writes=[gACC], inc=cV)
        mm(psR[0:1, 17:18], acchl[:, 0:1], ones_c[:], True, False,
           reads=[gACC, gC], writes=[gR], inc=cT)
        mm(psR[0:1, 17:18], acchl[:, 1:2], ones_c[:], False, True,
           reads=[gACC], writes=[gR], inc=cT)
        cnt = nr * ncl
        vop("ts", sc11[0:1, 15:16], psR[0:1, 17:18], 0.5 / cnt, ALU.mult,
            reads=[gR], writes=[gSC], inc=cV)
        vop("ts", sc11[0:1, 15:16], sc11[0:1, 15:16], 0.5 * EPS_Q, ALU.add,
            writes=[gSC], inc=cV)
        vop("copy", sc11b[0:1, 2:3], sc11[0:1, 15:16], writes=[gSC], inc=cV)
        vop("tt", sc11[0:1, 14:15], sc11[0:1, 15:16], sc11b[0:1, 2:3],
            ALU.subtract, writes=[gSC], inc=cV)
        vop("copy", sc11b[0:1, 3:4], sc11[0:1, 14:15], writes=[gSC], inc=cV)
        mm(psR[:, 18:19], ones_r[:], sc11b[0:1, 2:3], True, False,
           reads=[gSC], writes=[gR], inc=cT)
        mm(psR[:, 18:19], ones_r[:], sc11b[0:1, 3:4], False, True,
           reads=[gSC], writes=[gR], inc=cT)
        vop("copy", thr_sb[:, wi:wi + 1], psR[:, 18:19],
            reads=[gR], writes=[gTHR[wi]], inc=cV)
        vop("ts", thrn_sb[:, wi:wi + 1], thr_sb[:, wi:wi + 1], -1.0,
            ALU.mult, reads=[gTHR[wi]], writes=[gTHR[wi]], inc=cV)

    qB_jobs = []
    if quant_on:
        for j in range(NH3):
            qB_jobs.append((0,
                            w1T[:, j * 128:(j + 1) * 128].rearrange(
                                "(c p) n -> p c n", p=128),
                            q1s[j * 128:(j + 1) * 128, :], True))
            qB_jobs.append((1,
                            w2T[:, j * 128:(j + 1) * 128].rearrange(
                                "(c p) n -> p c n", p=128),
                            q2s[j * 128:(j + 1) * 128, :], True))
        for tile in range(NH3):
            qB_jobs.append((2, w3T[tile * 128:(tile + 1) * 128, :],
                            q3s[tile * 128:(tile + 1) * 128, :], False))
    qB_pos = [0]

    def emit_passB(njobs):
        end = min(qB_pos[0] + njobs, len(qB_jobs))
        while qB_pos[0] < end:
            wi, src, dst, is3d = qB_jobs[qB_pos[0]]
            b = qB_pos[0] % 2
            qB_pos[0] += 1
            stv = stB[:, b * 2048:(b + 1) * 2048]
            qo = qout[:, b * 2048:(b + 1) * 2048]
            dma("sync", stv.rearrange("p (c n) -> p c n", c=NCD) if is3d
                else stv, src, cL, writes=[gSB[b]])
            vop("ts", qo, stv, thr_sb[:, wi:wi + 1], ALU.is_ge,
                reads=[gSB[b], gTHR[wi]], writes=[gQO[b]], inc=cV)
            vop("ts", qneg[:], stv, thrn_sb[:, wi:wi + 1], ALU.is_le,
                reads=[gSB[b], gTHR[wi]], writes=[gQN], inc=cV)
            vop("tt", qo, qo, qneg[:], ALU.subtract,
                reads=[gQN], writes=[gQO[b]], inc=cV)
            dma("gpsimd", dst, qo, cS, reads=[gQO[b]])

    at_stack = ExitStack()
    o_wr_toks = {}
    if phases >= 3:
        sba = lambda name, shape, dt: at_stack.enter_context(
            nc.sbuf_tensor(name, shape, dt))
        qf = sba("qf", [128, 4 * S], BF16)
        kf = sba("kf", [128, 4 * S], BF16)
        gQF = [Guard() for _ in range(4)]
        gKF = [Guard() for _ in range(4)]
        vt = sba("vt", [128, NST * DH], BF16)
        gVT = [Guard() for _ in range(NST)]
        sqb = sba("sqb", [128, 512], BF16)
        gSQB = Guard()
        ebuf = [sba(f"ebuf{i}", [128, 512], BF16) for i in range(3)]
        gE = [Guard(), Guard(), Guard()]
        pvsb = sba("pvsb", [128, 8 * 257], F32)
        gPVS = Guard()
        rsb = sba("rsb", [128, 8], F32)
        gRS = Guard()
        onq = [sba(f"onq{i}", [128, 4 * DH], F32) for i in range(2)]
        gON = [Guard(), Guard()]
        odiff = sba("odiff", [128, 256], F32)
        gOD = Guard()
        otmp = sba("otmp", [128, 256], F32)
        gOT = Guard()
        osq, gOSQ = otmp, gOT
        rmsc = sba("rmsc", [128, 16], F32)
        gRMS = Guard()
        obf = [sba(f"obf{i}", [128, 512], BF16) for i in range(4)]
        gOBF = [Guard() for _ in range(4)]

        pend_transp = []
        tpi = [0]

        def flush_transposes():
            while pend_transp:
                onqb, head, qg, gONb = pend_transp.pop(0)
                for fc in range(2):
                    ob = obf[(qg % 2) * 2 + fc]
                    gob = gOBF[(qg % 2) * 2 + fc]
                    for sub in range(4):
                        reg = tpi[0] % 2
                        tpi[0] += 1
                        pt = (psA if reg == 0 else psB)[:, 0:128]
                        gpt = gA if reg == 0 else gB
                        transp(pt, onqb[:, sub * DH + fc * 128:
                                        sub * DH + fc * 128 + 128],
                               idenf[:], reads=[gONb, gC], writes=[gpt],
                               inc=cT)
                        vop("copy", ob[:, sub * 128:(sub + 1) * 128], pt,
                            reads=[gpt], writes=[gob], inc=cV)
                    row0 = ((head * 2 + fc) * 4 + qg) * 128
                    tok = dma("gpsimd", o_scr[row0:row0 + 128, :], ob[:],
                              cS, reads=[gob])
                    o_wr_toks[(head * 2 + fc, qg)] = tok

        def emit_wload(wsrc, colr, slot):
            ws, gws = wstage, gWS
            wb, gwb = wbf[slot % 3], gWB[slot % 3]
            for half in range(2):
                hc = slice(colr.start + half * 128, colr.start + half * 128 + 128)
                dma_chunked("sync", ws[:].rearrange("p (c n) -> p c n", c=NCD),
                            wsrc[:, hc].rearrange("(c p) n -> p c n", p=128),
                            cL, 2, NCD, writes=[gws])
                vop("copy",
                    wb[:].rearrange("p (c n) -> p c n", c=NCD)[
                        :, :, half * 128:(half + 1) * 128],
                    ws[:].rearrange("p (c n) -> p c n", c=NCD),
                    eng="gpsimd", reads=[gws], writes=[gwb], inc=cP)

        def proj_qk(dst, dstg, hh_local, slot):
            scale = 0.25
            wb, gwb = wbf[slot % 3], gWB[slot % 3]
            for m in range(2):
                r_ = 2 * hh_local + m
                for sg in range(4):
                    ps, gps = [(psA, gA), (psB, gB),
                               (psC, gCps)][(m * 4 + sg) % 3]
                    for c in range(NCD):
                        mm(ps[:], wb[:, c * 256 + m * 128:
                                     c * 256 + m * 128 + 128],
                           xnc(c, slice(sg * 512, (sg + 1) * 512)),
                           c == 0, c == NCD - 1,
                           reads=[gwb] + gXNq, writes=[gps],
                           inc=cT if c == NCD - 1 else None)
                    acopy(dst[:, r_ * S + sg * 512:r_ * S + (sg + 1) * 512],
                          ps[:], scale=scale,
                          reads=[gps], writes=[dstg[r_]], inc=cA)

        def proj_v(head, slot):
            wb, gwb = wbf[slot % 3], gWB[slot % 3]
            for t2 in range(NST):
                ps, gps = [(psA, gA), (psB, gB), (psC, gCps)][t2 % 3]
                for c in range(NCD):
                    mm(ps[:, 0:256],
                       xnc(c, slice(t2 * 128, (t2 + 1) * 128)),
                       wb[:, c * 256:c * 256 + 256],
                       c == 0, c == NCD - 1,
                       reads=[gwb] + gXNq, writes=[gps],
                       inc=cT if c == NCD - 1 else None)
                acopy(vt[:, t2 * DH:(t2 + 1) * DH], ps[:, 0:256],
                      reads=[gps], writes=[gVT[t2]], inc=cA)

        def wload_job(head, i):
            base = head * 5
            hhA, hhB = head, head + 8
            if i == 0:
                emit_wload(wqT, slice(256 * hhA, 256 * hhA + 256), base + 0)
            elif i == 1:
                emit_wload(wqT, slice(256 * hhB, 256 * hhB + 256), base + 1)
            elif i == 2:
                emit_wload(wkT, slice(256 * hhA, 256 * hhA + 256), base + 2)
            elif i == 3:
                emit_wload(wkT, slice(256 * hhB, 256 * hhB + 256), base + 3)
            else:
                emit_wload(wvT, slice(256 * head, 256 * head + 256), base + 4)

        def emit_head(head, bg_fn=None):
            base = head * 5
            if not DEV_PREFETCH and head > 0:
                for i in range(3):
                    wload_job(head, i)
            proj_qk(qf, gQF, 0, base + 0)
            wload_job(head, 3)
            proj_qk(qf, gQF, 1, base + 1)
            wload_job(head, 4)
            proj_qk(kf, gKF, 0, base + 2)
            proj_qk(kf, gKF, 1, base + 3)
            proj_v(head, base + 4)
            nexth = head + 1 if head + 1 < min(H, dev_heads) else None
            if not DEV_PREFETCH:
                nexth = None
            if head == 0:
                if "qf" in taps:
                    dma("sync", tap_outs["qf"], qf[:], cL, reads=list(gQF))
                if "kf" in taps:
                    dma("sync", tap_outs["kf"], kf[:], cL, reads=list(gKF))
                if "vt" in taps:
                    dma("sync", tap_outs["vt"], vt[:], cL, reads=list(gVT))

            if DEV_STAGE == "proj":
                return
            for hh in range(2):
                for which, (src, gsrc) in enumerate([(qf, gQF), (kf, gKF)]):
                    for cc in range(8):
                        for ri in range(2):
                            r_ = 2 * hh + ri
                            act(sqb[:, ri * 256:(ri + 1) * 256],
                                src[:, r_ * S + cc * 256:
                                    r_ * S + (cc + 1) * 256],
                                AF.Square, reads=[gsrc[r_]], writes=[gSQB],
                                inc=cA)
                        for ri in range(2):
                            mm(psR[0:1, 256:512], ones_c[:],
                               sqb[:, ri * 256:(ri + 1) * 256],
                               ri == 0, ri == 1,
                               reads=[gSQB, gC], writes=[gR],
                               inc=cT if ri == 1 else None)
                        vop("red", sc11[0:1, which * 8 + cc:
                                        which * 8 + cc + 1],
                            psR[0:1, 256:512], ALU.max, reads=[gR],
                            writes=[gSC], inc=cV)
                vop("red", sc11b_f[0:1, 0:1], sc11[0:1, 0:8], ALU.max,
                    writes=[gSC], inc=cV)
                vop("red", sc11b_f[0:1, 1:2], sc11[0:1, 8:16], ALU.max,
                    writes=[gSC], inc=cV)
                vop("tt", sc11b_f[0:1, 2:3], sc11b_f[0:1, 0:1],
                    sc11b_f[0:1, 1:2], ALU.add, writes=[gSC], inc=cV)
                vop("ts", sc11b_f[0:1, 3:4], sc11b_f[0:1, 2:3], -0.5,
                    ALU.mult, writes=[gSC], inc=cV)
                vop("copy", sc11b[0:1, 4:5], sc11b_f[0:1, 3:4],
                    writes=[gSC], inc=cV)
                mm(psR[:, 19:20], ones_r[:], sc11b[0:1, 4:5], True, True,
                   reads=[gSC, gC], writes=[gR], inc=cT)
                vop("copy", biasB[hh][:], psR[:, 19:20], reads=[gR],
                    writes=[gBB[hh]], inc=cV)
            if head == 0 and "bias0" in taps:
                dma("sync", tap_outs["bias0"], biasB[0][:], cL, reads=[gBB[0]])
                dma("sync", tap_outs["bias1"], biasB[1][:], cL, reads=[gBB[1]])

            if DEV_STAGE == "bounds":
                return
            for qg in range(4):
                if nexth is not None and qg > 0:
                    wload_job(nexth, qg - 1)
                if bg_fn is not None and qg > 0:
                    bg_fn()
                for hh in range(2):
                    nkt = 4 * qg + 4

                    def emit_score(kt, qg=qg, hh=hh):
                        diag = kt >= 4 * qg
                        c0 = 128 * kt - 512 * qg if diag else 0
                        eb, geb = ebuf[kt % 3], gE[kt % 3]
                        ps, gps = [(psA, gA), (psB, gB), (psC, gCps)][kt % 3]
                        for ri in range(2):
                            r_ = 2 * hh + ri
                            mm(ps[:, c0:512],
                               kf[:, r_ * S + kt * 128:
                                  r_ * S + kt * 128 + 128],
                               qf[:, r_ * S + qg * 512 + c0:
                                  r_ * S + (qg + 1) * 512],
                               ri == 0, ri == 1,
                               reads=[gKF[r_], gQF[r_]], writes=[gps],
                               inc=cT if ri == 1 else None)
                        act(eb[:, c0:512], ps[:, c0:512], AF.Exp,
                            bias=biasB[hh][:],
                            reads=[gps, gBB[hh]], writes=[geb], inc=cA)
                        if diag:
                            vop("tt", eb[:, c0:c0 + 128], eb[:, c0:c0 + 128],
                                mask01[:], ALU.mult, eng="gpsimd",
                                reads=[gC], writes=[geb], inc=cP)

                    def emit_pv(kt, qg=qg, hh=hh):
                        diag = kt >= 4 * qg
                        c0 = 128 * kt - 512 * qg if diag else 0
                        eb, geb = ebuf[kt % 3], gE[kt % 3]
                        sub0 = c0 // 128
                        for sub in range(sub0, 4):
                            ecol = slice(sub * 128, sub * 128 + 128)
                            start_pv = kt == 0
                            stop_pv = kt == 4 * qg + sub
                            bank = 2 * hh + (sub // 2)
                            reg = psPV[bank][:, (sub % 2) * 256:
                                             (sub % 2) * 256 + 256]
                            mm(reg, eb[:, ecol], vt[:, kt * DH:(kt + 1) * DH],
                               start_pv and sub % 2 == 0, stop_pv,
                               reads=[geb, gVT[kt]], writes=[gPV[bank]],
                               inc=cT if stop_pv else None)
                            rcol_i = (qg % 2) * 8 + hh * 4 + sub
                            mm(psR[:, rcol_i:rcol_i + 1], eb[:, ecol],
                               ones_c[:],
                               start_pv and sub == 0, stop_pv,
                               reads=[geb, gC, gVT[kt]], writes=[gR],
                               inc=cT if sub == 3 else None)

                    for kt in range(nkt):
                        emit_score(kt)
                        if kt >= 2:
                            emit_pv(kt - 2)
                    emit_pv(nkt - 2)
                    emit_pv(nkt - 1)
                    if bg_fn is not None and hh == 0 and qg >= 2:
                        bg_fn()
                    for sub in range(4):
                        bank = 2 * hh + (sub // 2)
                        reg = psPV[bank][:, (sub % 2) * 256:
                                         (sub % 2) * 256 + 256]
                        acopy(pvsb[:, (hh * 4 + sub) * 257:
                                   (hh * 4 + sub) * 257 + 256], reg,
                              reads=[gPV[bank]], writes=[gPVS], inc=cA)
                    acopy(rsb[:, hh * 4:hh * 4 + 4],
                          psR[:, (qg % 2) * 8 + hh * 4:
                              (qg % 2) * 8 + hh * 4 + 4],
                          reads=[gR], writes=[gRS], inc=cA)
                if DEV_STAGE == "loop":
                    continue
                flush_transposes()
                onqb, gONb = onq[qg % 2], gON[qg % 2]
                for sub in range(4):
                    p1 = pvsb[:, sub * 257:sub * 257 + 256]
                    p2 = pvsb[:, (4 + sub) * 257:(4 + sub) * 257 + 256]
                    vop("recip", rmsc[:, 8:9], rsb[:, sub:sub + 1],
                        reads=[gRS], writes=[gRMS], inc=cV)
                    vop("recip", rmsc[:, 9:10], rsb[:, 4 + sub:5 + sub],
                        reads=[gRS], writes=[gRMS], inc=cV)
                    vop("tt", rmsc[:, 10:11], rmsc[:, 9:10], lam_sb[:],
                        ALU.mult, reads=[gLAM], writes=[gRMS], inc=cV)
                    vop("ts", odiff[:], p1, rmsc[:, 8:9], ALU.mult,
                        reads=[gPVS, gRMS], writes=[gOD], inc=cV)
                    vop("ts", otmp[:], p2, rmsc[:, 10:11], ALU.mult,
                        reads=[gPVS, gRMS], writes=[gOT], inc=cV)
                    vop("tt", odiff[:], odiff[:], otmp[:], ALU.subtract,
                        reads=[gOT], writes=[gOD], inc=cV)
                    act(osq[:], odiff[:], AF.Square,
                        reads=[gOD], writes=[gOSQ], inc=cA)
                    vop("red", rmsc[:, 0:8],
                        osq[:].rearrange("p (a b) -> p a b", a=8), ALU.add,
                        reads=[gOSQ], writes=[gRMS], inc=cV)
                    act(rmsc[:, 0:8], rmsc[:, 0:8], AF.Ln, bias=epsrms[:],
                        scale=1.0 / 32, reads=[gRMS, gC], writes=[gRMS],
                        inc=cA)
                    act(rmsc[:, 0:8], rmsc[:, 0:8], AF.Exp, scale=-0.5,
                        reads=[gRMS], writes=[gRMS], inc=cA)
                    for g8 in range(8):
                        vop("ts", onqb[:, sub * DH + g8 * 32:
                                       sub * DH + g8 * 32 + 32],
                            odiff[:, g8 * 32:(g8 + 1) * 32],
                            rmsc[:, g8:g8 + 1], ALU.mult,
                            reads=[gOD, gRMS], writes=[gONb], inc=cV)
                pend_transp.append((onqb, head, qg, gONb))

        for i in range(3):
            wload_job(0, i)
        em.do("sync", lambda e: e.nop(), waits=tok_ln)
        npA = (len(qA_jobs) + 11) // 12 if quant_on else 0
        bgA = (lambda: emit_passA(npA)) if quant_on else None
        for head in range(min(4, dev_heads)):
            emit_head(head, bgA)
            if quant_on:
                emit_passA(npA)
        if dev_heads > 4:
            emit_head(4, bgA)
        if quant_on:
            emit_passA(len(qA_jobs))
            emit_gamma(0)
            emit_gamma(1)
            emit_gamma(2)
            if "thr" in taps:
                dma("sync", tap_outs["thr"], thr_sb[:], cL, reads=[gTHR[2]])
        bgB = (lambda: emit_passB(7)) if quant_on else None
        for head in range(5, min(H, dev_heads)):
            emit_head(head, bgB)
            if quant_on:
                emit_passB(7)
        flush_transposes()
        if quant_on:
            emit_passB(len(qB_jobs))
    tok_attn = [(cV, cV.n), (cA, cA.n), (cT, cT.n), (cP, cP.n)] + all_dma_toks()
    at_stack.close()

    wo_stack = ExitStack()
    if phases >= 4:
        sbw = lambda name, shape, dt: wo_stack.enter_context(
            nc.sbuf_tensor(name, shape, dt))
        wostage = sbw("wostage", [128, NCD * 256], F32)
        gWOS = Guard()
        wobf = sbw("wobf", [128, NCD * 512], BF16)
        gWOB = Guard()
        oslice = [sbw(f"oslice{i}", [128, NCD * 512], BF16) for i in range(2)]
        gOSL = [Guard(), Guard()]
        sstage = sbw("sstage", [128, 512], BF16)
        gSST = Guard()

        for ds in range(4):
            dcols = slice(ds * 512, (ds + 1) * 512)
            for half in range(2):
                hcols = slice(ds * 512 + half * 256, ds * 512 + half * 256 + 256)
                dma_chunked("sync",
                            wostage[:].rearrange("p (c n) -> p c n", c=NCD),
                            woT[:, hcols].rearrange("(c p) n -> p c n", p=128),
                            cL, 2, NCD, writes=[gWOS],
                            waits=tok_attn if ds == 0 and half == 0 else ())
                vop("ts",
                    wobf[:].rearrange("p (c n) -> p c n", c=NCD)[
                        :, :, half * 256:(half + 1) * 256],
                    wostage[:].rearrange("p (c n) -> p c n", c=NCD),
                    OML, ALU.mult,
                    reads=[gWOS], writes=[gWOB], inc=cV)
            for qg in range(4):
                osl, gosl = oslice[qg % 2], gOSL[qg % 2]
                for c in range(NCD):
                    row0 = (c * 4 + qg) * 128
                    dma("sync", osl[:, c * 512:(c + 1) * 512],
                        o_scr[row0:row0 + 128, :], cL, writes=[gosl],
                        waits=[o_wr_toks[(c, qg)]] if phases >= 3 else ())
                for sub in range(4):
                    ps, gps = [(psA, gA), (psB, gB),
                               (psC, gCps)][(qg * 4 + sub) % 3]
                    for c in range(NCD):
                        mm(ps[:], osl[:, c * 512 + sub * 128:
                                      c * 512 + sub * 128 + 128],
                           wobf[:, c * 512:(c + 1) * 512],
                           c == 0, c == NCD - 1,
                           reads=[gosl, gWOB], writes=[gps],
                           inc=cT if c == NCD - 1 else None)
                    acopy(sstage[:], ps[:], reads=[gps], writes=[gSST],
                          inc=cA)
                    t_ = qg * 4 + sub
                    dma("scalar", sattn[t_ * 128:(t_ + 1) * 128, dcols],
                        sstage[:], cS, reads=[gSST])
    tok_wo = [(cV, cV.n), (cA, cA.n), (cT, cT.n), (cP, cP.n)] + all_dma_toks()
    tok_sattn = [(c, c.n) for c in cSs if c.n > 0]
    wo_stack.close()
    q_stack.close()
    wq_stack.close()

    ffn_stack = ExitStack()
    if phases >= 5:
        sbf = lambda name, shape, dt: ffn_stack.enter_context(
            nc.sbuf_tensor(name, shape, dt))
        hid = sbf("hid", [128, NH3 * 512], BF16)
        gHID = [Guard() for _ in range(NH3)]
        q12 = [sbf(f"q12_{i}", [128, NCD * 128], BF16) for i in range(4)]
        gQ12 = [Guard() for _ in range(4)]
        q3sl = [sbf(f"q3sl{i}", [128, NH3 * 256], BF16) for i in range(2)]
        gQ3 = [Guard(), Guard()]
        sigb = sbf("sigb", [128, 512], F32)
        gSIG = Guard()
        xsl = sbf("xsl", [128, 512], F32)
        gXSL = [Guard(), Guard()]
        ssl = sbf("ssl", [128, 512], BF16)
        gSSL = [Guard(), Guard()]
        sfup = sbf("sfup", [128, 256], F32)
        gSF = Guard()
        ost = sbf("ost", [128, 512], F32)
        gOST = [Guard(), Guard()]

        tok_q = tok_wo
        q3v = q3s.rearrange("(j p) n -> p j n", p=128)
        for blk in range(4):
            bcols = slice(blk * 512, (blk + 1) * 512)
            for j in range(NH3):
                b1, g1 = q12[(2 * j) % 4], gQ12[(2 * j) % 4]
                b2, g2 = q12[(2 * j + 1) % 4], gQ12[(2 * j + 1) % 4]
                dma("sync", b1[:], q1s[j * 128:(j + 1) * 128, :], cL,
                    writes=[g1], waits=tok_q if blk == 0 and j < 2 else ())
                dma("sync", b2[:], q2s[j * 128:(j + 1) * 128, :], cL,
                    writes=[g2], waits=tok_q if blk == 0 and j < 2 else ())
                for c in range(NCD):
                    mm(psA[:], b1[:, c * 128:(c + 1) * 128], xnc(c, bcols),
                       c == 0, c == NCD - 1, reads=[g1, gXNq[blk]],
                       writes=[gA], inc=cT if c == NCD - 1 else None)
                for c in range(NCD):
                    mm(psB[:], b2[:, c * 128:(c + 1) * 128], xnc(c, bcols),
                       c == 0, c == NCD - 1, reads=[g2, gXNq[blk]],
                       writes=[gB], inc=cT if c == NCD - 1 else None)
                act(sigb[:], psA[:], AF.Sigmoid,
                    reads=[gA], writes=[gSIG], inc=cA)
                vop("tt", hid[:, j * 512:(j + 1) * 512], sigb[:], psB[:],
                    ALU.mult, reads=[gSIG, gB], writes=[gHID[j]], inc=cV)
            if blk == 0 and "hid" in taps:
                dma("sync", tap_outs["hid"], hid[:], cL,
                    reads=[gHID[jj] for jj in range(NH3)])
            for dch in range(8):
                q3b, gq3 = q3sl[dch % 2], gQ3[dch % 2]
                for piece in range(4):
                    dma("sync",
                        q3b[:].rearrange("p (j n) -> p j n", j=NH3)[
                            :, piece * 11:(piece + 1) * 11, :],
                        q3v[:, piece * 11:(piece + 1) * 11,
                            dch * 256:(dch + 1) * 256],
                        cL, writes=[gq3],
                        waits=tok_q if blk == 0 and dch < 2 else ())
                dc = slice(dch * 256, (dch + 1) * 256)
                for t_ in range(4):
                    trow = blk * 4 + t_
                    nb = (dch * 4 + t_) % 2
                    xsl_v = xsl[:, nb * 256:(nb + 1) * 256]
                    ssl_v = ssl[:, nb * 256:(nb + 1) * 256]
                    ost_v = ost[:, nb * 256:(nb + 1) * 256]
                    dma("sync", xsl_v,
                        x_in[trow * 128:(trow + 1) * 128, dc], cL,
                        writes=[gXSL[nb]])
                    dma("sync", ssl_v,
                        sattn[trow * 128:(trow + 1) * 128, dc], cL,
                        writes=[gSSL[nb]],
                        waits=tok_sattn if phases >= 4 else ())
                    idx = (dch * 4 + t_) % 8
                    bank = idx % 4
                    reg = psPV[bank][:, (idx // 4) * 256:
                                     (idx // 4) * 256 + 256]
                    tok = None
                    for j in range(NH3):
                        tok = mm(reg,
                                 hid[:, j * 512 + t_ * 128:
                                     j * 512 + t_ * 128 + 128],
                                 q3b[:, j * 256:(j + 1) * 256],
                                 j == 0, j == NH3 - 1,
                                 reads=[gHID[j], gq3], writes=[gPV[bank]],
                                 inc=cT if j == NH3 - 1 else None)
                    for j in range(NH3):
                        gHID[j].r.append(tok)
                    gq3.r.append(tok)
                    act(sigb[:, 0:256], reg, AF.Silu,
                        reads=[gPV[bank]], writes=[gSIG], inc=cA)
                    act(sfup[:], ssl_v, AF.Identity,
                        reads=[gSSL[nb]], writes=[gSF], inc=cA)
                    vop("tt", sigb[:, 256:512], sigb[:, 0:256], sfup[:],
                        ALU.mult, reads=[gSIG, gSF], writes=[gSIG], inc=cV)
                    vop("tt", ost_v, sigb[:, 256:512], xsl_v,
                        ALU.add, reads=[gSIG, gXSL[nb]], writes=[gOST[nb]],
                        inc=cV)
                    dma("gpsimd", out_d[trow * 128:(trow + 1) * 128, dc],
                        ost_v, cS, reads=[gOST[nb]])

    em.do("sync", lambda e: e.nop(), waits=all_dma_toks())

    with nc.Block() as block:
        for eng in ENGINES:
            stream = em.streams[eng]

            def runner(engine, _s=stream):
                for f in _s:
                    f(engine)
            getattr(block, eng)(runner)

    ffn_stack.close()
    st.close()
    return nc, em.counts


def kernel(L0_attn_x, L0_attn_l_q1, L0_attn_l_k1, L0_attn_l_q2, L0_attn_l_k2,
           L0_attn_wq, L0_attn_wk, L0_attn_wv, L0_attn_wo,
           L0_ffn_w1, L0_ffn_w2, L0_ffn_w3):
    x = np.ascontiguousarray(np.asarray(L0_attn_x, dtype=np.float32))
    f32 = lambda a: np.asarray(a, dtype=np.float32)
    tr = lambda a: np.ascontiguousarray(np.asarray(a, dtype=np.float32).T)
    shared = {
        "wqT": tr(L0_attn_wq), "wkT": tr(L0_attn_wk), "wvT": tr(L0_attn_wv),
        "woT": tr(L0_attn_wo), "w1T": tr(L0_ffn_w1), "w2T": tr(L0_ffn_w2),
        "w3T": tr(L0_ffn_w3),
        "lvec": np.ascontiguousarray(np.concatenate(
            [f32(L0_attn_l_q1), f32(L0_attn_l_k1),
             f32(L0_attn_l_q2), f32(L0_attn_l_k2)]).reshape(1, 32)),
        "mask01": np.triu(np.ones((128, 128))).astype(ml_dtypes.bfloat16),
        "idenb": np.eye(128).astype(ml_dtypes.bfloat16),
        "idenf": np.eye(128, dtype=np.float32),
    }
    in_maps = []
    for c in range(8):
        m = dict(shared)
        m["x"] = x[c]
        m["xT"] = np.ascontiguousarray(x[c].T)
        in_maps.append(m)
    nc, _ = build_nc(taps=TAPS, phases=PHASES)
    res = run_bass_kernel_spmd(nc, in_maps, core_ids=list(range(8)))
    global LAST_RESULTS
    LAST_RESULTS = res.results
    return np.stack([np.asarray(res.results[c]["out"]) for c in range(8)], 0)


LAST_RESULTS = None
